# version 01
# baseline (speedup 1.0000x reference)
import numpy as np
import ml_dtypes
from contextlib import ExitStack

import concourse.bass as bass
import concourse.mybir as mybir
from concourse.bass_utils import run_bass_kernel_spmd
from concourse.bass_core_rust import IndirectOffsetOnAxis

F32 = mybir.dt.float32
BF16 = mybir.dt.bfloat16
I32 = mybir.dt.int32
AF = mybir.ActivationFunctionType
ALU = mybir.AluOpType
AX = mybir.AxisListType

D = 1024
H = 4
DK = 256
DV = 512
E = 8
II = 4096
C = 128


class Cfg:
    n_chunks = 64
    own_chunks = 32
    st_chunks = 8
    n_exp = 8
    newton = 3
    debug = ()


ENGINES = ("sync", "tensor", "vector", "scalar", "gpsimd")


class Sem:
    __slots__ = ("h", "n", "name")

    def __init__(self, h, name):
        self.h = h
        self.n = 0
        self.name = name


class P:
    def __init__(self, nc, es):
        self.nc = nc
        self.es = es
        self.q = {e: [] for e in ENGINES}
        self._sems = {}

    def sem(self, name):
        if name not in self._sems:
            self._sems[name] = Sem(self.es.enter_context(self.nc.semaphore(name)), name)
        return self._sems[name]

    def wait(self, eng, sem, val=None):
        v = sem.n if val is None else val
        if v > 0:
            self.q[eng].append(lambda e, s=sem.h, v=v: e.wait_ge(s, v))

    def op(self, eng, fn, inc=None, inc_k=1):
        if inc is not None:
            inc.n += inc_k

            def run(e, fn=fn, s=inc.h, k=inc_k):
                fn(e).then_inc(s, k)
        else:
            def run(e, fn=fn):
                fn(e)
        self.q[eng].append(run)


def _rsqrt(p, buf, col0, ncol, scr, iters):
    sn = p.sem("snl")
    x = buf[:, col0:col0 + ncol]
    y = scr[:, 0:ncol]
    t = scr[:, ncol:2 * ncol]
    p.op("vector", lambda e: e.tensor_scalar(out=scr[:, 0:ncol].bitcast(I32),
                                             in0=x.bitcast(I32), scalar1=1, scalar2=None,
                                             op0=ALU.logical_shift_right), inc=sn)
    p.wait("vector", sn)
    p.op("vector", lambda e: e.tensor_scalar(out=scr[:, 2 * ncol:3 * ncol].bitcast(I32),
                                             in0=scr[:, 0:ncol].bitcast(I32),
                                             scalar1=0x5F3759DF, scalar2=-1,
                                             op0=ALU.subtract, op1=ALU.mult), inc=sn)
    y = scr[:, 2 * ncol:3 * ncol]
    for _ in range(iters):
        p.wait("vector", sn)
        p.op("vector", lambda e: e.tensor_tensor(t, y, y, op=ALU.mult), inc=sn)
        p.wait("vector", sn)
        p.op("vector", lambda e: e.scalar_tensor_tensor(out=t, in0=x, scalar=-0.5, in1=t,
                                                        op0=ALU.mult, op1=ALU.mult), inc=sn)
        p.wait("vector", sn)
        p.op("vector", lambda e: e.tensor_scalar(out=t, in0=t, scalar1=1.5, scalar2=None,
                                                 op0=ALU.add), inc=sn)
        p.wait("vector", sn)
        p.op("vector", lambda e: e.tensor_tensor(y, y, t, op=ALU.mult), inc=sn)
    p.wait("vector", sn)
    p.op("vector", lambda e: e.tensor_copy(x, y), inc=sn)
    p.wait("vector", sn)


def build(cfg):
    nc = bass.Bass("TRN2", target_bir_lowering=False)
    NCH = cfg.n_chunks
    OCH = cfg.own_chunks
    STC = cfg.st_chunks
    NE = cfg.n_exp
    NST = OCH // STC
    STOK = STC * C
    TOKB = NCH * C
    TOKO = OCH * C

    def din(name, shape, dtype=F32):
        return nc.dram_tensor(name, list(shape), dtype, kind="ExternalInput").ap()

    x_full = din("x_full", (TOKB, D))
    x_own = din("x_own", (TOKO, D))
    ln_w = din("ln_w", (1, D))
    ln_b = din("ln_b", (1, D))
    Wq = din("Wq", (D, H * DK))
    Wk = din("Wk", (D, H * DK))
    Wv = din("Wv", (D, H * DV))
    Wb = din("Wb", (D, H))
    onw = din("onw", (1, DV))
    Wo = din("Wo", (H * DV, D))
    gate_w = din("gate_w", (D, E))
    gate_b = din("gate_b", (1, E))
    upw = din("upw", (NE, D, II))
    upb_t = din("upb_t", (128, NE * 32))
    dnw = din("dnw", (NE, II, D))
    dnb_t = din("dnb_t", (128, NE * 8))
    sw_up = din("sw_up", (D, II))
    sb_up_t = din("sb_up_t", (128, 32))
    sw_dn = din("sw_dn", (II, D))
    sb_dn_t = din("sb_dn_t", (128, 8))
    idxw = din("idxw", (1, OCH * 128), I32)

    out_d = nc.dram_tensor("out", [TOKO, D], F32, kind="ExternalOutput").ap()
    s_scr = nc.dram_tensor("s_scr", [NCH * 128, D], F32, kind="Internal").ap()
    wbf_up = nc.dram_tensor("wbf_up", [NE, D * II], BF16, kind="Internal").ap()
    wbf_dn = nc.dram_tensor("wbf_dn", [NE, 8, 128 * 32 * 128], BF16, kind="Internal").ap()
    wbf_su = nc.dram_tensor("wbf_su", [1, D * II], BF16, kind="Internal").ap()
    wbf_sd = nc.dram_tensor("wbf_sd", [1, 8, 128 * 32 * 128], BF16, kind="Internal").ap()

    dbg = {}
    for name in cfg.debug:
        shp = {"dbg_s": (TOKB, D), "dbg_xn": (TOKB, D), "dbg_nrm": (TOKB, D),
               "dbg_q": (TOKB, H * DK), "dbg_k": (TOKB, H * DK),
               "dbg_v": (TOKB, H * DV), "dbg_beta": (TOKB, H),
               "dbg_o": (TOKB, H * DV), "dbg_scores": (TOKO, E),
               "dbg_w8": (TOKO, E), "dbg_zg": (TOKO, D)}[name]
        dbg[name] = nc.dram_tensor(name, list(shp), F32, kind="ExternalOutput").ap()

    es = ExitStack()
    p = P(nc, es)
    sems = p.sem

    def sb(name, shape, dtype):
        return es.enter_context(nc.sbuf_tensor(name, list(shape), dtype))

    def psum(name, shape, dtype=F32):
        return es.enter_context(nc.psum_tensor(name, list(shape), dtype))

    ident_b = sb("ident_b", (128, 128), BF16)
    ident_f = sb("ident_f", (128, 128), F32)
    tril01 = sb("tril01", (128, 128), F32)
    triu01 = sb("triu01", (128, 128), F32)
    ones_row = sb("ones_row", (1, 128), F32)
    nsc = sb("nsc", (128, 32), F32)
    stat = sb("stat", (128, 24), F32)
    idxw_sb = sb("idxw_sb", (128, 1), I32)

    PB = [psum(f"pb{i}", (128, 512), F32) for i in range(4)]
    PT = psum("pt", (128, 1024), BF16)
    PI = psum("pi", (128, 1536), F32)

    s_init = sems("s_init")
    p.op("gpsimd", lambda e: e.memset(ident_b[:], 0.0), inc=s_init)
    p.wait("gpsimd", s_init)
    p.op("gpsimd", lambda e: e.affine_select(out=ident_b[:], in_=ident_b[:],
                                             compare_op=ALU.not_equal, fill=1.0, base=0,
                                             pattern=[[-1, 128]], channel_multiplier=1),
         inc=s_init)
    p.op("gpsimd", lambda e: e.memset(ident_f[:], 0.0), inc=s_init)
    p.wait("gpsimd", s_init)
    p.op("gpsimd", lambda e: e.affine_select(out=ident_f[:], in_=ident_f[:],
                                             compare_op=ALU.not_equal, fill=1.0, base=0,
                                             pattern=[[-1, 128]], channel_multiplier=1),
         inc=s_init)
    p.op("gpsimd", lambda e: e.memset(tril01[:], 0.0), inc=s_init)
    p.wait("gpsimd", s_init)
    p.op("gpsimd", lambda e: e.affine_select(out=tril01[:], in_=tril01[:],
                                             compare_op=ALU.is_ge, fill=1.0, base=0,
                                             pattern=[[1, 128]], channel_multiplier=-1),
         inc=s_init)
    p.op("gpsimd", lambda e: e.memset(triu01[:], 1.0), inc=s_init)
    p.wait("gpsimd", s_init)
    p.op("gpsimd", lambda e: e.affine_select(out=triu01[:], in_=triu01[:],
                                             compare_op=ALU.is_ge, fill=0.0, base=0,
                                             pattern=[[1, 128]], channel_multiplier=-1),
         inc=s_init)
    p.op("gpsimd", lambda e: e.memset(ones_row[:], 1.0), inc=s_init)

    mes = ExitStack()

    def msb(name, shape, dtype):
        return mes.enter_context(nc.sbuf_tensor(name, list(shape), dtype))

    wq_sb = msb("wq_sb", (128, 8 * 1024), BF16)
    wk_sb = msb("wk_sb", (128, 8 * 1024), BF16)
    wv_sb = msb("wv_sb", (128, 8 * 2048), BF16)
    wb_sb = msb("wb_sb", (128, 8 * H), BF16)
    wo_sb = msb("wo_sb", (128, 16 * 1024), BF16)
    lnw_bc = msb("lnw_bc", (128, D), F32)
    lnb_bc = msb("lnb_bc", (128, D), F32)
    onw_bc = msb("onw_bc", (128, DV), F32)
    lnrow = msb("lnrow", (1, 2 * D + DV), F32)
    S_b = [msb(f"S_b{h}", (128, 2 * DV), BF16) for h in range(H)]
    xch = [msb("xch0", (128, D), F32)]
    xnch = [msb(f"xnch{i}", (128, D), F32) for i in range(2)]
    sqch = msb("sqch", (128, D), F32)
    nrm0 = msb("nrm0", (128, D), F32)
    osq = msb("osq", (128, 2 * DV), F32)
    nrmb = [msb("nrmb0", (128, D), BF16)]
    nT = [msb("nT0", (128, 8 * 128), BF16)]
    qraw = [msb("qraw0", (128, H * DK), F32)]
    kraw = [msb("kraw0", (128, H * DK), F32)]
    qn = [msb("qn0", (128, H * DK), BF16)]
    kn = [msb("kn0", (128, H * DK), BF16)]
    vsil = [msb("vsil0", (128, H * DV), F32)]
    beta_t = [msb(f"beta_t{i}", (128, 2 * H), F32) for i in range(2)]
    qT = [msb("qT0", (128, H * 2 * 128), BF16)]
    kT = [msb("kT0", (128, H * 2 * 128), BF16)]
    vd_b = [msb("vd_b0", (128, H * DV), BF16)]
    u_b = msb("u_b", (128, H * DV), BF16)
    gm_b = msb("gm_b", (128, H * 128), BF16)
    m_f = msb("m_f", (128, H * 128), F32)
    mt_f = msb("mt_f", (128, H * 128), F32)
    x2_f = msb("x2_f", (128, H * 128), F32)
    rt_f = msb("rt_f", (128, H * 128), F32)
    rt_b = msb("rt_b", (128, H * 128), BF16)
    o_nw = msb("o_nw", (128, H * DV), BF16)
    oT_b = msb("oT_b", (128, 8 * 128), BF16)
    s_sb = [msb("s_sb0", (128, D), F32)]

    s_x = sems("s_x")
    s_wm = sems("s_wm")
    s_ld = sems("s_ld")
    s_bc = sems("s_bc")
    s_cast = sems("s_cast")
    s_stat = sems("s_stat")
    s_nrm = sems("s_nrm")
    s_trn = sems("s_trn")
    s_trnv = sems("s_trnv")
    s_pqk = sems("s_pqk")
    s_sqk = sems("s_sqk")
    s_l2 = sems("s_l2")
    s_tqk = sems("s_tqk")
    s_eqk = sems("s_eqk")
    s_pv = sems("s_pv")
    s_sv = sems("s_sv")
    s_bet = sems("s_bet")
    s_kv = sems("s_kv")
    s_vd = sems("s_vd")
    s_A = sems("s_A")
    s_M = sems("s_M")
    s_Mt = sems("s_Mt")
    s_it = sems("s_it")
    s_iv = sems("s_iv")
    s_U = sems("s_U")
    s_Ue = sems("s_Ue")
    s_G = sems("s_G")
    s_Gm = sems("s_Gm")
    s_O = sems("s_O")
    s_oe = sems("s_oe")
    s_to = sems("s_to")
    s_toe = sems("s_toe")
    s_wo = sems("s_wo")
    s_s = sems("s_s")
    s_sw = sems("s_sw")
    s_su = sems("s_su")
    s_dbg = sems("s_dbg")
    s_dbg2 = sems("s_dbg2")

    done = {k: {} for k in ("nrm", "trnv", "l2", "eqk", "vd", "su", "s", "oe",
                            "sv", "bet", "sqk")}

    def cast_load(dst, src, cdim):
        view = src.rearrange("(c p) n -> p c n", p=128)
        p.op("gpsimd", lambda e: e.dma_start(
            out=dst[:].rearrange("p (c n) -> p c n", c=cdim), in_=view),
            inc=s_wm, inc_k=16)

    cast_load(wq_sb, Wq, 8)
    cast_load(wk_sb, Wk, 8)
    cast_load(wv_sb, Wv, 8)
    cast_load(wb_sb, Wb, 8)
    cast_load(wo_sb, Wo, 16)
    wm_done = s_wm.n
    
    p.op("sync", lambda e: e.dma_start(out=lnrow[0:1, 0:D], in_=ln_w), inc=s_ld, inc_k=16)
    p.op("sync", lambda e: e.dma_start(out=lnrow[0:1, D:2 * D], in_=ln_b),
         inc=s_ld, inc_k=16)
    p.op("sync", lambda e: e.dma_start(out=lnrow[0:1, 2 * D:2 * D + DV], in_=onw),
         inc=s_ld, inc_k=16)

    p.wait("tensor", s_ld, s_ld.n)
    p.wait("tensor", s_init)
    for dst, src0, ng in ((lnw_bc, 0, 2), (lnb_bc, D, 2), (onw_bc, 2 * D, 1)):
        for g in range(ng):
            p.op("tensor", lambda e, g=g, src0=src0: e.matmul(
                PB[0][:, 0:512], ones_row[:],
                lnrow[0:1, src0 + g * 512:src0 + (g + 1) * 512],
                start=True, stop=True), inc=s_bc)
            p.wait("vector", s_bc)
            p.op("vector", lambda e, g=g, dst=dst: e.tensor_copy(
                dst[:, g * 512:(g + 1) * 512], PB[0][:, 0:512]), inc=s_bc)
            p.wait("tensor", s_bc)
    bc_done = s_bc.n

    s_z = sems("s_z")
    for h in range(H):
        p.op("vector", lambda e, h=h: e.memset(S_b[h][:], 0.0),
             inc=(s_z if h == H - 1 else None))

    for ei in range(NE):
        p.op("gpsimd", lambda e, ei=ei: e.dma_start(
            out=wbf_up[ei].rearrange("(a b) -> a b", a=II),
            in_=upw[ei].rearrange("d i -> (d i)").rearrange("(a b) -> a b", a=II)),
            inc=s_cast, inc_k=16)
    p.op("gpsimd", lambda e: e.dma_start(
        out=wbf_su[0].rearrange("(a b) -> a b", a=II),
        in_=sw_up.rearrange("d i -> (d i)").rearrange("(a b) -> a b", a=II)),
        inc=s_cast, inc_k=16)
    for ei in range(NE):
        for dt in range(8):
            p.op("gpsimd", lambda e, ei=ei, dt=dt: e.dma_start(
                out=wbf_dn[ei][dt].rearrange("(q c j) -> q c j", q=128, c=32),
                in_=dnw[ei].rearrange("(c q) (dt j) -> dt q c j", q=128, j=128)[dt]),
                inc=s_cast, inc_k=16)
    for dt in range(8):
        p.op("gpsimd", lambda e, dt=dt: e.dma_start(
            out=wbf_sd[0][dt].rearrange("(q c j) -> q c j", q=128, c=32),
            in_=sw_dn.rearrange("(c q) (dt j) -> dt q c j", q=128, j=128)[dt]),
            inc=s_cast, inc_k=16)
    cast_done = s_cast.n

    for c in range(NCH):
        pe = c % 2
        if c >= 1:
            p.wait("sync", s_nrm, done["nrm"][c - 1])
        p.op("sync", lambda e, c=c, pe=pe: e.dma_start(
            out=xch[0][:], in_=x_full[c * 128:(c + 1) * 128, :]), inc=s_x, inc_k=16)
        xld = s_x.n

        p.wait("scalar", s_x, xld)
        if c == 0:
            p.wait("scalar", s_init)
        if c >= 1:
            p.wait("scalar", s_stat, sq_cons)
        p.op("scalar", lambda e, pe=pe: e.activation(sqch[:], xch[0][:], AF.Square),
             inc=s_stat)
        p.wait("vector", s_x, xld)
        p.op("vector", lambda e, pe=pe: e.reduce_sum(stat[:, 0:1], xch[0][:], axis=AX.X))
        p.wait("vector", s_stat, s_stat.n)
        p.op("vector", lambda e: e.reduce_sum(stat[:, 1:2], sqch[:], axis=AX.X),
             inc=s_stat)
        sq_cons = s_stat.n
        p.wait("vector", s_stat, s_stat.n)
        p.op("vector", lambda e: e.tensor_scalar(out=stat[:, 2:3], in0=stat[:, 0:1],
                                                 scalar1=1.0 / D, scalar2=None,
                                                 op0=ALU.mult), inc=s_stat)
        p.wait("vector", s_stat, s_stat.n)
        p.op("vector", lambda e: e.tensor_tensor(stat[:, 3:4], stat[:, 2:3], stat[:, 2:3],
                                                 op=ALU.mult), inc=s_stat)
        p.wait("vector", s_stat, s_stat.n)
        p.op("vector", lambda e: e.scalar_tensor_tensor(
            out=stat[:, 4:5], in0=stat[:, 1:2], scalar=1.0 / D, in1=stat[:, 3:4],
            op0=ALU.mult, op1=ALU.subtract), inc=s_stat)
        p.wait("vector", s_stat, s_stat.n)
        p.op("vector", lambda e: e.tensor_scalar(out=stat[:, 5:6], in0=stat[:, 4:5],
                                                 scalar1=1e-5, scalar2=None, op0=ALU.add),
             inc=s_stat)
        p.wait("vector", s_stat, s_stat.n)
        _rsqrt(p, stat, 5, 1, nsc, cfg.newton)
        p.op("vector", lambda e: e.tensor_tensor(stat[:, 6:7], stat[:, 5:6], stat[:, 5:6],
                                                 op=ALU.mult), inc=s_stat)
        p.wait("vector", s_stat, s_stat.n)
        p.op("vector", lambda e: e.tensor_tensor(stat[:, 6:7], stat[:, 6:7], stat[:, 4:5],
                                                 op=ALU.mult), inc=s_stat)
        p.wait("vector", s_stat, s_stat.n)
        p.op("vector", lambda e: e.tensor_scalar(out=stat[:, 6:7], in0=stat[:, 6:7],
                                                 scalar1=1e-5, scalar2=None, op0=ALU.add),
             inc=s_stat)
        p.wait("vector", s_stat, s_stat.n)
        _rsqrt(p, stat, 6, 1, nsc, cfg.newton)
        p.op("vector", lambda e: e.tensor_tensor(stat[:, 7:8], stat[:, 5:6], stat[:, 6:7],
                                                 op=ALU.mult), inc=s_stat)
        p.wait("vector", s_stat, s_stat.n)
        p.op("vector", lambda e: e.tensor_tensor(stat[:, 8:9], stat[:, 2:3], stat[:, 5:6],
                                                 op=ALU.mult))
        p.op("vector", lambda e: e.tensor_tensor(stat[:, 9:10], stat[:, 2:3], stat[:, 7:8],
                                                 op=ALU.mult), inc=s_stat)
        p.wait("vector", s_stat, s_stat.n)

        if c >= 2:
            p.wait("vector", s_s, done["s"][c - 2])
        if dbg:
            p.wait("vector", s_dbg, s_dbg.n)
        p.op("vector", lambda e, pe=pe: e.tensor_scalar(
            out=xnch[pe][:], in0=xch[0][:], scalar1=stat[:, 5:6], scalar2=stat[:, 8:9],
            op0=ALU.mult, op1=ALU.subtract))
        p.op("vector", lambda e, pe=pe: e.tensor_scalar(
            out=nrm0[:], in0=xch[0][:], scalar1=stat[:, 7:8], scalar2=stat[:, 9:10],
            op0=ALU.mult, op1=ALU.subtract), inc=s_nrm)
        p.wait("vector", s_nrm, s_nrm.n)
        if c == 0:
            p.wait("vector", s_bc, bc_done)
        p.op("vector", lambda e: e.tensor_tensor(nrm0[:], nrm0[:], lnw_bc[:], op=ALU.mult),
             inc=s_nrm)
        p.wait("vector", s_nrm, s_nrm.n)
        if c >= 1:
            p.wait("vector", s_trn, done_trn[c - 1])
        p.op("vector", lambda e, pe=pe: e.tensor_tensor(nrmb[0][:], nrm0[:], lnb_bc[:],
                                                        op=ALU.add), inc=s_nrm)
        done["nrm"][c] = s_nrm.n
        if "dbg_xn" in dbg:
            p.wait("sync", s_nrm, done["nrm"][c])
            p.op("sync", lambda e, c=c, pe=pe: e.dma_start(
                out=dbg["dbg_xn"][c * 128:(c + 1) * 128, :], in_=xnch[pe][:]),
                inc=s_dbg, inc_k=16)

        if c == 0:
            p.wait("tensor", s_wm, wm_done)
        if c >= 2:
            p.wait("tensor", s_trnv, done["trnv"][c - 2])
        if c >= 1:
            p.wait("tensor", s_toe, done_toe[c - 1])
        p.wait("tensor", s_nrm, done["nrm"][c])
        for g in range(8):
            p.op("tensor", lambda e, g=g, pe=pe: e.transpose(
                PT[:, g * 128:(g + 1) * 128], nrmb[0][:, g * 128:(g + 1) * 128],
                ident_b[:]), inc=(s_trn if g == 7 else None))
        done_trn = done_trn if c > 0 else {}
        done_trn[c] = s_trn.n
        p.wait("vector", s_trn, s_trn.n)
        if c >= 1:
            p.wait("vector", s_pv, done_pv[c - 1])
        p.op("vector", lambda e, pe=pe: e.tensor_copy(nT[0][:], PT[:]), inc=s_trnv)
        done["trnv"][c] = s_trnv.n

        p.wait("tensor", s_trnv, done["trnv"][c])
        if c >= 1:
            p.wait("tensor", s_sqk, done["sqk"][c - 1])
        pqk_base = s_pqk.n
        for g in range(2):
            for cc in range(8):
                p.op("tensor", lambda e, g=g, cc=cc, pe=pe: e.matmul(
                    PB[g][:, 0:512], nT[0][:, cc * 128:(cc + 1) * 128],
                    wq_sb[:, cc * 1024 + g * 512:cc * 1024 + (g + 1) * 512],
                    start=(cc == 0), stop=(cc == 7)), inc=(s_pqk if cc == 7 else None))
        for g in range(2):
            for cc in range(8):
                p.op("tensor", lambda e, g=g, cc=cc, pe=pe: e.matmul(
                    PB[2 + g][:, 0:512], nT[0][:, cc * 128:(cc + 1) * 128],
                    wk_sb[:, cc * 1024 + g * 512:cc * 1024 + (g + 1) * 512],
                    start=(cc == 0), stop=(cc == 7)), inc=(s_pqk if cc == 7 else None))
        if c >= 1:
            p.wait("scalar", s_l2, done["l2"][c - 1])
        for g in range(4):
            p.wait("scalar", s_pqk, pqk_base + g + 1)
            dst = (qraw, kraw)[g // 2]
            p.op("scalar", lambda e, g=g, pe=pe, dst=dst: e.activation(
                dst[0][:, (g % 2) * 512:((g % 2) + 1) * 512], PB[g][:, 0:512], AF.Silu),
                inc=(s_sqk if g == 3 else None))
        p.wait("scalar", s_sqk, s_sqk.n)
        p.op("scalar", lambda e, pe=pe: e.activation(sqch[:], qraw[0][:], AF.Square))
        p.op("scalar", lambda e, pe=pe: e.activation(nrm0[:], kraw[0][:], AF.Square),
             inc=s_sqk)
        done["sqk"][c] = s_sqk.n
        p.wait("vector", s_sqk, done["sqk"][c])
        for h in range(H):
            p.op("vector", lambda e, h=h: e.reduce_sum(
                stat[:, 10 + h:11 + h], sqch[:, h * DK:(h + 1) * DK], axis=AX.X))
        for h in range(H):
            p.op("vector", lambda e, h=h: e.reduce_sum(
                stat[:, 14 + h:15 + h], nrm0[:, h * DK:(h + 1) * DK], axis=AX.X),
                inc=(s_l2 if h == H - 1 else None))
        p.wait("vector", s_l2, s_l2.n)
        p.op("vector", lambda e: e.tensor_scalar(out=stat[:, 10:18], in0=stat[:, 10:18],
                                                 scalar1=1e-6, scalar2=None, op0=ALU.add),
             inc=s_l2)
        p.wait("vector", s_l2, s_l2.n)
        _rsqrt(p, stat, 10, 8, nsc, cfg.newton)
        p.op("vector", lambda e: e.tensor_scalar(out=stat[:, 10:14], in0=stat[:, 10:14],
                                                 scalar1=DK ** -0.5, scalar2=None,
                                                 op0=ALU.mult), inc=s_l2)
        p.wait("vector", s_l2, s_l2.n)
        if c >= 1:
            p.wait("vector", s_su, done["su"][c - 1])
            p.wait("vector", s_eqk, done["eqk"][c - 1])
        if dbg:
            p.wait("vector", s_dbg2, s_dbg2.n)
        for h in range(H):
            p.op("vector", lambda e, h=h, pe=pe: e.tensor_scalar(
                out=qn[0][:, h * DK:(h + 1) * DK], in0=qraw[0][:, h * DK:(h + 1) * DK],
                scalar1=stat[:, 10 + h:11 + h], scalar2=None, op0=ALU.mult))
        for h in range(H):
            p.op("vector", lambda e, h=h, pe=pe: e.tensor_scalar(
                out=kn[0][:, h * DK:(h + 1) * DK], in0=kraw[0][:, h * DK:(h + 1) * DK],
                scalar1=stat[:, 14 + h:15 + h], scalar2=None, op0=ALU.mult),
                inc=(s_l2 if h == H - 1 else None))
        done["l2"][c] = s_l2.n
        if "dbg_q" in dbg:
            p.wait("gpsimd", s_l2, done["l2"][c])
            p.op("gpsimd", lambda e, c=c, pe=pe: e.dma_start(
                out=dbg["dbg_q"][c * 128:(c + 1) * 128, :], in_=qn[0][:]),
                inc=s_dbg2, inc_k=16)
            p.op("gpsimd", lambda e, c=c, pe=pe: e.dma_start(
                out=dbg["dbg_k"][c * 128:(c + 1) * 128, :], in_=kn[0][:]),
                inc=s_dbg2, inc_k=16)

        p.wait("tensor", s_l2, done["l2"][c])
        for src, dstT in ((qn, qT), (kn, kT)):
            for g8 in range(8):
                p.op("tensor", lambda e, g8=g8, pe=pe, src=src: e.transpose(
                    PT[:, g8 * 128:(g8 + 1) * 128], src[0][:, g8 * 128:(g8 + 1) * 128],
                    ident_b[:]), inc=(s_tqk if g8 == 7 else None))
            p.wait("vector", s_tqk, s_tqk.n)
            p.op("vector", lambda e, pe=pe, dstT=dstT: e.tensor_copy(dstT[0][:], PT[:]),
                 inc=s_eqk)
            p.wait("tensor", s_eqk, s_eqk.n)
        done["eqk"][c] = s_eqk.n

        p.wait("tensor", s_sqk, done["sqk"][c])
        if c >= 1:
            p.wait("tensor", s_bet, done["bet"][c - 1])
            p.wait("tensor", s_vd, done["vd"][c - 1])
        pv_base = s_pv.n
        for g in range(4):
            for cc in range(8):
                p.op("tensor", lambda e, g=g, cc=cc, pe=pe: e.matmul(
                    PB[g][:, 0:512], nT[0][:, cc * 128:(cc + 1) * 128],
                    wv_sb[:, cc * 2048 + g * 512:cc * 2048 + (g + 1) * 512],
                    start=(cc == 0), stop=(cc == 7)), inc=(s_pv if cc == 7 else None))
        for cc in range(8):
            p.op("tensor", lambda e, cc=cc, pe=pe: e.matmul(
                PI[:, 0:H], nT[0][:, cc * 128:(cc + 1) * 128],
                wb_sb[:, cc * H:(cc + 1) * H], start=(cc == 0), stop=(cc == 7)),
                inc=(s_pv if cc == 7 else None))
        done_pv = done_pv if c > 0 else {}
        done_pv[c] = s_pv.n
        if c >= 1:
            p.wait("scalar", s_vd, done["vd"][c - 1])
        if dbg:
            p.wait("scalar", s_dbg, s_dbg.n)
        for g in range(4):
            p.wait("scalar", s_pv, pv_base + g + 1)
            p.op("scalar", lambda e, g=g, pe=pe: e.activation(
                vsil[0][:, g * 512:(g + 1) * 512], PB[g][:, 0:512], AF.Silu),
                inc=(s_sv if g == 3 else None))
        done["sv"][c] = s_sv.n
        p.wait("scalar", s_pv, pv_base + 5)
        if dbg:
            p.wait("scalar", s_dbg, s_dbg.n)
        p.op("scalar", lambda e, pe=pe: e.activation(beta_t[pe][:, 0:H], PI[:, 0:H],
                                                     AF.Tanh, scale=0.5), inc=s_bet)
        p.wait("vector", s_bet, s_bet.n)
        p.op("vector", lambda e, pe=pe: e.tensor_scalar(
            out=beta_t[pe][:, 0:H], in0=beta_t[pe][:, 0:H], scalar1=0.5, scalar2=0.5,
            op0=ALU.mult, op1=ALU.add), inc=s_bet)
        p.wait("vector", s_bet, s_bet.n)
        p.op("vector", lambda e, pe=pe: e.tensor_scalar(
            out=beta_t[pe][:, H:2 * H], in0=beta_t[pe][:, 0:H], scalar1=-1.0,
            scalar2=None, op0=ALU.mult), inc=s_bet)
        done["bet"][c] = s_bet.n
        if "dbg_v" in dbg:
            p.wait("sync", s_sv, done["sv"][c])
            p.op("sync", lambda e, c=c, pe=pe: e.dma_start(
                out=dbg["dbg_v"][c * 128:(c + 1) * 128, :], in_=vsil[0][:]),
                inc=s_dbg, inc_k=16)
        if "dbg_beta" in dbg:
            p.wait("sync", s_bet, done["bet"][c])
            p.op("sync", lambda e, c=c, pe=pe: e.dma_start(
                out=dbg["dbg_beta"][c * 128:(c + 1) * 128, :], in_=beta_t[pe][:, 0:H]),
                inc=s_dbg, inc_k=16)

        p.wait("tensor", s_sv, done["sv"][c])
        p.wait("tensor", s_eqk, done["eqk"][c])
        if c >= 1:
            p.wait("tensor", s_su, done["su"][c - 1])
        if c >= 1:
            p.wait("tensor", s_vd, done["vd"][c - 1])
        for h in range(H):
            for g in range(2):
                p.op("tensor", lambda e, h=h, g=g, pe=pe: e.matmul(
                    PB[h][:, 0:512], kT[0][:, (h * 2 + g) * 128:(h * 2 + g + 1) * 128],
                    S_b[h][:, g * 512:(g + 1) * 512], start=(g == 0), stop=(g == 1)),
                    inc=(s_kv if g == 1 else None))
        kv_now = s_kv.n
        p.wait("vector", s_kv, kv_now)
        p.wait("vector", s_sv, done["sv"][c])
        p.wait("vector", s_bet, done["bet"][c])
        for h in range(H):
            p.op("vector", lambda e, h=h, pe=pe: e.tensor_tensor(
                osq[:, (h % 2) * DV:((h % 2) + 1) * DV],
                vsil[0][:, h * DV:(h + 1) * DV],
                PB[h][:, 0:512], op=ALU.subtract), inc=s_vd)
            if h == 0:
                if c >= 1:
                    p.wait("vector", s_U, u_done_prev)
            p.wait("vector", s_vd, s_vd.n)
            p.op("vector", lambda e, h=h, pe=pe: e.tensor_scalar(
                out=vd_b[0][:, h * DV:(h + 1) * DV],
                in0=osq[:, (h % 2) * DV:((h % 2) + 1) * DV],
                scalar1=beta_t[pe][:, h:h + 1], scalar2=None, op0=ALU.mult),
                inc=(s_vd if h == H - 1 else None))
        done["vd"][c] = s_vd.n

        p.wait("tensor", s_vd, done["vd"][c])
        for h in range(H):
            for g in range(2):
                p.op("tensor", lambda e, h=h, g=g, pe=pe: e.matmul(
                    PB[h][:, 0:128], kT[0][:, (h * 2 + g) * 128:(h * 2 + g + 1) * 128],
                    kT[0][:, (h * 2 + g) * 128:(h * 2 + g + 1) * 128],
                    start=(g == 0), stop=(g == 1)), inc=(s_A if g == 1 else None))
        p.wait("vector", s_A, s_A.n)
        for h in range(H):
            p.op("vector", lambda e, h=h, pe=pe: e.scalar_tensor_tensor(
                out=m_f[:, h * 128:(h + 1) * 128], in0=PB[h][:, 0:128],
                scalar=beta_t[pe][:, H + h:H + h + 1], in1=tril01[:],
                op0=ALU.mult, op1=ALU.mult), inc=(s_M if h == H - 1 else None))
        p.wait("tensor", s_M, s_M.n)
        for h in range(H):
            p.op("tensor", lambda e, h=h: e.transpose(
                PI[:, 512 + h * 128:512 + (h + 1) * 128], m_f[:, h * 128:(h + 1) * 128],
                ident_f[:]), inc=(s_Mt if h == H - 1 else None))
        p.wait("vector", s_Mt, s_Mt.n)
        p.op("vector", lambda e: e.tensor_copy(mt_f[:], PI[:, 512:1024]), inc=s_Mt)
        p.wait("vector", s_Mt, s_Mt.n)
        for h in range(H):
            p.op("vector", lambda e, h=h: e.tensor_tensor(
                rt_f[:, h * 128:(h + 1) * 128], mt_f[:, h * 128:(h + 1) * 128],
                ident_f[:], op=ALU.add), inc=(s_Mt if h == H - 1 else None))
        p.wait("tensor", s_Mt, s_Mt.n)
        for lev in range(1, 7):
            last = lev == 6
            for h in range(H):
                p.op("tensor", lambda e, h=h: e.matmul(
                    PI[:, h * 128:(h + 1) * 128], mt_f[:, h * 128:(h + 1) * 128],
                    m_f[:, h * 128:(h + 1) * 128], start=True, stop=True),
                    inc=(s_it if (last and h == H - 1) else None))
            if not last:
                for h in range(H):
                    p.op("tensor", lambda e, h=h: e.matmul(
                        PI[:, 512 + h * 128:512 + (h + 1) * 128],
                        m_f[:, h * 128:(h + 1) * 128], mt_f[:, h * 128:(h + 1) * 128],
                        start=True, stop=True), inc=(s_it if h == H - 1 else None))
            p.wait("vector", s_it, s_it.n)
            p.op("vector", lambda e: e.tensor_copy(x2_f[:], PI[:, 0:512]), inc=s_iv)
            if not last:
                p.op("vector", lambda e: e.tensor_copy(mt_f[:], PI[:, 512:1024]),
                     inc=s_iv)
            p.wait("tensor", s_iv, s_iv.n)
            for h in range(H):
                p.op("tensor", lambda e, h=h: e.matmul(
                    PI[:, 1024 + h * 128:1024 + (h + 1) * 128],
                    x2_f[:, h * 128:(h + 1) * 128], rt_f[:, h * 128:(h + 1) * 128],
                    start=True, stop=True), inc=(s_it if h == H - 1 else None))
            p.wait("vector", s_it, s_it.n)
            for h in range(H):
                p.op("vector", lambda e, h=h: e.tensor_tensor(
                    rt_f[:, h * 128:(h + 1) * 128], rt_f[:, h * 128:(h + 1) * 128],
                    PI[:, 1024 + h * 128:1024 + (h + 1) * 128], op=ALU.add),
                    inc=(s_iv if h == H - 1 else None))
            if not last:
                p.op("vector", lambda e: e.tensor_copy(m_f[:], x2_f[:]), inc=s_iv)
            p.wait("tensor", s_iv, s_iv.n)
        p.wait("vector", s_iv, s_iv.n)
        p.op("vector", lambda e: e.tensor_copy(rt_b[:], rt_f[:]), inc=s_iv)

        p.wait("tensor", s_iv, s_iv.n)
        for h in range(H):
            p.op("tensor", lambda e, h=h, pe=pe: e.matmul(
                PB[h][:, 0:512], rt_b[:, h * 128:(h + 1) * 128],
                vd_b[0][:, h * DV:(h + 1) * DV], start=True, stop=True),
                inc=(s_U if h == H - 1 else None))
        u_done_prev = s_U.n
        p.wait("vector", s_U, s_U.n)
        for h in range(H):
            p.op("vector", lambda e, h=h: e.tensor_copy(
                u_b[:, h * DV:(h + 1) * DV], PB[h][:, 0:512]),
                inc=(s_Ue if h == H - 1 else None))

        p.wait("tensor", s_Ue, s_Ue.n)
        for h in range(H):
            for g in range(2):
                p.op("tensor", lambda e, h=h, g=g, pe=pe: e.matmul(
                    PI[:, h * 128:(h + 1) * 128],
                    kT[0][:, (h * 2 + g) * 128:(h * 2 + g + 1) * 128],
                    qT[0][:, (h * 2 + g) * 128:(h * 2 + g + 1) * 128],
                    start=(g == 0), stop=(g == 1)), inc=(s_G if g == 1 else None))
        p.wait("vector", s_G, s_G.n)
        for h in range(H):
            p.op("vector", lambda e, h=h: e.tensor_tensor(
                gm_b[:, h * 128:(h + 1) * 128], PI[:, h * 128:(h + 1) * 128],
                triu01[:], op=ALU.mult), inc=(s_Gm if h == H - 1 else None))

        p.wait("tensor", s_Gm, s_Gm.n)
        for h in range(H):
            for g in range(2):
                p.op("tensor", lambda e, h=h, g=g, pe=pe: e.matmul(
                    PB[h][:, 0:512], qT[0][:, (h * 2 + g) * 128:(h * 2 + g + 1) * 128],
                    S_b[h][:, g * 512:(g + 1) * 512], start=(g == 0), stop=False))
            p.op("tensor", lambda e, h=h: e.matmul(
                PB[h][:, 0:512], gm_b[:, h * 128:(h + 1) * 128],
                u_b[:, h * DV:(h + 1) * DV], start=False, stop=True),
                inc=(s_O if h == H - 1 else None))
        done_O = done_O if c > 0 else {}
        done_O[c] = s_O.n
        p.wait("scalar", s_O, s_O.n)
        for h in range(H):
            p.op("scalar", lambda e, h=h: e.activation(
                osq[:, (h % 2) * DV:((h % 2) + 1) * DV], PB[h][:, 0:512], AF.Square),
                inc=s_oe)
            p.wait("vector", s_oe, s_oe.n)
            p.op("vector", lambda e, h=h: e.reduce_sum(
                stat[:, 18 + h:19 + h], osq[:, (h % 2) * DV:((h % 2) + 1) * DV],
                axis=AX.X), inc=s_oe)
            if h < H - 1:
                p.wait("scalar", s_oe, s_oe.n)
        p.wait("vector", s_oe, s_oe.n)
        p.op("vector", lambda e: e.tensor_scalar(out=stat[:, 18:22], in0=stat[:, 18:22],
                                                 scalar1=1.0 / DV, scalar2=1e-5,
                                                 op0=ALU.mult, op1=ALU.add), inc=s_oe)
        p.wait("vector", s_oe, s_oe.n)
        _rsqrt(p, stat, 18, 4, nsc, cfg.newton)
        if dbg:
            p.wait("vector", s_dbg2, s_dbg2.n)
            p.wait("vector", s_dbg, s_dbg.n)
        for h in range(H):
            p.op("vector", lambda e, h=h: e.scalar_tensor_tensor(
                out=o_nw[:, h * DV:(h + 1) * DV], in0=PB[h][:, 0:512],
                scalar=stat[:, 18 + h:19 + h], in1=onw_bc[:],
                op0=ALU.mult, op1=ALU.mult), inc=(s_oe if h == H - 1 else None))
        done["oe"][c] = s_oe.n
        if "dbg_o" in dbg:
            p.wait("gpsimd", s_oe, done["oe"][c])
            p.op("gpsimd", lambda e, c=c: e.dma_start(
                out=dbg["dbg_o"][c * 128:(c + 1) * 128, :], in_=o_nw[:]),
                inc=s_dbg2, inc_k=16)

        p.wait("tensor", s_oe, done["oe"][c])
        for h in range(H):
            for g in range(2):
                p.op("tensor", lambda e, h=h, g=g, pe=pe: e.matmul(
                    PB[(h % 2) * 2 + g][:, 0:512],
                    kn[0][:, h * DK + g * 128:h * DK + (g + 1) * 128],
                    u_b[:, h * DV:(h + 1) * DV], start=True, stop=True),
                    inc=(s_kv if g == 1 else None))
            p.wait("vector", s_kv, s_kv.n)
            for g in range(2):
                p.op("vector", lambda e, h=h, g=g: e.tensor_tensor(
                    S_b[h][:, g * 512:(g + 1) * 512], S_b[h][:, g * 512:(g + 1) * 512],
                    PB[(h % 2) * 2 + g][:, 0:512], op=ALU.add),
                    inc=(s_su if g == 1 else None))
            if h % 2 == 1 and h < H - 1:
                p.wait("tensor", s_su, s_su.n)
        done["su"][c] = s_su.n

        p.wait("tensor", s_su, done["su"][c])
        for grp in range(2):
            for j in range(8):
                hd = grp * 8 + j
                p.op("tensor", lambda e, hd=hd, j=j: e.transpose(
                    PT[:, j * 128:(j + 1) * 128], o_nw[:, hd * 128:(hd + 1) * 128],
                    ident_b[:]), inc=(s_to if j == 7 else None))
            p.wait("vector", s_to, s_to.n)
            p.op("vector", lambda e, grp=grp: e.tensor_copy(
                oT_b[:], PT[:]), inc=s_toe)
            p.wait("tensor", s_toe, s_toe.n)
            for g in range(2):
                for cc in range(8):
                    p.op("tensor", lambda e, g=g, cc=cc, grp=grp: e.matmul(
                        PB[g][:, 0:512], oT_b[:, cc * 128:(cc + 1) * 128],
                        wo_sb[:, (grp * 8 + cc) * 1024 + g * 512:
                              (grp * 8 + cc) * 1024 + (g + 1) * 512],
                        start=(grp == 0 and cc == 0), stop=(grp == 1 and cc == 7)),
                        inc=(s_wo if (grp == 1 and cc == 7) else None))
        done_toe = done_toe if c > 0 else {}
        done_toe[c] = s_toe.n
        p.wait("vector", s_wo, s_wo.n)
        if c >= 1:
            p.wait("vector", s_sw, done_sw[c - 1])
        if dbg:
            p.wait("vector", s_dbg, s_dbg.n)
        for g in range(2):
            p.op("vector", lambda e, g=g, pe=pe: e.tensor_tensor(
                s_sb[0][:, g * 512:(g + 1) * 512], xnch[pe][:, g * 512:(g + 1) * 512],
                PB[g][:, 0:512], op=ALU.add), inc=(s_s if g == 1 else None))
        done["s"][c] = s_s.n
        done_sw = done_sw if c > 0 else {}
        if c == 0:
            p.wait("gpsimd", s_cast, cast_done)
            p.wait("gpsimd", s_ld, s_ld.n)
        p.wait("gpsimd", s_s, done["s"][c])
        p.op("gpsimd", lambda e, c=c, pe=pe: e.dma_start(
            out=s_scr[c * 128:(c + 1) * 128, :],
            in_=s_sb[0][:]), inc=s_sw, inc_k=16)
        done_sw[c] = s_sw.n
        if "dbg_s" in dbg:
            p.wait("sync", s_s, s_s.n)
            p.op("sync", lambda e, c=c, pe=pe: e.dma_start(
                out=dbg["dbg_s"][c * 128:(c + 1) * 128, :], in_=s_sb[0][:]),
                inc=s_dbg, inc_k=16)

    sw_all = s_sw.n
    s_ph1 = sems("s_ph1")
    for eng in ENGINES:
        if eng in ("sync", "tensor", "scalar"):
            p.op(eng, lambda e: e.nop(), inc=s_ph1)
        else:
            p.op(eng, lambda e: e.engine_nop(), inc=s_ph1)
    ph1 = s_ph1.n
    mes.close()

    oes = ExitStack()

    def osb(name, shape, dtype):
        return oes.enter_context(nc.sbuf_tensor(name, list(shape), dtype))

    xnT_st = osb("xnT_st", (128, 8 * STOK), BF16)
    comb = osb("comb", (128, 8 * STOK), BF16)
    hact = osb("hact", (128, 32 * STOK), BF16)
    ubuf = [osb(f"ubuf{i}", (128, 8 * 512), BF16) for i in range(2)]
    dbuf = [osb(f"dbuf{i}", (128, 32 * 128), BF16) for i in range(2)]
    gw_sb = osb("gw_sb", (128, 8 * E), F32)
    gb_bc = osb("gb_bc", (128, E), F32)
    upb_sb = osb("upb_sb", (128, NE * 32), F32)
    dnb_sb = osb("dnb_sb", (128, NE * 8), F32)
    sbu_sb = osb("sbu_sb", (128, 32), F32)
    sbd_sb = osb("sbd_sb", (128, 8), F32)
    xo_ch = [osb(f"xo_ch{i}", (128, D), F32) for i in range(2)]
    xn_mo = osb("xn_mo", (128, D), F32)
    xnb_mo = osb("xnb_mo", (128, D), BF16)
    sq_mo = osb("sq_mo", (128, D), F32)
    xnf_T = osb("xnf_T", (128, 8 * 128), F32)
    sc_tm = osb("sc_tm", (128, E), F32)
    w_tm = osb("w_tm", (128, E), F32)
    wT_all = osb("wT_all", (8, STOK), F32)
    rtm = osb("rtm", (128, STC), F32)
    rT_sb = osb("rT_sb", (16, 128), F32)
    wrow = osb("wrow", (1, STOK), F32)
    wbc = osb("wbc", (128, 512), F32)
    tmp_ev = osb("tmp_ev", (128, 512), F32)
    mx8 = osb("mx8", (128, STOK), F32)
    rbc = osb("rbc", (128, STOK), F32)
    graw = osb("graw", (128, STOK), F32)
    zg = osb("zg", (128, 8 * STOK), BF16)
    zgT = osb("zgT", (128, D), BF16)
    sch = [osb("sch0", (128, D), F32)]

    s_mo = sems("s_mo")
    s_xo = [sems("s_xo0"), sems("s_xo1")]
    s_ln2 = sems("s_ln2")
    s_xb = sems("s_xb")
    s_tx = sems("s_tx")
    s_txe = sems("s_txe")
    s_gt = sems("s_gt")
    s_gs = sems("s_gs")
    s_gw = sems("s_gw")
    s_wup = [sems("s_wup0"), sems("s_wup1")]
    s_wdn = [sems("s_wdn0"), sems("s_wdn1")]
    s_mmu = sems("s_mmu")
    s_gel = sems("s_gel")
    s_mmd = sems("s_mmd")
    s_evm = sems("s_evm")
    s_eva = sems("s_eva")
    s_wb2 = sems("s_wb2")
    s_wrow = sems("s_wrow")
    s_n2 = sems("s_n2")
    s_n2g = sems("s_n2g")
    s_tz = sems("s_tz")
    s_zg = sems("s_zg")
    s_ep = sems("s_ep")
    s_epg = sems("s_epg")
    s_out = sems("s_out")
    s_cz = sems("s_cz")

    for eng in ENGINES:
        p.wait(eng, s_ph1, ph1)

    p.op("sync", lambda e: e.dma_start(
        out=gw_sb[:].rearrange("p (c n) -> p c n", c=8),
        in_=gate_w.rearrange("(c p) n -> p c n", p=128)), inc=s_mo, inc_k=16)
    p.op("sync", lambda e: e.dma_start(out=upb_sb[:], in_=upb_t), inc=s_mo, inc_k=16)
    p.op("sync", lambda e: e.dma_start(out=dnb_sb[:], in_=dnb_t), inc=s_mo, inc_k=16)
    p.op("sync", lambda e: e.dma_start(out=sbu_sb[:], in_=sb_up_t), inc=s_mo, inc_k=16)
    p.op("sync", lambda e: e.dma_start(out=sbd_sb[:], in_=sb_dn_t), inc=s_mo, inc_k=16)
    p.op("sync", lambda e: e.dma_start(out=wrow[0:1, 0:E], in_=gate_b), inc=s_mo, inc_k=16)
    p.wait("tensor", s_mo, s_mo.n)
    s_gbb = sems("s_gbb")
    p.op("tensor", lambda e: e.matmul(PB[0][:, 0:E], ones_row[:], wrow[0:1, 0:E],
                                      start=True, stop=True), inc=s_gbb)
    p.wait("vector", s_gbb, s_gbb.n)
    p.op("vector", lambda e: e.tensor_copy(gb_bc[:], PB[0][:, 0:E]), inc=s_gbb)
    gbb_done = s_gbb.n

    done_gs = {}
    done_txe = {}
    up_ld = {}
    dn_ld = {}
    up_cons = {}
    dn_cons = {}
    out_cons = {}
    nup = 0
    ndn = 0

    def load_up(ei, grp):
        gi = ei * 4 + grp
        if gi >= 2:
            p.wait("sync", s_mmu, up_cons[gi - 2])
        src = wbf_su[0] if ei == NE else wbf_up[ei]
        p.op("sync", lambda e, grp=grp, src=src, gi=gi: e.dma_start(
            out=ubuf[gi % 2][:].rearrange("p (c n) -> p c n", c=8),
            in_=src.rearrange("(c p n) -> p c n", p=128, n=II)[:, :, grp * 1024:(grp + 1) * 1024]),
            inc=s_wup, inc_k=16)
        up_ld[gi] = s_wup.n

    def load_dn(ei, dt):
        di = ei * 8 + dt
        if di >= 2:
            p.wait("scalar", s_mmd, dn_cons[di - 2])
        srcd = wbf_sd[0] if ei == NE else wbf_dn[ei]
        p.op("scalar", lambda e, dt=dt, srcd=srcd, di=di: e.dma_start(
            out=dbuf[di % 2][:], in_=srcd[dt].rearrange("(q m) -> q m", q=128)),
            inc=s_wdn, inc_k=16)
        dn_ld[di] = s_wdn.n

    for st in range(NST):
        st_c0 = st * STC
        for ci in range(STC):
            c = st_c0 + ci
            pe = ci % 2
            if ci >= 2:
                p.wait("sync", s_gs, done_gs[ci - 2])
            if ci <= 1 and st > 0:
                p.wait("sync", s_zg, prev_epi)
            p.op("sync", lambda e, c=c, pe=pe: e.dma_start(
                out=xo_ch[pe][:], in_=x_own[c * 128:(c + 1) * 128, :]),
                inc=s_xo[pe], inc_k=16)
            xod = s_xo[pe].n
            p.wait("scalar", s_xo[pe], xod)
            if st == 0 and ci == 0:
                p.wait("scalar", s_ph1, ph1)
            if not (st == 0 and ci == 0):
                p.wait("scalar", s_ln2, sq2_cons)
            p.op("scalar", lambda e, pe=pe: e.activation(sq_mo[:], xo_ch[pe][:],
                                                         AF.Square), inc=s_ln2)
            p.wait("vector", s_xo[pe], xod)
            p.op("vector", lambda e, pe=pe: e.reduce_sum(stat[:, 0:1], xo_ch[pe][:],
                                                         axis=AX.X))
            p.wait("vector", s_ln2, s_ln2.n)
            p.op("vector", lambda e: e.reduce_sum(stat[:, 1:2], sq_mo[:], axis=AX.X),
                 inc=s_ln2)
            sq2_cons = s_ln2.n
            p.wait("vector", s_ln2, s_ln2.n)
            p.op("vector", lambda e: e.tensor_scalar(out=stat[:, 2:3], in0=stat[:, 0:1],
                                                     scalar1=1.0 / D, scalar2=None,
                                                     op0=ALU.mult), inc=s_ln2)
            p.wait("vector", s_ln2, s_ln2.n)
            p.op("vector", lambda e: e.tensor_tensor(stat[:, 3:4], stat[:, 2:3],
                                                     stat[:, 2:3], op=ALU.mult),
                 inc=s_ln2)
            p.wait("vector", s_ln2, s_ln2.n)
            p.op("vector", lambda e: e.scalar_tensor_tensor(
                out=stat[:, 4:5], in0=stat[:, 1:2], scalar=1.0 / D, in1=stat[:, 3:4],
                op0=ALU.mult, op1=ALU.subtract), inc=s_ln2)
            p.wait("vector", s_ln2, s_ln2.n)
            p.op("vector", lambda e: e.tensor_scalar(out=stat[:, 5:6], in0=stat[:, 4:5],
                                                     scalar1=1e-5, scalar2=None,
                                                     op0=ALU.add), inc=s_ln2)
            p.wait("vector", s_ln2, s_ln2.n)
            _rsqrt(p, stat, 5, 1, nsc, cfg.newton)
            p.op("vector", lambda e: e.tensor_tensor(stat[:, 8:9], stat[:, 2:3],
                                                     stat[:, 5:6], op=ALU.mult),
                 inc=s_ln2)
            p.wait("vector", s_ln2, s_ln2.n)
            p.op("vector", lambda e, pe=pe: e.tensor_scalar(
                out=xn_mo[:], in0=xo_ch[pe][:], scalar1=stat[:, 5:6],
                scalar2=stat[:, 8:9], op0=ALU.mult, op1=ALU.subtract), inc=s_ln2)
            p.wait("vector", s_ln2, s_ln2.n)
            p.op("vector", lambda e: e.tensor_copy(xnb_mo[:], xn_mo[:]), inc=s_ln2)
            ln2_done = s_ln2.n

            p.wait("tensor", s_ln2, ln2_done)
            if ci >= 2:
                p.wait("tensor", s_txe, done_txe[ci - 2])
            if not (st == 0 and ci == 0):
                p.wait("tensor", s_gs, prev_gs)
            for g in range(8):
                p.op("tensor", lambda e, g=g: e.transpose(
                    PT[:, g * 128:(g + 1) * 128], xnb_mo[:, g * 128:(g + 1) * 128],
                    ident_b[:]), inc=(s_tx if g == 7 else None))
            for half4 in range(2):
                for g in range(4):
                    p.op("tensor", lambda e, g=g, half4=half4: e.transpose(
                        PI[:, g * 128:(g + 1) * 128],
                        xn_mo[:, (half4 * 4 + g) * 128:(half4 * 4 + g + 1) * 128],
                        ident_f[:]), inc=(s_tx if g == 3 else None))
                p.wait("vector", s_tx, s_tx.n)
                p.op("vector", lambda e, half4=half4: e.tensor_copy(
                    xnf_T[:, half4 * 512:(half4 + 1) * 512], PI[:, 0:512]), inc=s_txe)
                if half4 == 0:
                    p.wait("tensor", s_txe, s_txe.n)
            p.wait("vector", s_tx, s_tx.n - 2)
            p.op("vector", lambda e, ci=ci: e.tensor_copy(
                xnT_st[:].rearrange("p (g n) -> p g n", g=8)[:, :, ci * 128:(ci + 1) * 128],
                PT[:].rearrange("p (g n) -> p g n", g=8)), inc=s_txe)
            done_txe[ci] = s_txe.n

            p.wait("tensor", s_txe, done_txe[ci])
            if st == 0 and ci == 0:
                p.wait("tensor", s_gbb, gbb_done)
            if not (st == 0 and ci == 0):
                p.wait("tensor", s_gs, sc_cons)
            for cc in range(8):
                p.op("tensor", lambda e, cc=cc: e.matmul(
                    PB[0][:, 0:E], xnf_T[:, cc * 128:(cc + 1) * 128],
                    gw_sb[:, cc * E:(cc + 1) * E], start=(cc == 0), stop=(cc == 7)),
                    inc=(s_gt if cc == 7 else None))
            p.wait("vector", s_gt, s_gt.n)
            if dbg:
                p.wait("vector", s_mo, s_mo.n)
            p.op("vector", lambda e: e.tensor_tensor(sc_tm[:], PB[0][:, 0:E], gb_bc[:],
                                                     op=ALU.add), inc=s_gs)
            sc_cons = s_gs.n
            p.wait("scalar", s_gs, s_gs.n)
            p.op("scalar", lambda e: e.activation(w_tm[:], sc_tm[:], AF.Tanh, scale=0.5),
                 inc=s_gs)
            p.wait("vector", s_gs, s_gs.n)
            p.op("vector", lambda e: e.tensor_scalar(out=sc_tm[:], in0=w_tm[:],
                                                     scalar1=0.5, scalar2=0.5,
                                                     op0=ALU.mult, op1=ALU.add), inc=s_gs)
            p.wait("vector", s_gs, s_gs.n)
            if "dbg_scores" in dbg:
                p.wait("sync", s_gs, s_gs.n)
                p.op("sync", lambda e, c=c: e.dma_start(
                    out=dbg["dbg_scores"][c * 128:(c + 1) * 128, :], in_=sc_tm[:]),
                    inc=s_mo, inc_k=16)
            p.op("vector", lambda e: e.reduce_max(stat[:, 0:1], sc_tm[:], axis=AX.X),
                 inc=s_gs)
            p.wait("vector", s_gs, s_gs.n)
            p.op("vector", lambda e: e.tensor_scalar(out=w_tm[:], in0=sc_tm[:],
                                                     scalar1=stat[:, 0:1], scalar2=None,
                                                     op0=ALU.is_ge), inc=s_gs)
            p.wait("vector", s_gs, s_gs.n)
            p.op("vector", lambda e: e.scalar_tensor_tensor(
                out=w_tm[:], in0=w_tm[:], scalar=-1e30, in1=sc_tm[:],
                op0=ALU.mult, op1=ALU.add), inc=s_gs)
            p.wait("vector", s_gs, s_gs.n)
            p.op("vector", lambda e: e.reduce_max(stat[:, 1:2], w_tm[:], axis=AX.X),
                 inc=s_gs)
            p.wait("vector", s_gs, s_gs.n)
            p.op("vector", lambda e: e.tensor_tensor(stat[:, 2:3], stat[:, 0:1],
                                                     stat[:, 1:2], op=ALU.add), inc=s_gs)
            p.wait("vector", s_gs, s_gs.n)
            p.op("vector", lambda e: e.tensor_scalar(out=stat[:, 2:3], in0=stat[:, 2:3],
                                                     scalar1=1e-6, scalar2=None,
                                                     op0=ALU.add), inc=s_gs)
            p.wait("vector", s_gs, s_gs.n)
            p.op("vector", lambda e: e.reciprocal(stat[:, 3:4], stat[:, 2:3]), inc=s_gs)
            p.wait("vector", s_gs, s_gs.n)
            p.op("vector", lambda e: e.tensor_scalar(out=w_tm[:], in0=sc_tm[:],
                                                     scalar1=stat[:, 1:2], scalar2=None,
                                                     op0=ALU.is_ge), inc=s_gs)
            p.wait("vector", s_gs, s_gs.n)
            p.op("vector", lambda e: e.tensor_tensor(w_tm[:], w_tm[:], sc_tm[:],
                                                     op=ALU.mult), inc=s_gs)
            p.wait("vector", s_gs, s_gs.n)
            p.op("vector", lambda e: e.tensor_scalar(out=w_tm[:], in0=w_tm[:],
                                                     scalar1=stat[:, 3:4], scalar2=None,
                                                     op0=ALU.mult), inc=s_gs)
            p.wait("vector", s_gs, s_gs.n)
            if "dbg_w8" in dbg:
                p.wait("sync", s_gs, s_gs.n)
                p.op("sync", lambda e, c=c: e.dma_start(
                    out=dbg["dbg_w8"][c * 128:(c + 1) * 128, :], in_=w_tm[:]),
                    inc=s_mo, inc_k=16)
            p.wait("tensor", s_gs, s_gs.n)
            p.op("tensor", lambda e: e.transpose(PI[0:E, 0:128], w_tm[:], ident_f[:]),
                 inc=s_gt)
            p.wait("vector", s_gt, s_gt.n)
            p.op("vector", lambda e, ci=ci: e.tensor_copy(
                wT_all[0:E, ci * 128:(ci + 1) * 128], PI[0:E, 0:128]), inc=s_gs)
            done_gs[ci] = s_gs.n
            prev_gs = s_gs.n
        gates_done = done_gs[STC - 1]

        p.wait("vector", s_eva, s_eva.n)
        p.op("vector", lambda e: e.memset(comb[:], 0.0), inc=s_cz)
        p.wait("gpsimd", s_cz, s_cz.n)
        cz_done = s_cz.n

        for ei in range(NE + 1):
            shared = ei == NE
            egi = ei + st * (NE + 1)
            for g0 in range(2):
                gi = egi * 8 + g0
                if gi >= 2:
                    p.wait("sync", s_mmu, up_cons[gi - 2])
                src = wbf_su[0] if shared else wbf_up[ei]
                p.op("sync", lambda e, g0=g0, src=src, gi=gi: e.dma_start(
                    out=ubuf[gi % 2][:].rearrange("p (c n) -> p c n", c=8),
                    in_=src.rearrange("(c p n) -> p c n", p=128, n=II)
                    [:, :, g0 * 512:(g0 + 1) * 512]), inc=s_wup[gi % 2], inc_k=16)
                up_ld[gi] = s_wup[gi % 2].n
            for d0 in range(2):
                di = egi * 8 + d0
                if di >= 2:
                    p.wait("scalar", s_mmd, dn_cons[di - 2])
                srcd = wbf_sd[0] if shared else wbf_dn[ei]
                p.op("scalar", lambda e, d0=d0, srcd=srcd, di=di: e.dma_start(
                    out=dbuf[di % 2][:], in_=srcd[d0].rearrange("(q m) -> q m", q=128)),
                    inc=s_wdn[di % 2], inc_k=16)
                dn_ld[di] = s_wdn[di % 2].n

            nt512 = STOK // 512
            for grp in range(8):
                gi = egi * 8 + grp
                p.wait("tensor", s_wup[gi % 2], up_ld[gi])
                if ei == 0 and grp == 0:
                    p.wait("tensor", s_gs, gates_done)
                for it4 in range(4):
                    it = grp * 4 + it4
                    for t5 in range(nt512):
                        if nup >= 2:
                            p.wait("tensor", s_gel, nup - 1)
                        bank = PB[nup % 2]
                        for cc in range(8):
                            p.op("tensor", lambda e, it4=it4, cc=cc, gi=gi, t5=t5,
                                 bank=bank: e.matmul(
                                     bank[:, 0:512],
                                     ubuf[gi % 2][:, cc * 512 + it4 * 128:
                                                  cc * 512 + it4 * 128 + 128],
                                     xnT_st[:, cc * STOK + t5 * 512:
                                            cc * STOK + (t5 + 1) * 512],
                                     start=(cc == 0), stop=(cc == 7)),
                                 inc=(s_mmu if cc == 7 else None))
                        p.wait("scalar", s_mmu, s_mmu.n)
                        if it == 0 and t5 == 0 and grp == 0:
                            p.wait("scalar", s_mmd, s_mmd.n)
                        bcol = (0 if shared else ei * 32) + it
                        bsrc = sbu_sb if shared else upb_sb
                        p.op("scalar", lambda e, it=it, t5=t5, bank=bank, bcol=bcol,
                             bsrc=bsrc: e.activation(
                                 hact[:, it * STOK + t5 * 512:it * STOK + (t5 + 1) * 512],
                                 bank[:, 0:512], AF.Gelu,
                                 bias=bsrc[:, bcol:bcol + 1], scale=1.0), inc=s_gel)
                        nup += 1
                up_cons[gi] = s_mmu.n
                if grp < 6:
                    gi2 = egi * 8 + grp + 2
                    p.wait("sync", s_mmu, up_cons[gi])
                    src = wbf_su[0] if shared else wbf_up[ei]
                    p.op("sync", lambda e, grp=grp, src=src, gi2=gi2: e.dma_start(
                        out=ubuf[gi2 % 2][:].rearrange("p (c n) -> p c n", c=8),
                        in_=src.rearrange("(c p n) -> p c n", p=128, n=II)
                        [:, :, (grp + 2) * 512:(grp + 3) * 512]), inc=s_wup[gi2 % 2],
                        inc_k=16)
                    up_ld[gi2] = s_wup[gi2 % 2].n

            p.wait("tensor", s_gel, s_gel.n)
            for dt in range(8):
                di = egi * 8 + dt
                p.wait("tensor", s_wdn[di % 2], dn_ld[di])
                for t5 in range(nt512):
                    if not shared:
                        p.wait("gpsimd", s_gs, gates_done)
                        if not (st == 0 and ei == 0 and dt == 0 and t5 == 0):
                            p.wait("gpsimd", s_wb2, wb2_cons)
                        p.op("gpsimd", lambda e, ei=ei, t5=t5: e.dma_start(
                            out=wrow[0:1, 0:512],
                            in_=wT_all[ei:ei + 1, t5 * 512:(t5 + 1) * 512]),
                            inc=s_wrow, inc_k=16)
                        p.wait("tensor", s_wrow, s_wrow.n)
                        p.op("tensor", lambda e: e.matmul(
                            PI[:, 0:512], ones_row[:], wrow[0:1, 0:512],
                            start=True, stop=True), inc=s_wb2)
                        p.wait("vector", s_wb2, s_wb2.n)
                        p.op("vector", lambda e: e.tensor_copy(wbc[:], PI[:, 0:512]),
                             inc=s_wb2)
                        wb2_cons = s_wb2.n
                    if ndn >= 2:
                        p.wait("tensor", s_evm, ndn - 1)
                    bank = PB[2 + ndn % 2]
                    for cc in range(32):
                        p.op("tensor", lambda e, cc=cc, di=di, t5=t5, bank=bank:
                             e.matmul(bank[:, 0:512],
                                      dbuf[di % 2][:, cc * 128:(cc + 1) * 128],
                                      hact[:, cc * STOK + t5 * 512:
                                           cc * STOK + (t5 + 1) * 512],
                                      start=(cc == 0), stop=(cc == 31)),
                             inc=(s_mmd if cc == 31 else None))
                    p.wait("vector", s_mmd, s_mmd.n)
                    if not shared:
                        p.wait("vector", s_wb2, wb2_cons)
                    p.wait("vector", s_eva, s_eva.n)
                    if shared:
                        p.op("vector", lambda e, dt=dt, bank=bank: e.tensor_scalar(
                            out=tmp_ev[:], in0=bank[:, 0:512],
                            scalar1=sbd_sb[:, dt:dt + 1], scalar2=None, op0=ALU.add),
                            inc=s_evm)
                    else:
                        p.op("vector", lambda e, dt=dt, bank=bank, ei=ei:
                             e.scalar_tensor_tensor(
                                 out=tmp_ev[:], in0=bank[:, 0:512],
                                 scalar=dnb_sb[:, ei * 8 + dt:ei * 8 + dt + 1],
                                 in1=wbc[:], op0=ALU.add, op1=ALU.mult), inc=s_evm)
                    p.wait("gpsimd", s_evm, s_evm.n)
                    if st > 0 or ei > 0 or dt > 0 or t5 > 0:
                        pass
                    if ei == 0 and dt == 0 and t5 == 0:
                        p.wait("gpsimd", s_cz, cz_done)
                    p.op("gpsimd", lambda e, dt=dt, t5=t5: e.tensor_tensor(
                        comb[:, dt * STOK + t5 * 512:dt * STOK + (t5 + 1) * 512],
                        comb[:, dt * STOK + t5 * 512:dt * STOK + (t5 + 1) * 512],
                        tmp_ev[:], op=ALU.add), inc=s_eva)
                    ndn += 1
                dn_cons[di] = s_mmd.n
                if dt < 6:
                    di2 = egi * 8 + dt + 2
                    p.wait("scalar", s_mmd, dn_cons[di])
                    srcd = wbf_sd[0] if shared else wbf_dn[ei]
                    p.op("scalar", lambda e, dt=dt, srcd=srcd, di2=di2: e.dma_start(
                        out=dbuf[di2 % 2][:],
                        in_=srcd[dt + 2].rearrange("(q m) -> q m", q=128)),
                        inc=s_wdn[di2 % 2], inc_k=16)
                    dn_ld[di2] = s_wdn[di2 % 2].n

        p.wait("scalar", s_eva, s_eva.n)
        p.op("scalar", lambda e: e.activation(mx8[:], comb[:, 0:STOK], AF.Abs), inc=s_n2)
        for g in range(1, 8):
            p.wait("scalar", s_n2, s_n2.n)
            p.op("scalar", lambda e, g=g: e.activation(
                graw[:], comb[:, g * STOK:(g + 1) * STOK], AF.Abs), inc=s_n2)
            p.wait("vector", s_n2, s_n2.n)
            p.op("vector", lambda e: e.tensor_tensor(mx8[:], mx8[:], graw[:],
                                                     op=ALU.max), inc=s_n2)
        p.wait("gpsimd", s_n2, s_n2.n)
        p.op("gpsimd", lambda e: e.dma_start(out=graw[0:64, :], in_=mx8[64:128, :]),
             inc=s_n2g, inc_k=16)
        p.wait("vector", s_n2g, s_n2g.n)
        p.op("vector", lambda e: e.tensor_tensor(mx8[0:64, :], mx8[0:64, :],
                                                 graw[0:64, :], op=ALU.max),
             inc=s_n2)
        p.wait("gpsimd", s_n2, s_n2.n)
        p.op("gpsimd", lambda e: e.dma_start(out=graw[0:32, :], in_=mx8[32:64, :]),
             inc=s_n2g, inc_k=16)
        p.wait("vector", s_n2g, s_n2g.n)
        p.wait("vector", s_n2, s_n2.n)
        p.op("vector", lambda e: e.tensor_tensor(mx8[0:32, :], mx8[0:32, :],
                                                 graw[0:32, :], op=ALU.max), inc=s_n2)
        for ci in range(STC):
            p.wait("tensor", s_n2, s_n2.n)
            p.op("tensor", lambda e, ci=ci: e.transpose(
                PI[:, 0:32], mx8[0:32, ci * 128:(ci + 1) * 128],
                ident_f[0:32, 0:32]), inc=s_tz)
            p.wait("vector", s_tz, s_tz.n)
            p.op("vector", lambda e, ci=ci: e.reduce_max(
                stat[:, 0:1], PI[:, 0:32], axis=AX.X), inc=s_n2)
            p.wait("vector", s_n2, s_n2.n)
            p.op("vector", lambda e: e.tensor_scalar(out=stat[:, 0:1], in0=stat[:, 0:1],
                                                     scalar1=1e-5, scalar2=None,
                                                     op0=ALU.add), inc=s_n2)
            p.wait("vector", s_n2, s_n2.n)
            p.op("vector", lambda e, ci=ci: e.reciprocal(rtm[:, ci:ci + 1],
                                                         stat[:, 0:1]), inc=s_n2)
            p.wait("vector", s_n2, s_n2.n)
            p.wait("tensor", s_n2, s_n2.n)
        p.op("tensor", lambda e: e.transpose(PI[0:STC, 0:128], rtm[:], ident_f[:]),
             inc=s_tz)
        p.wait("vector", s_tz, s_tz.n)
        p.op("vector", lambda e: e.tensor_copy(rT_sb[0:STC, :], PI[0:STC, 0:128]),
             inc=s_n2)
        p.wait("gpsimd", s_n2, s_n2.n)
        for ci in range(STC):
            p.op("gpsimd", lambda e, ci=ci: e.dma_start(
                out=wrow[0:1, ci * 128:(ci + 1) * 128], in_=rT_sb[ci:ci + 1, :]),
                inc=s_n2g, inc_k=16)
        rrow_done = s_n2g.n
        p.wait("tensor", s_n2g, rrow_done)
        for g5 in range(STOK // 512):
            p.op("tensor", lambda e, g5=g5: e.matmul(
                PI[:, 0:512], ones_row[:], wrow[0:1, g5 * 512:(g5 + 1) * 512],
                start=True, stop=True), inc=s_tz)
            p.wait("vector", s_tz, s_tz.n)
            p.op("vector", lambda e, g5=g5: e.tensor_copy(
                rbc[:, g5 * 512:(g5 + 1) * 512], PI[:, 0:512]), inc=s_n2)
            p.wait("tensor", s_n2, s_n2.n)
        for g in range(8):
            p.wait("vector", s_n2, s_n2.n)
            p.op("vector", lambda e, g=g: e.tensor_tensor(
                graw[:], comb[:, g * STOK:(g + 1) * STOK], rbc[:], op=ALU.mult),
                inc=s_n2)
            p.wait("scalar", s_n2, s_n2.n)
            p.op("scalar", lambda e: e.activation(mx8[:], graw[:], AF.Tanh, scale=0.5),
                 inc=s_n2)
            p.wait("vector", s_n2, s_n2.n)
            p.op("vector", lambda e, g=g: e.scalar_tensor_tensor(
                out=zg[:, g * STOK:(g + 1) * STOK], in0=mx8[:], scalar=1.0, in1=graw[:],
                op0=ALU.add, op1=ALU.mult), inc=s_zg)
            p.wait("vector", s_zg, s_zg.n)
        zg_done = s_zg.n
        for ci in range(STC):
            c = st_c0 + ci
            pe = ci % 2
            p.wait("tensor", s_zg, zg_done)
            if not (st == 0 and ci == 0):
                p.wait("tensor", s_zg, zgt_cons)
            for g in range(8):
                p.op("tensor", lambda e, g=g, ci=ci: e.transpose(
                    PT[:, g * 128:(g + 1) * 128],
                    zg[:, g * STOK + ci * 128:g * STOK + (ci + 1) * 128], ident_b[:]),
                    inc=(s_tz if g == 7 else None))
            p.wait("vector", s_tz, s_tz.n)
            p.op("vector", lambda e: e.tensor_copy(zgT[:], PT[:]), inc=s_zg)
            zgt_cons = s_zg.n
            if ci == 0:
                p.wait("sync", s_ln2, s_ln2.n)
            if not (st == 0 and ci == 0):
                p.wait("sync", s_zg, szg_done)
            if not (st == 0 and ci <= 1):
                key = (st, ci - 2) if ci >= 2 else (st - 1, STC + ci - 2)
                p.wait("sync", s_out, out_cons.get(key, 0))
            oc = st * STC + ci
            p.wait("gpsimd", s_sw, sw_all)
            if not (st == 0 and ci == 0):
                p.wait("gpsimd", s_zg, szg_done)
            p.op("gpsimd", lambda e, oc=oc: e.dma_start(
                out=idxw_sb[:],
                in_=idxw[0:1, oc * 128:(oc + 1) * 128].rearrange("r (p n) -> (r p) n", n=1)),
                inc=s_epg, inc_k=16)
            p.wait("gpsimd", s_epg, s_epg.n)
            p.op("gpsimd", lambda e: e.indirect_dma_start(
                out=sch[0][:], out_offset=None, in_=s_scr,
                in_offset=IndirectOffsetOnAxis(ap=idxw_sb[:], axis=0)),
                inc=s_epg, inc_k=16)
            p.op("sync", lambda e, c=c, pe=pe: e.dma_start(
                out=xo_ch[pe][:], in_=x_own[c * 128:(c + 1) * 128, :]),
                inc=s_ep, inc_k=16)
            ep_ld = s_ep.n
            p.wait("vector", s_ep, ep_ld)
            p.wait("vector", s_out, s_out.n)
            p.wait("vector", s_epg, s_epg.n)
            p.wait("vector", s_zg, zgt_cons)
            if dbg:
                p.wait("vector", s_mo, s_mo.n)
            p.op("vector", lambda e, pe=pe: e.tensor_tensor(
                sq_mo[:], sch[0][:], zgT[:], op=ALU.mult), inc=s_zg)
            szg_done = s_zg.n
            p.wait("vector", s_zg, s_zg.n)
            p.op("vector", lambda e, pe=pe: e.scalar_tensor_tensor(
                out=xn_mo[:], in0=sq_mo[:], scalar=0.5, in1=xo_ch[pe][:],
                op0=ALU.mult, op1=ALU.add), inc=s_zg)
            och_done = s_zg.n
            prev_epi = s_zg.n
            p.wait("sync", s_zg, och_done)
            p.op("sync", lambda e, c=c, pe=pe: e.dma_start(
                out=out_d[c * 128:(c + 1) * 128, :], in_=xn_mo[:]),
                inc=s_out, inc_k=16)
            out_cons[(st, ci)] = s_out.n
            if "dbg_zg" in dbg:
                p.op("sync", lambda e, c=c: e.dma_start(
                    out=dbg["dbg_zg"][c * 128:(c + 1) * 128, :], in_=zgT[:]
                    if False else sq_mo[:]), inc=s_mo, inc_k=16)

    p.wait("sync", s_out, s_out.n)
    if dbg:
        p.wait("sync", s_mo, s_mo.n)
        p.wait("sync", s_dbg, s_dbg.n)
        p.wait("gpsimd", s_dbg2, s_dbg2.n)

    with nc.Block() as blk:
        blk.sync(lambda e: [f(e) for f in p.q["sync"]])
        blk.tensor(lambda e: [f(e) for f in p.q["tensor"]])
        blk.vector(lambda e: [f(e) for f in p.q["vector"]])
        blk.scalar(lambda e: [f(e) for f in p.q["scalar"]])
        blk.gpsimd(lambda e: [f(e) for f in p.q["gpsimd"]])
    oes.close()
    es.close()
    return nc


_CACHE = {}


def _get_nc(cfg_key):
    if cfg_key not in _CACHE:
        cfg = Cfg()
        _CACHE[cfg_key] = build(cfg)
    return _CACHE[cfg_key]


def kernel(L0_attn_x, L0_attn_ln_w, L0_attn_ln_b, L0_attn_Wq, L0_attn_Wk, L0_attn_Wv,
           L0_attn_Wb, L0_attn_o_norm_w, L0_attn_Wo, L0_ffn_gate_w, L0_ffn_bias,
           L0_ffn_up_w, L0_ffn_up_b, L0_ffn_down_w, L0_ffn_down_b, L0_ffn_sw_up,
           L0_ffn_sb_up, L0_ffn_sw_down, L0_ffn_sb_down):
    x = np.ascontiguousarray(L0_attn_x, dtype=np.float32)
    nc = _get_nc("full")

    upb_t = np.ascontiguousarray(
        L0_ffn_up_b.reshape(E, 32, 128).transpose(2, 0, 1).reshape(128, E * 32))
    dnb_t = np.ascontiguousarray(
        L0_ffn_down_b.reshape(E, 8, 128).transpose(2, 0, 1).reshape(128, E * 8))
    sbu_t = np.ascontiguousarray(L0_ffn_sb_up.reshape(32, 128).T)
    sbd_t = np.ascontiguousarray(L0_ffn_sb_down.reshape(8, 128).T)

    common = {
        "ln_w": L0_attn_ln_w.reshape(1, D), "ln_b": L0_attn_ln_b.reshape(1, D),
        "Wq": L0_attn_Wq, "Wk": L0_attn_Wk, "Wv": L0_attn_Wv, "Wb": L0_attn_Wb,
        "onw": L0_attn_o_norm_w.reshape(1, DV), "Wo": L0_attn_Wo,
        "gate_w": L0_ffn_gate_w, "gate_b": L0_ffn_bias.reshape(1, E),
        "upw": L0_ffn_up_w, "upb_t": upb_t, "dnw": L0_ffn_down_w, "dnb_t": dnb_t,
        "sw_up": L0_ffn_sw_up, "sb_up_t": sbu_t, "sw_dn": L0_ffn_sw_down,
        "sb_dn_t": sbd_t,
    }
    in_maps = []
    for core in range(8):
        b = core // 2
        half = core % 2
        m = dict(common)
        m["x_full"] = x[b]
        m["x_own"] = x[b, half * 4096:(half + 1) * 4096]
        idx = np.arange(4096, dtype=np.int32) + 4096 * half
        m["idxw"] = idx.reshape(1, 4096)
        in_maps.append(m)

    res = run_bass_kernel_spmd(nc, in_maps, core_ids=list(range(8)))
    out = np.empty((B, T, D), np.float32)
    for core in range(8):
        b = core // 2
        half = core % 2
        out[b, half * 4096:(half + 1) * 4096] = res.results[core]["out"]
    return out


B = 4
T = 8192


# version 04
# speedup vs baseline: 1.1744x; 1.1744x over previous
import numpy as np
import ml_dtypes
from contextlib import ExitStack

import concourse.bass as bass
import concourse.mybir as mybir
from concourse.bass_utils import run_bass_kernel_spmd
from concourse.bass_core_rust import IndirectOffsetOnAxis

F32 = mybir.dt.float32
BF16 = mybir.dt.bfloat16
I32 = mybir.dt.int32
AF = mybir.ActivationFunctionType
ALU = mybir.AluOpType
AX = mybir.AxisListType

D = 1024
H = 4
DK = 256
DV = 512
E = 8
II = 4096
C = 128


class Cfg:
    n_chunks = 64
    own_chunks = 32
    st_chunks = 8
    n_exp = 8
    newton = 3
    debug = ()


ENGINES = ("sync", "tensor", "vector", "scalar", "gpsimd")


class Sem:
    __slots__ = ("h", "n", "name")

    def __init__(self, h, name):
        self.h = h
        self.n = 0
        self.name = name


class P:
    def __init__(self, nc, es):
        self.nc = nc
        self.es = es
        self.q = {e: [] for e in ENGINES}
        self._sems = {}

    def sem(self, name):
        if name not in self._sems:
            self._sems[name] = Sem(self.es.enter_context(self.nc.semaphore(name)), name)
        return self._sems[name]

    def wait(self, eng, sem, val=None):
        v = sem.n if val is None else val
        if v > 0:
            self.q[eng].append(lambda e, s=sem.h, v=v: e.wait_ge(s, v))

    def op(self, eng, fn, inc=None, inc_k=1):
        if inc is not None:
            inc.n += inc_k

            def run(e, fn=fn, s=inc.h, k=inc_k):
                fn(e).then_inc(s, k)
        else:
            def run(e, fn=fn):
                fn(e)
        self.q[eng].append(run)


def _rsqrt(p, buf, col0, ncol, scr, iters):
    sn = p.sem("snl")
    x = buf[:, col0:col0 + ncol]
    y = scr[:, 0:ncol]
    t = scr[:, ncol:2 * ncol]
    p.op("vector", lambda e: e.tensor_scalar(out=scr[:, 0:ncol].bitcast(I32),
                                             in0=x.bitcast(I32), scalar1=1, scalar2=None,
                                             op0=ALU.logical_shift_right), inc=sn)
    p.wait("vector", sn)
    p.op("vector", lambda e: e.tensor_scalar(out=scr[:, 2 * ncol:3 * ncol].bitcast(I32),
                                             in0=scr[:, 0:ncol].bitcast(I32),
                                             scalar1=0x5F3759DF, scalar2=-1,
                                             op0=ALU.subtract, op1=ALU.mult), inc=sn)
    y = scr[:, 2 * ncol:3 * ncol]
    for _ in range(iters):
        p.wait("vector", sn)
        p.op("vector", lambda e: e.tensor_tensor(t, y, y, op=ALU.mult), inc=sn)
        p.wait("vector", sn)
        p.op("vector", lambda e: e.scalar_tensor_tensor(out=t, in0=x, scalar=-0.5, in1=t,
                                                        op0=ALU.mult, op1=ALU.mult), inc=sn)
        p.wait("vector", sn)
        p.op("vector", lambda e: e.tensor_scalar(out=t, in0=t, scalar1=1.5, scalar2=None,
                                                 op0=ALU.add), inc=sn)
        p.wait("vector", sn)
        p.op("vector", lambda e: e.tensor_tensor(y, y, t, op=ALU.mult), inc=sn)
    p.wait("vector", sn)
    p.op("vector", lambda e: e.tensor_copy(x, y), inc=sn)
    p.wait("vector", sn)


def build(cfg):
    nc = bass.Bass("TRN2", target_bir_lowering=False)
    NCH = cfg.n_chunks
    OCH = cfg.own_chunks
    STC = cfg.st_chunks
    NE = cfg.n_exp
    NST = OCH // STC
    STOK = STC * C
    TOKB = NCH * C
    TOKO = OCH * C

    def din(name, shape, dtype=F32):
        return nc.dram_tensor(name, list(shape), dtype, kind="ExternalInput").ap()

    x_full = din("x_full", (TOKB, D))
    x_own = din("x_own", (TOKO, D))
    ln_w = din("ln_w", (1, D))
    ln_b = din("ln_b", (1, D))
    Wq = din("Wq", (D, H * DK))
    Wk = din("Wk", (D, H * DK))
    Wv = din("Wv", (D, H * DV))
    Wb = din("Wb", (D, H))
    onw = din("onw", (1, DV))
    Wo = din("Wo", (H * DV, D))
    gate_w = din("gate_w", (D, E))
    gate_b = din("gate_b", (1, E))
    upw = din("upw", (NE, D, II))
    upb_t = din("upb_t", (128, NE * 32))
    dnw = din("dnw", (NE, II, D))
    dnb_t = din("dnb_t", (128, NE * 8))
    sw_up = din("sw_up", (D, II))
    sb_up_t = din("sb_up_t", (128, 32))
    sw_dn = din("sw_dn", (II, D))
    sb_dn_t = din("sb_dn_t", (128, 8))
    idxw = din("idxw", (1, OCH * 128), I32)

    out_d = nc.dram_tensor("out", [TOKO, D], F32, kind="ExternalOutput").ap()
    s_scr = nc.dram_tensor("s_scr", [NCH * 128, D], F32, kind="Internal").ap()
    xn_scr = nc.dram_tensor("xn_scr", [NCH * 128, D], F32, kind="Internal").ap()
    on_scr = nc.dram_tensor("on_scr", [NCH * 128, H * DV], BF16, kind="Internal").ap()
    wbf_up = nc.dram_tensor("wbf_up", [NE, D * II], BF16, kind="Internal").ap()
    wbf_dn = nc.dram_tensor("wbf_dn", [NE, 8, 128 * 32 * 128], BF16, kind="Internal").ap()
    wbf_su = nc.dram_tensor("wbf_su", [1, D * II], BF16, kind="Internal").ap()
    wbf_sd = nc.dram_tensor("wbf_sd", [1, 8, 128 * 32 * 128], BF16, kind="Internal").ap()

    dbg = {}
    for name in cfg.debug:
        shp = {"dbg_s": (TOKB, D), "dbg_xn": (TOKB, D), "dbg_nrm": (TOKB, D),
               "dbg_q": (TOKB, H * DK), "dbg_k": (TOKB, H * DK),
               "dbg_v": (TOKB, H * DV), "dbg_beta": (TOKB, H),
               "dbg_o": (TOKB, H * DV), "dbg_scores": (TOKO, E),
               "dbg_w8": (TOKO, E), "dbg_zg": (TOKO, D)}[name]
        dbg[name] = nc.dram_tensor(name, list(shp), F32, kind="ExternalOutput").ap()

    es = ExitStack()
    p = P(nc, es)
    sems = p.sem

    def sb(name, shape, dtype):
        return es.enter_context(nc.sbuf_tensor(name, list(shape), dtype))

    def psum(name, shape, dtype=F32):
        return es.enter_context(nc.psum_tensor(name, list(shape), dtype))

    ident_b = sb("ident_b", (128, 128), BF16)
    ident_f = sb("ident_f", (128, 128), F32)
    tril01 = sb("tril01", (128, 128), F32)
    triu01 = sb("triu01", (128, 128), F32)
    ones_row = sb("ones_row", (1, 128), F32)
    nsc = sb("nsc", (128, 32), F32)
    stat = sb("stat", (128, 24), F32)
    idxw_sb = sb("idxw_sb", (128, 1), I32)

    PB = [psum(f"pb{i}", (128, 512), F32) for i in range(4)]
    PT = psum("pt", (128, 1024), BF16)
    PI = psum("pi", (128, 1536), F32)

    s_init = sems("s_init")
    p.op("gpsimd", lambda e: e.memset(ident_b[:], 0.0), inc=s_init)
    p.wait("gpsimd", s_init)
    p.op("gpsimd", lambda e: e.affine_select(out=ident_b[:], in_=ident_b[:],
                                             compare_op=ALU.not_equal, fill=1.0, base=0,
                                             pattern=[[-1, 128]], channel_multiplier=1),
         inc=s_init)
    p.op("gpsimd", lambda e: e.memset(ident_f[:], 0.0), inc=s_init)
    p.wait("gpsimd", s_init)
    p.op("gpsimd", lambda e: e.affine_select(out=ident_f[:], in_=ident_f[:],
                                             compare_op=ALU.not_equal, fill=1.0, base=0,
                                             pattern=[[-1, 128]], channel_multiplier=1),
         inc=s_init)
    p.op("gpsimd", lambda e: e.memset(tril01[:], 0.0), inc=s_init)
    p.wait("gpsimd", s_init)
    p.op("gpsimd", lambda e: e.affine_select(out=tril01[:], in_=tril01[:],
                                             compare_op=ALU.is_ge, fill=1.0, base=0,
                                             pattern=[[1, 128]], channel_multiplier=-1),
         inc=s_init)
    p.op("gpsimd", lambda e: e.memset(triu01[:], 1.0), inc=s_init)
    p.wait("gpsimd", s_init)
    p.op("gpsimd", lambda e: e.affine_select(out=triu01[:], in_=triu01[:],
                                             compare_op=ALU.is_ge, fill=0.0, base=0,
                                             pattern=[[1, 128]], channel_multiplier=-1),
         inc=s_init)
    p.op("gpsimd", lambda e: e.memset(ones_row[:], 1.0), inc=s_init)

    mes = ExitStack()

    def msb(name, shape, dtype):
        return mes.enter_context(nc.sbuf_tensor(name, list(shape), dtype))

    wq_sb = msb("wq_sb", (128, 8 * 1024), BF16)
    wk_sb = msb("wk_sb", (128, 8 * 1024), BF16)
    wv_sb = msb("wv_sb", (128, 8 * 2048), BF16)
    wb_sb = msb("wb_sb", (128, 8 * H), BF16)
    lnw_bc = msb("lnw_bc", (128, D), F32)
    lnb_bc = msb("lnb_bc", (128, D), F32)
    onw_bc = msb("onw_bc", (128, DV), F32)
    lnrow = msb("lnrow", (1, 2 * D + DV), F32)
    S_b = [msb(f"S_b{h}", (128, 2 * DV), BF16) for h in range(H)]
    xch = [msb(f"xch{i}", (128, D), F32) for i in range(2)]
    xnch = [msb(f"xnch{i}", (128, D), F32) for i in range(2)]
    sqch = msb("sqch", (128, D), F32)
    nrm0 = msb("nrm0", (128, D), F32)
    osq = msb("osq", (128, 2 * DV), F32)
    nrmb = [msb(f"nrmb{i}", (128, D), BF16) for i in range(2)]
    nT = [msb(f"nT{i}", (128, 8 * 128), BF16) for i in range(2)]
    qraw = [msb("qraw0", (128, H * DK), F32)]
    kraw = [msb("kraw0", (128, H * DK), F32)]
    qn = [msb(f"qn{i}", (128, H * DK), BF16) for i in range(2)]
    kn = [msb(f"kn{i}", (128, H * DK), BF16) for i in range(2)]
    vsil = [msb(f"vsil{i}", (128, H * DV), F32) for i in range(2)]
    beta_t = [msb(f"beta_t{i}", (128, 2 * H), F32) for i in range(2)]
    qT = [msb(f"qT{i}", (128, H * 2 * 128), BF16) for i in range(2)]
    kT = [msb(f"kT{i}", (128, H * 2 * 128), BF16) for i in range(2)]
    vd_b = [msb("vd_b0", (128, H * DV), BF16)]
    u_b = msb("u_b", (128, H * DV), BF16)
    gm_b = msb("gm_b", (128, H * 128), BF16)
    m_f = [msb(f"m_f{i}", (128, H * 128), F32) for i in range(2)]
    mt_f = [msb(f"mt_f{i}", (128, H * 128), F32) for i in range(2)]
    x2_f = [msb(f"x2_f{i}", (128, H * 128), F32) for i in range(2)]
    rt_f = [msb(f"rt_f{i}", (128, H * 128), F32) for i in range(2)]
    rt_b = [msb(f"rt_b{i}", (128, H * 128), BF16) for i in range(2)]
    o_nw = [msb(f"o_nw{i}", (128, H * DV), BF16) for i in range(2)]

    s_x = sems("s_x")
    s_wm = sems("s_wm")
    s_ld = sems("s_ld")
    s_bc = sems("s_bc")
    s_cast = sems("s_cast")
    s_stat = sems("s_stat")
    s_nrm = sems("s_nrm")
    s_trn = sems("s_trn")
    s_trnv = sems("s_trnv")
    s_pqk = sems("s_pqk")
    s_sqk = sems("s_sqk")
    s_l2 = sems("s_l2")
    s_tqk = sems("s_tqk")
    s_eqk = sems("s_eqk")
    s_pv = sems("s_pv")
    s_sv = sems("s_sv")
    s_bet = sems("s_bet")
    s_kv = sems("s_kv")
    s_vd = sems("s_vd")
    s_A = sems("s_A")
    s_M = sems("s_M")
    s_Mt = sems("s_Mt")
    s_it = sems("s_it")
    s_iv = sems("s_iv")
    s_U = sems("s_U")
    s_Ue = sems("s_Ue")
    s_G = sems("s_G")
    s_Gm = sems("s_Gm")
    s_O = sems("s_O")
    s_oe = sems("s_oe")
    s_ow = [sems("s_ow0"), sems("s_ow1")]
    s_xw = [sems("s_xw0"), sems("s_xw1")]
    s_to = sems("s_to")
    s_toe = sems("s_toe")
    s_wo = sems("s_wo")
    s_s = sems("s_s")
    s_sw = [sems("s_sw0"), sems("s_sw1")]
    s_su = sems("s_su")
    s_dbg = sems("s_dbg")
    s_dbg2 = sems("s_dbg2")

    done = {k: {} for k in ("nrm", "trnv", "l2", "eqk", "su", "oe", "sv", "bet",
                            "sqk", "kl2", "ktr", "A", "Mt", "vd", "ow", "xw",
                            "G", "O", "stat")}

    def cast_load(dst, src, cdim):
        view = src.rearrange("(c p) n -> p c n", p=128)
        p.op("gpsimd", lambda e: e.dma_start(
            out=dst[:].rearrange("p (c n) -> p c n", c=cdim), in_=view),
            inc=s_wm, inc_k=16)

    cast_load(wq_sb, Wq, 8)
    cast_load(wk_sb, Wk, 8)
    cast_load(wv_sb, Wv, 8)
    cast_load(wb_sb, Wb, 8)
    wm_done = s_wm.n

    p.op("sync", lambda e: e.dma_start(out=lnrow[0:1, 0:D], in_=ln_w), inc=s_ld, inc_k=16)
    p.op("sync", lambda e: e.dma_start(out=lnrow[0:1, D:2 * D], in_=ln_b),
         inc=s_ld, inc_k=16)
    p.op("sync", lambda e: e.dma_start(out=lnrow[0:1, 2 * D:2 * D + DV], in_=onw),
         inc=s_ld, inc_k=16)

    p.wait("tensor", s_ld, s_ld.n)
    p.wait("tensor", s_init)
    for dst, src0, ng in ((lnw_bc, 0, 2), (lnb_bc, D, 2), (onw_bc, 2 * D, 1)):
        for g in range(ng):
            p.op("tensor", lambda e, g=g, src0=src0: e.matmul(
                PB[0][:, 0:512], ones_row[:],
                lnrow[0:1, src0 + g * 512:src0 + (g + 1) * 512],
                start=True, stop=True), inc=s_bc)
            p.wait("vector", s_bc)
            p.op("vector", lambda e, g=g, dst=dst: e.tensor_copy(
                dst[:, g * 512:(g + 1) * 512], PB[0][:, 0:512]), inc=s_bc)
            p.wait("tensor", s_bc)
    bc_done = s_bc.n

    s_z = sems("s_z")
    for h in range(H):
        p.op("vector", lambda e, h=h: e.memset(S_b[h][:], 0.0),
             inc=(s_z if h == H - 1 else None))

    for ei in range(NE):
        p.op("gpsimd", lambda e, ei=ei: e.dma_start(
            out=wbf_up[ei].rearrange("(a b) -> a b", a=II),
            in_=upw[ei].rearrange("d i -> (d i)").rearrange("(a b) -> a b", a=II)),
            inc=s_cast, inc_k=16)
    p.op("gpsimd", lambda e: e.dma_start(
        out=wbf_su[0].rearrange("(a b) -> a b", a=II),
        in_=sw_up.rearrange("d i -> (d i)").rearrange("(a b) -> a b", a=II)),
        inc=s_cast, inc_k=16)
    for ei in range(NE):
        for dt in range(8):
            p.op("gpsimd", lambda e, ei=ei, dt=dt: e.dma_start(
                out=wbf_dn[ei][dt].rearrange("(q c j) -> q c j", q=128, c=32),
                in_=dnw[ei].rearrange("(c q) (dt j) -> dt q c j", q=128, j=128)[dt]),
                inc=s_cast, inc_k=16)
    for dt in range(8):
        p.op("gpsimd", lambda e, dt=dt: e.dma_start(
            out=wbf_sd[0][dt].rearrange("(q c j) -> q c j", q=128, c=32),
            in_=sw_dn.rearrange("(c q) (dt j) -> dt q c j", q=128, j=128)[dt]),
            inc=s_cast, inc_k=16)
    cast_done = s_cast.n

    def front(c):
        pe = c % 2
        if c >= 2:
            p.wait("sync", s_nrm, done["nrm"][c - 2])
        p.op("sync", lambda e, c=c, pe=pe: e.dma_start(
            out=xch[pe][:], in_=x_full[c * 128:(c + 1) * 128, :]), inc=s_x, inc_k=16)
        xld = s_x.n

        p.wait("scalar", s_x, xld)
        if c == 0:
            p.wait("scalar", s_init)
        if c >= 1:
            p.wait("scalar", s_l2, done["kl2"][c - 1])
        p.op("scalar", lambda e, pe=pe: e.activation(sqch[:], xch[pe][:], AF.Square),
             inc=s_stat)
        p.wait("vector", s_x, xld)
        p.op("vector", lambda e, pe=pe: e.reduce_sum(stat[:, 0:1], xch[pe][:], axis=AX.X))
        p.wait("vector", s_stat, s_stat.n)
        p.op("vector", lambda e: e.reduce_sum(stat[:, 1:2], sqch[:], axis=AX.X),
             inc=s_stat)
        p.wait("vector", s_stat, s_stat.n)
        p.op("vector", lambda e: e.tensor_scalar(out=stat[:, 2:3], in0=stat[:, 0:1],
                                                 scalar1=1.0 / D, scalar2=None,
                                                 op0=ALU.mult), inc=s_stat)
        p.wait("vector", s_stat, s_stat.n)
        p.op("vector", lambda e: e.tensor_tensor(stat[:, 3:4], stat[:, 2:3], stat[:, 2:3],
                                                 op=ALU.mult), inc=s_stat)
        p.wait("vector", s_stat, s_stat.n)
        p.op("vector", lambda e: e.scalar_tensor_tensor(
            out=stat[:, 4:5], in0=stat[:, 1:2], scalar=1.0 / D, in1=stat[:, 3:4],
            op0=ALU.mult, op1=ALU.subtract), inc=s_stat)
        p.wait("vector", s_stat, s_stat.n)
        p.op("vector", lambda e: e.tensor_scalar(out=stat[:, 5:6], in0=stat[:, 4:5],
                                                 scalar1=1e-5, scalar2=None, op0=ALU.add))
        p.op("vector", lambda e: e.tensor_scalar(out=stat[:, 6:7], in0=stat[:, 4:5],
                                                 scalar1=1.00001, scalar2=1e-10,
                                                 op0=ALU.mult, op1=ALU.add), inc=s_stat)
        p.wait("vector", s_stat, s_stat.n)
        _rsqrt(p, stat, 5, 2, nsc, cfg.newton)
        p.op("vector", lambda e: e.tensor_tensor(stat[:, 8:9], stat[:, 2:3], stat[:, 5:6],
                                                 op=ALU.mult))
        p.op("vector", lambda e: e.tensor_tensor(stat[:, 9:10], stat[:, 2:3], stat[:, 6:7],
                                                 op=ALU.mult), inc=s_stat)
        p.wait("vector", s_stat, s_stat.n)
        done["stat"][c] = s_stat.n

        if c >= 2:
            p.wait("vector", s_xw[pe], done["xw"][c - 2])
        if dbg:
            p.wait("vector", s_dbg, s_dbg.n)
        p.op("vector", lambda e, pe=pe: e.tensor_scalar(
            out=xnch[pe][:], in0=xch[pe][:], scalar1=stat[:, 5:6], scalar2=stat[:, 8:9],
            op0=ALU.mult, op1=ALU.subtract))
        p.op("vector", lambda e, pe=pe: e.tensor_scalar(
            out=nrm0[:], in0=xch[pe][:], scalar1=stat[:, 6:7], scalar2=stat[:, 9:10],
            op0=ALU.mult, op1=ALU.subtract), inc=s_nrm)
        p.wait("vector", s_nrm, s_nrm.n)
        if c == 0:
            p.wait("vector", s_bc, bc_done)
        p.op("vector", lambda e: e.tensor_tensor(nrm0[:], nrm0[:], lnw_bc[:], op=ALU.mult),
             inc=s_nrm)
        p.wait("vector", s_nrm, s_nrm.n)
        if c >= 2:
            p.wait("vector", s_trn, done["trnv"][c - 2])
        p.op("vector", lambda e, pe=pe: e.tensor_tensor(nrmb[pe][:], nrm0[:], lnb_bc[:],
                                                        op=ALU.add), inc=s_nrm)
        done["nrm"][c] = s_nrm.n
        p.wait("sync", s_nrm, done["nrm"][c])
        p.op("sync", lambda e, c=c, pe=pe: e.dma_start(
            out=xn_scr[c * 128:(c + 1) * 128, :], in_=xnch[pe][:]), inc=s_xw[pe],
            inc_k=16)
        done["xw"][c] = s_xw[pe].n
        if "dbg_xn" in dbg:
            p.op("sync", lambda e, c=c, pe=pe: e.dma_start(
                out=dbg["dbg_xn"][c * 128:(c + 1) * 128, :], in_=xnch[pe][:]),
                inc=s_dbg, inc_k=16)

        if c == 0:
            p.wait("tensor", s_wm, wm_done)
        if c >= 2:
            p.wait("tensor", s_trnv, done["trnv"][c - 2])
        p.wait("tensor", s_nrm, done["nrm"][c])
        for g in range(8):
            p.op("tensor", lambda e, g=g, pe=pe: e.transpose(
                PT[:, g * 128:(g + 1) * 128], nrmb[pe][:, g * 128:(g + 1) * 128],
                ident_b[:]), inc=(s_trn if g == 7 else None))
        p.wait("vector", s_trn, s_trn.n)
        p.op("vector", lambda e, pe=pe: e.tensor_copy(nT[pe][:], PT[:]), inc=s_trnv)
        done["trnv"][c] = s_trnv.n

        p.wait("tensor", s_trnv, done["trnv"][c])
        if c >= 2:
            p.wait("tensor", s_oe, done["oe"][c - 2])
        pqk_base = s_pqk.n
        for g in range(2):
            for cc in range(8):
                p.op("tensor", lambda e, g=g, cc=cc, pe=pe: e.matmul(
                    PB[2 + g][:, 0:512], nT[pe][:, cc * 128:(cc + 1) * 128],
                    wk_sb[:, cc * 1024 + g * 512:cc * 1024 + (g + 1) * 512],
                    start=(cc == 0), stop=(cc == 7)), inc=(s_pqk if cc == 7 else None))
        if c >= 1:
            p.wait("scalar", s_l2, done["l2"][c - 1])
        for g in range(2):
            p.wait("scalar", s_pqk, pqk_base + g + 1)
            p.op("scalar", lambda e, g=g: e.activation(
                kraw[0][:, g * 512:(g + 1) * 512], PB[2 + g][:, 0:512], AF.Silu),
                inc=(s_sqk if g == 1 else None))
        p.wait("scalar", s_sqk, s_sqk.n)
        p.op("scalar", lambda e: e.activation(nrm0[:], kraw[0][:], AF.Square), inc=s_sqk)
        done["sqk"][c] = s_sqk.n
        p.wait("vector", s_sqk, s_sqk.n)
        for h in range(H):
            p.op("vector", lambda e, h=h: e.reduce_sum(
                stat[:, 14 + h:15 + h], nrm0[:, h * DK:(h + 1) * DK], axis=AX.X),
                inc=(s_l2 if h == H - 1 else None))
        p.wait("vector", s_l2, s_l2.n)
        p.op("vector", lambda e: e.tensor_scalar(out=stat[:, 14:18], in0=stat[:, 14:18],
                                                 scalar1=1e-6, scalar2=None, op0=ALU.add),
             inc=s_l2)
        p.wait("vector", s_l2, s_l2.n)
        _rsqrt(p, stat, 14, 4, nsc, cfg.newton)
        if c >= 2:
            p.wait("vector", s_su, done["su"][c - 2])
        if dbg:
            p.wait("vector", s_dbg2, s_dbg2.n)
        for h in range(H):
            p.op("vector", lambda e, h=h, pe=pe: e.tensor_scalar(
                out=kn[pe][:, h * DK:(h + 1) * DK], in0=kraw[0][:, h * DK:(h + 1) * DK],
                scalar1=stat[:, 14 + h:15 + h], scalar2=None, op0=ALU.mult),
                inc=(s_l2 if h == H - 1 else None))
        done["kl2"][c] = s_l2.n
        p.wait("tensor", s_l2, done["kl2"][c])
        if c >= 2:
            p.wait("tensor", s_G, done["G"][c - 2])
        for g8 in range(8):
            p.op("tensor", lambda e, g8=g8, pe=pe: e.transpose(
                PT[:, g8 * 128:(g8 + 1) * 128], kn[pe][:, g8 * 128:(g8 + 1) * 128],
                ident_b[:]), inc=(s_tqk if g8 == 7 else None))
        p.wait("vector", s_tqk, s_tqk.n)
        p.op("vector", lambda e, pe=pe: e.tensor_copy(kT[pe][:], PT[:]), inc=s_eqk)
        done["ktr"][c] = s_eqk.n
        p.wait("tensor", s_eqk, done["ktr"][c])
        p.wait("tensor", s_iv, s_iv.n)
        for h in range(H):
            for g in range(2):
                p.op("tensor", lambda e, h=h, g=g, pe=pe: e.matmul(
                    PI[:, h * 128:(h + 1) * 128],
                    kT[pe][:, (h * 2 + g) * 128:(h * 2 + g + 1) * 128],
                    kT[pe][:, (h * 2 + g) * 128:(h * 2 + g + 1) * 128],
                    start=(g == 0), stop=(g == 1)), inc=(s_A if g == 1 else None))
        done["A"][c] = s_A.n
        s_pb_base = s_pv.n
        if c >= 2:
            p.wait("vector", s_vd, done["vd"][c - 2])
        for cc in range(8):
            p.op("tensor", lambda e, cc=cc, pe=pe: e.matmul(
                PI[:, 1024:1024 + H], nT[pe][:, cc * 128:(cc + 1) * 128],
                wb_sb[:, cc * H:(cc + 1) * H], start=(cc == 0), stop=(cc == 7)),
                inc=(s_pv if cc == 7 else None))
        p.wait("scalar", s_pv, s_pb_base + 1)
        p.op("scalar", lambda e, pe=pe: e.activation(beta_t[pe][:, 0:H],
                                                     PI[:, 1024:1024 + H],
                                                     AF.Tanh, scale=0.5), inc=s_bet)
        p.wait("vector", s_bet, s_bet.n)
        p.op("vector", lambda e, pe=pe: e.tensor_scalar(
            out=beta_t[pe][:, 0:H], in0=beta_t[pe][:, 0:H], scalar1=0.5, scalar2=0.5,
            op0=ALU.mult, op1=ALU.add), inc=s_bet)
        p.wait("vector", s_bet, s_bet.n)
        p.op("vector", lambda e, pe=pe: e.tensor_scalar(
            out=beta_t[pe][:, H:2 * H], in0=beta_t[pe][:, 0:H], scalar1=-1.0,
            scalar2=None, op0=ALU.mult), inc=s_bet)
        done["bet"][c] = s_bet.n

        for g in range(2):
            for cc in range(8):
                p.op("tensor", lambda e, g=g, cc=cc, pe=pe: e.matmul(
                    PB[g][:, 0:512], nT[pe][:, cc * 128:(cc + 1) * 128],
                    wq_sb[:, cc * 1024 + g * 512:cc * 1024 + (g + 1) * 512],
                    start=(cc == 0), stop=(cc == 7)), inc=(s_pqk if cc == 7 else None))
        for g in range(2):
            p.wait("scalar", s_pqk, pqk_base + 2 + g + 1)
            p.op("scalar", lambda e, g=g: e.activation(
                qraw[0][:, g * 512:(g + 1) * 512], PB[g][:, 0:512], AF.Silu),
                inc=(s_sqk if g == 1 else None))
        p.wait("scalar", s_sqk, s_sqk.n)
        p.op("scalar", lambda e: e.activation(sqch[:], qraw[0][:], AF.Square), inc=s_sqk)
        p.wait("vector", s_sqk, s_sqk.n)
        for h in range(H):
            p.op("vector", lambda e, h=h: e.reduce_sum(
                stat[:, 10 + h:11 + h], sqch[:, h * DK:(h + 1) * DK], axis=AX.X),
                inc=(s_l2 if h == H - 1 else None))
        p.wait("vector", s_l2, s_l2.n)
        p.op("vector", lambda e: e.tensor_scalar(out=stat[:, 10:14], in0=stat[:, 10:14],
                                                 scalar1=1e-6, scalar2=None, op0=ALU.add),
             inc=s_l2)
        p.wait("vector", s_l2, s_l2.n)
        _rsqrt(p, stat, 10, 4, nsc, cfg.newton)
        p.op("vector", lambda e: e.tensor_scalar(out=stat[:, 10:14], in0=stat[:, 10:14],
                                                 scalar1=DK ** -0.5, scalar2=None,
                                                 op0=ALU.mult), inc=s_l2)
        p.wait("vector", s_l2, s_l2.n)
        if c >= 2:
            p.wait("vector", s_O, done["O"][c - 2])
        for h in range(H):
            p.op("vector", lambda e, h=h, pe=pe: e.tensor_scalar(
                out=qn[pe][:, h * DK:(h + 1) * DK], in0=qraw[0][:, h * DK:(h + 1) * DK],
                scalar1=stat[:, 10 + h:11 + h], scalar2=None, op0=ALU.mult),
                inc=(s_l2 if h == H - 1 else None))
        done["l2"][c] = s_l2.n
        if "dbg_q" in dbg:
            p.wait("gpsimd", s_l2, done["l2"][c])
            p.op("gpsimd", lambda e, c=c, pe=pe: e.dma_start(
                out=dbg["dbg_q"][c * 128:(c + 1) * 128, :], in_=qn[pe][:]),
                inc=s_dbg2, inc_k=16)
            p.op("gpsimd", lambda e, c=c, pe=pe: e.dma_start(
                out=dbg["dbg_k"][c * 128:(c + 1) * 128, :], in_=kn[pe][:]),
                inc=s_dbg2, inc_k=16)
        p.wait("vector", s_A, done["A"][c])
        p.wait("vector", s_bet, done["bet"][c])
        p.wait("vector", s_it, s_it.n)
        for h in range(H):
            p.op("vector", lambda e, h=h, pe=pe: e.scalar_tensor_tensor(
                out=m_f[pe][:, h * 128:(h + 1) * 128], in0=PI[:, h * 128:(h + 1) * 128],
                scalar=beta_t[pe][:, H + h:H + h + 1], in1=tril01[:],
                op0=ALU.mult, op1=ALU.mult), inc=(s_M if h == H - 1 else None))
        p.wait("tensor", s_M, s_M.n)
        for h in range(H):
            p.op("tensor", lambda e, h=h, pe=pe: e.transpose(
                PI[:, 512 + h * 128:512 + (h + 1) * 128],
                m_f[pe][:, h * 128:(h + 1) * 128],
                ident_f[:]), inc=(s_Mt if h == H - 1 else None))
        p.wait("vector", s_Mt, s_Mt.n)
        p.op("vector", lambda e, pe=pe: e.tensor_copy(mt_f[pe][:], PI[:, 512:1024]),
             inc=s_Mt)
        p.wait("vector", s_Mt, s_Mt.n)
        for h in range(H):
            p.op("vector", lambda e, h=h, pe=pe: e.tensor_tensor(
                rt_f[pe][:, h * 128:(h + 1) * 128], mt_f[pe][:, h * 128:(h + 1) * 128],
                ident_f[:], op=ALU.add), inc=(s_Mt if h == H - 1 else None))
        done["Mt"][c] = s_Mt.n

        p.wait("tensor", s_l2, done["l2"][c])
        for g8 in range(8):
            p.op("tensor", lambda e, g8=g8, pe=pe: e.transpose(
                PT[:, g8 * 128:(g8 + 1) * 128], qn[pe][:, g8 * 128:(g8 + 1) * 128],
                ident_b[:]), inc=(s_tqk if g8 == 7 else None))
        p.wait("vector", s_tqk, s_tqk.n)
        p.op("vector", lambda e, pe=pe: e.tensor_copy(qT[pe][:], PT[:]), inc=s_eqk)
        done["eqk"][c] = s_eqk.n

        p.wait("tensor", s_sqk, done["sqk"][c])
        if c >= 2:
            p.wait("tensor", s_vd, done["vd"][c - 2])
        pv_base = s_pv.n
        for g in range(4):
            for cc in range(8):
                p.op("tensor", lambda e, g=g, cc=cc, pe=pe: e.matmul(
                    PB[g][:, 0:512], nT[pe][:, cc * 128:(cc + 1) * 128],
                    wv_sb[:, cc * 2048 + g * 512:cc * 2048 + (g + 1) * 512],
                    start=(cc == 0), stop=(cc == 7)), inc=(s_pv if cc == 7 else None))
        if c >= 2:
            p.wait("scalar", s_vd, done["vd"][c - 2])
        if dbg:
            p.wait("scalar", s_dbg, s_dbg.n)
        for g in range(4):
            p.wait("scalar", s_pv, pv_base + g + 1)
            p.op("scalar", lambda e, g=g, pe=pe: e.activation(
                vsil[pe][:, g * 512:(g + 1) * 512], PB[g][:, 0:512], AF.Silu),
                inc=(s_sv if g == 3 else None))
        done["sv"][c] = s_sv.n

        if "dbg_v" in dbg:
            p.wait("sync", s_sv, done["sv"][c])
            p.op("sync", lambda e, c=c, pe=pe: e.dma_start(
                out=dbg["dbg_v"][c * 128:(c + 1) * 128, :], in_=vsil[pe][:]),
                inc=s_dbg, inc_k=16)
        if "dbg_beta" in dbg:
            p.wait("sync", s_bet, done["bet"][c])
            p.op("sync", lambda e, c=c, pe=pe: e.dma_start(
                out=dbg["dbg_beta"][c * 128:(c + 1) * 128, :], in_=beta_t[pe][:, 0:H]),
                inc=s_dbg, inc_k=16)

    def inv_level(cn, lev):
        pn = cn % 2
        last = lev == 6
        if lev == 1:
            p.wait("tensor", s_bet, s_bet.n)
            p.wait("tensor", s_Mt, s_Mt.n)
        for h in range(H):
            p.op("tensor", lambda e, h=h, pn=pn: e.matmul(
                PI[:, h * 128:(h + 1) * 128], mt_f[pn][:, h * 128:(h + 1) * 128],
                m_f[pn][:, h * 128:(h + 1) * 128], start=True, stop=True),
                inc=(s_it if (last and h == H - 1) else None))
        if not last:
            for h in range(H):
                p.op("tensor", lambda e, h=h, pn=pn: e.matmul(
                    PI[:, 512 + h * 128:512 + (h + 1) * 128],
                    m_f[pn][:, h * 128:(h + 1) * 128],
                    mt_f[pn][:, h * 128:(h + 1) * 128],
                    start=True, stop=True), inc=(s_it if h == H - 1 else None))
        p.wait("vector", s_it, s_it.n)
        p.op("vector", lambda e, pn=pn: e.tensor_copy(x2_f[pn][:], PI[:, 0:512]),
             inc=s_iv)
        if not last:
            p.op("vector", lambda e, pn=pn: e.tensor_copy(mt_f[pn][:], PI[:, 512:1024]),
                 inc=s_iv)
        p.wait("tensor", s_iv, s_iv.n)
        for h in range(H):
            p.op("tensor", lambda e, h=h, pn=pn: e.matmul(
                PI[:, 1024 + h * 128:1024 + (h + 1) * 128],
                x2_f[pn][:, h * 128:(h + 1) * 128], rt_f[pn][:, h * 128:(h + 1) * 128],
                start=True, stop=True), inc=(s_it if h == H - 1 else None))
        p.wait("vector", s_it, s_it.n)
        for h in range(H):
            p.op("vector", lambda e, h=h, pn=pn: e.tensor_tensor(
                rt_f[pn][:, h * 128:(h + 1) * 128], rt_f[pn][:, h * 128:(h + 1) * 128],
                PI[:, 1024 + h * 128:1024 + (h + 1) * 128], op=ALU.add),
                inc=(s_iv if h == H - 1 else None))
        if not last:
            p.op("vector", lambda e, pn=pn: e.tensor_copy(m_f[pn][:], x2_f[pn][:]),
                 inc=s_iv)
        p.wait("tensor", s_iv, s_iv.n)
        if last:
            p.wait("vector", s_iv, s_iv.n)
            p.wait("vector", s_U, s_U.n)
            p.op("vector", lambda e, pn=pn: e.tensor_copy(rt_b[pn][:], rt_f[pn][:]),
                 inc=s_iv)
            inv_done[cn] = s_iv.n

    def rec(c):
        pe = c % 2
        cn = c + 1 if c + 1 < NCH else None
        p.wait("tensor", s_sv, s_sv.n)
        p.wait("tensor", s_eqk, done["eqk"][c])
        if c >= 1:
            p.wait("tensor", s_su, done["su"][c - 1])
        for h in range(H):
            for g in range(2):
                p.op("tensor", lambda e, h=h, g=g, pe=pe: e.matmul(
                    PB[h][:, 0:512], kT[pe][:, (h * 2 + g) * 128:(h * 2 + g + 1) * 128],
                    S_b[h][:, g * 512:(g + 1) * 512], start=(g == 0), stop=(g == 1)),
                    inc=(s_kv if g == 1 else None))
        kv_now = s_kv.n
        p.wait("vector", s_kv, kv_now)
        p.wait("vector", s_sv, done["sv"][c])
        for h in range(H):
            p.op("vector", lambda e, h=h, pe=pe: e.tensor_tensor(
                osq[:, (h % 2) * DV:((h % 2) + 1) * DV],
                vsil[pe][:, h * DV:(h + 1) * DV],
                PB[h][:, 0:512], op=ALU.subtract), inc=s_vd)
            if h == 0 and c >= 1:
                p.wait("vector", s_U, u_done_prev[0])
            p.wait("vector", s_vd, s_vd.n)
            p.op("vector", lambda e, h=h, pe=pe: e.tensor_scalar(
                out=vd_b[0][:, h * DV:(h + 1) * DV],
                in0=osq[:, (h % 2) * DV:((h % 2) + 1) * DV],
                scalar1=beta_t[pe][:, h:h + 1], scalar2=None, op0=ALU.mult),
                inc=(s_vd if h == H - 1 else None))
        done["vd"][c] = s_vd.n
        if cn is not None:
            inv_level(cn, 1)
        p.wait("tensor", s_vd, done["vd"][c])
        p.wait("tensor", s_iv, inv_done[c])
        for h in range(H):
            p.op("tensor", lambda e, h=h, pe=pe: e.matmul(
                PB[h][:, 0:512], rt_b[pe][:, h * 128:(h + 1) * 128],
                vd_b[0][:, h * DV:(h + 1) * DV], start=True, stop=True),
                inc=(s_U if h == H - 1 else None))
        u_done_prev[0] = s_U.n
        p.wait("vector", s_U, s_U.n)
        for h in range(H):
            p.op("vector", lambda e, h=h: e.tensor_copy(
                u_b[:, h * DV:(h + 1) * DV], PB[h][:, 0:512]),
                inc=(s_Ue if h == H - 1 else None))
        if cn is not None:
            inv_level(cn, 2)
        p.wait("tensor", s_Ue, s_Ue.n)
        for h in range(H):
            for g in range(2):
                p.op("tensor", lambda e, h=h, g=g, pe=pe: e.matmul(
                    PB[(h % 2) * 2 + g][:, 0:512],
                    kn[pe][:, h * DK + g * 128:h * DK + (g + 1) * 128],
                    u_b[:, h * DV:(h + 1) * DV], start=True, stop=True),
                    inc=(s_kv if g == 1 else None))
            p.wait("vector", s_kv, s_kv.n)
            for g in range(2):
                p.op("vector", lambda e, h=h, g=g: e.tensor_tensor(
                    S_b[h][:, g * 512:(g + 1) * 512], S_b[h][:, g * 512:(g + 1) * 512],
                    PB[(h % 2) * 2 + g][:, 0:512], op=ALU.add),
                    inc=(s_su if g == 1 else None))
            if h % 2 == 1 and h < H - 1:
                p.wait("tensor", s_su, s_su.n)
        done["su"][c] = s_su.n
        if cn is not None:
            inv_level(cn, 3)
        p.wait("tensor", s_su, done["su"][c])
        for h in range(H):
            for g in range(2):
                p.op("tensor", lambda e, h=h, g=g, pe=pe: e.matmul(
                    PB[h][:, 0:128],
                    kT[pe][:, (h * 2 + g) * 128:(h * 2 + g + 1) * 128],
                    qT[pe][:, (h * 2 + g) * 128:(h * 2 + g + 1) * 128],
                    start=(g == 0), stop=(g == 1)), inc=(s_G if g == 1 else None))
        done["G"][c] = s_G.n
        p.wait("vector", s_G, s_G.n)
        for h in range(H):
            p.op("vector", lambda e, h=h: e.scalar_tensor_tensor(
                out=gm_b[:, h * 128:(h + 1) * 128], in0=PB[h][:, 0:128],
                scalar=-1.0, in1=tril01[:], op0=ALU.mult, op1=ALU.mult),
                inc=(s_Gm if h == H - 1 else None))
        if cn is not None:
            inv_level(cn, 4)
        p.wait("tensor", s_Gm, s_Gm.n)
        for h in range(H):
            for g in range(2):
                p.op("tensor", lambda e, h=h, g=g, pe=pe: e.matmul(
                    PB[h][:, 0:512], qT[pe][:, (h * 2 + g) * 128:(h * 2 + g + 1) * 128],
                    S_b[h][:, g * 512:(g + 1) * 512], start=(g == 0), stop=False))
            p.op("tensor", lambda e, h=h: e.matmul(
                PB[h][:, 0:512], gm_b[:, h * 128:(h + 1) * 128],
                u_b[:, h * DV:(h + 1) * DV], start=False, stop=True),
                inc=(s_O if h == H - 1 else None))
        done["O"][c] = s_O.n
        if cn is not None:
            inv_level(cn, 5)
        p.wait("scalar", s_O, done["O"][c])
        for h in range(H):
            p.op("scalar", lambda e, h=h: e.activation(
                osq[:, (h % 2) * DV:((h % 2) + 1) * DV], PB[h][:, 0:512], AF.Square),
                inc=s_oe)
            p.wait("vector", s_oe, s_oe.n)
            p.op("vector", lambda e, h=h: e.reduce_sum(
                stat[:, 18 + h:19 + h], osq[:, (h % 2) * DV:((h % 2) + 1) * DV],
                axis=AX.X), inc=s_oe)
            if h < H - 1:
                p.wait("scalar", s_oe, s_oe.n)
        p.wait("vector", s_oe, s_oe.n)
        p.op("vector", lambda e: e.tensor_scalar(out=stat[:, 18:22], in0=stat[:, 18:22],
                                                 scalar1=1.0 / DV, scalar2=1e-5,
                                                 op0=ALU.mult, op1=ALU.add), inc=s_oe)
        p.wait("vector", s_oe, s_oe.n)
        _rsqrt(p, stat, 18, 4, nsc, cfg.newton)
        if c >= 2:
            p.wait("vector", s_ow[pe], done["ow"][c - 2])
        if dbg:
            p.wait("vector", s_dbg2, s_dbg2.n)
        for h in range(H):
            p.op("vector", lambda e, h=h, pe=pe: e.scalar_tensor_tensor(
                out=o_nw[pe][:, h * DV:(h + 1) * DV], in0=PB[h][:, 0:512],
                scalar=stat[:, 18 + h:19 + h], in1=onw_bc[:],
                op0=ALU.mult, op1=ALU.mult), inc=(s_oe if h == H - 1 else None))
        done["oe"][c] = s_oe.n
        p.wait("sync", s_oe, done["oe"][c])
        p.op("sync", lambda e, c=c, pe=pe: e.dma_start(
            out=on_scr[c * 128:(c + 1) * 128, :], in_=o_nw[pe][:]), inc=s_ow[pe],
            inc_k=16)
        done["ow"][c] = s_ow[pe].n
        if "dbg_o" in dbg:
            p.wait("gpsimd", s_oe, done["oe"][c])
            p.op("gpsimd", lambda e, c=c, pe=pe: e.dma_start(
                out=dbg["dbg_o"][c * 128:(c + 1) * 128, :], in_=o_nw[pe][:]),
                inc=s_dbg2, inc_k=16)
        if cn is not None:
            inv_level(cn, 6)

    u_done_prev = [0]
    inv_done = {}
    front(0)
    for c in range(NCH):
        if c + 1 < NCH:
            front(c + 1)
        if c == 0:
            for lev in range(1, 7):
                inv_level(0, lev)
        rec(c)

    s_ph0 = sems("s_ph0")
    for eng in ENGINES:
        if eng in ("sync", "tensor", "scalar"):
            p.op(eng, lambda e: e.nop(), inc=s_ph0)
        else:
            p.op(eng, lambda e: e.engine_nop(), inc=s_ph0)
    p.wait("sync", s_ph0, s_ph0.n)
    p.wait("gpsimd", s_ph0, s_ph0.n)
    p.wait("sync", s_x, s_x.n)
    mes.close()
    spes = ExitStack()

    def ssb(name, shape, dtype):
        return spes.enter_context(nc.sbuf_tensor(name, list(shape), dtype))

    wo_sb = ssb("wo_sb", (128, 16 * 1024), BF16)
    o_in = [ssb(f"o_in{i}", (128, H * DV), BF16) for i in range(2)]
    xn_in = [ssb(f"xn_in{i}", (128, D), F32) for i in range(2)]
    oT_b = [ssb(f"oT_b{i}", (128, 2048), BF16) for i in range(2)]
    s_out_b = [ssb(f"s_out_b{i}", (128, D), F32) for i in range(2)]

    s_oin = [sems("s_oin0"), sems("s_oin1")]
    s_xin = [sems("s_xin0"), sems("s_xin1")]
    p.op("gpsimd", lambda e: e.dma_start(
        out=wo_sb[:].rearrange("p (c n) -> p c n", c=16),
        in_=Wo.rearrange("(c p) n -> p c n", p=128)), inc=s_wm, inc_k=16)
    p.wait("tensor", s_wm, s_wm.n)
    done_sp = {}
    for c in range(NCH):
        pe = c % 2
        if c >= 2:
            p.wait("sync", s_toe, done_sp[c - 2][0])
            p.wait("sync", s_s, done_sp[c - 2][1])
            p.wait("sync", s_sw[pe], done_sp[c - 2][2])
        p.wait("sync", s_ow[pe], done["ow"][c])
        p.wait("sync", s_xw[pe], done["xw"][c])
        p.op("sync", lambda e, c=c, pe=pe: e.dma_start(
            out=o_in[pe][:], in_=on_scr[c * 128:(c + 1) * 128, :]),
            inc=s_oin[pe], inc_k=16)
        p.op("sync", lambda e, c=c, pe=pe: e.dma_start(
            out=xn_in[pe][:], in_=xn_scr[c * 128:(c + 1) * 128, :]),
            inc=s_xin[pe], inc_k=16)
        oin_d = s_oin[pe].n
        xin_d = s_xin[pe].n
        p.wait("tensor", s_oin[pe], oin_d)
        for grp in range(2):
            if not (c == 0 and grp == 0):
                p.wait("tensor", s_toe, s_toe.n)
            for j in range(8):
                p.op("tensor", lambda e, grp=grp, j=j, pe=pe: e.transpose(
                    PT[:, j * 128:(j + 1) * 128],
                    o_in[pe][:, (grp * 8 + j) * 128:(grp * 8 + j + 1) * 128],
                    ident_b[:]), inc=(s_to if j == 7 else None))
            p.wait("vector", s_to, s_to.n)
            p.op("vector", lambda e, grp=grp, pe=pe: e.tensor_copy(
                oT_b[pe][:, grp * 1024:(grp + 1) * 1024], PT[:]), inc=s_toe)
        toe_d = s_toe.n
        p.wait("tensor", s_toe, toe_d)
        if c >= 1:
            p.wait("tensor", s_s, s_s.n)
        for g in range(2):
            for cc in range(16):
                p.op("tensor", lambda e, g=g, cc=cc, pe=pe: e.matmul(
                    PB[g][:, 0:512], oT_b[pe][:, cc * 128:(cc + 1) * 128],
                    wo_sb[:, cc * 1024 + g * 512:cc * 1024 + (g + 1) * 512],
                    start=(cc == 0), stop=(cc == 15)), inc=(s_wo if cc == 15 else None))
        p.wait("vector", s_wo, s_wo.n)
        p.wait("vector", s_xin[pe], xin_d)
        if c >= 2:
            p.wait("vector", s_sw[pe], done_sp[c - 2][2])
        if "dbg_s" in dbg:
            p.wait("vector", s_dbg, s_dbg.n)
        for g in range(2):
            p.op("vector", lambda e, g=g, pe=pe: e.tensor_tensor(
                s_out_b[pe][:, g * 512:(g + 1) * 512],
                xn_in[pe][:, g * 512:(g + 1) * 512],
                PB[g][:, 0:512], op=ALU.add), inc=(s_s if g == 1 else None))
        p.wait("sync", s_s, s_s.n)
        p.op("sync", lambda e, c=c, pe=pe: e.dma_start(
            out=s_scr[c * 128:(c + 1) * 128, :], in_=s_out_b[pe][:]), inc=s_sw[pe],
            inc_k=16)
        done_sp[c] = (s_to.n, s_s.n, s_sw[pe].n)
        if "dbg_s" in dbg:
            p.op("sync", lambda e, c=c, pe=pe: e.dma_start(
                out=dbg["dbg_s"][c * 128:(c + 1) * 128, :], in_=s_out_b[pe][:]),
                inc=s_dbg, inc_k=16)
    spes.close()
    sw_all = (s_sw[0].n, s_sw[1].n)
    p.wait("sync", s_sw[0], s_sw[0].n)
    p.wait("sync", s_sw[1], s_sw[1].n)
    p.wait("sync", s_ow[0], s_ow[0].n)
    p.wait("sync", s_ow[1], s_ow[1].n)
    p.wait("sync", s_xw[0], s_xw[0].n)
    p.wait("sync", s_xw[1], s_xw[1].n)
    p.wait("sync", s_oin[0], s_oin[0].n)
    p.wait("sync", s_oin[1], s_oin[1].n)
    p.wait("sync", s_xin[0], s_xin[0].n)
    p.wait("sync", s_xin[1], s_xin[1].n)
    p.wait("sync", s_x, s_x.n)
    p.wait("sync", s_dbg, s_dbg.n)
    p.wait("gpsimd", s_wm, s_wm.n)
    p.wait("gpsimd", s_dbg2, s_dbg2.n)
    s_ph1 = sems("s_ph1")
    for eng in ENGINES:
        if eng in ("sync", "tensor", "scalar"):
            p.op(eng, lambda e: e.nop(), inc=s_ph1)
        else:
            p.op(eng, lambda e: e.engine_nop(), inc=s_ph1)
    ph1 = s_ph1.n
    mes.close()

    oes = ExitStack()

    def osb(name, shape, dtype):
        return oes.enter_context(nc.sbuf_tensor(name, list(shape), dtype))

    xnT_st = osb("xnT_st", (128, 8 * STOK), BF16)
    comb = osb("comb", (128, 8 * STOK), BF16)
    hact = osb("hact", (128, 32 * STOK), BF16)
    ubuf = [osb(f"ubuf{i}", (128, 8 * 512), BF16) for i in range(2)]
    dbuf = [osb(f"dbuf{i}", (128, 32 * 128), BF16) for i in range(2)]
    gw_sb = osb("gw_sb", (128, 8 * E), F32)
    gb_bc = osb("gb_bc", (128, E), F32)
    upb_sb = osb("upb_sb", (128, NE * 32), F32)
    dnb_sb = osb("dnb_sb", (128, NE * 8), F32)
    sbu_sb = osb("sbu_sb", (128, 32), F32)
    sbd_sb = osb("sbd_sb", (128, 8), F32)
    xo_ch = [osb(f"xo_ch{i}", (128, D), F32) for i in range(2)]
    xn_mo = osb("xn_mo", (128, D), F32)
    xnb_mo = osb("xnb_mo", (128, D), BF16)
    sq_mo = osb("sq_mo", (128, D), F32)
    xnf_T = osb("xnf_T", (128, 8 * 128), F32)
    sc_tm = osb("sc_tm", (128, E), F32)
    w_tm = osb("w_tm", (128, E), F32)
    wT_all = osb("wT_all", (8, STOK), F32)
    rtm = osb("rtm", (128, STC), F32)
    rT_sb = osb("rT_sb", (16, 128), F32)
    wrow = osb("wrow", (1, STOK), F32)
    wbc = osb("wbc", (128, 1024), F32)
    tmp_ev = osb("tmp_ev", (128, 512), F32)
    mx8 = osb("mx8", (128, STOK), F32)
    rbc = osb("rbc", (128, STOK), F32)
    graw = osb("graw", (128, STOK), F32)
    zg = osb("zg", (128, 8 * STOK), BF16)
    zgT = osb("zgT", (128, D), BF16)
    sch = [osb("sch0", (128, D), F32)]

    s_mo = sems("s_mo")
    s_xo = [sems("s_xo0"), sems("s_xo1")]
    s_ln2 = sems("s_ln2")
    s_xb = sems("s_xb")
    s_tx = sems("s_tx")
    s_txe = sems("s_txe")
    s_gt = sems("s_gt")
    s_gs = sems("s_gs")
    s_gw = sems("s_gw")
    s_wup = [sems("s_wup0"), sems("s_wup1")]
    s_wdn = [sems("s_wdn0"), sems("s_wdn1")]
    s_mmu = sems("s_mmu")
    s_gel = sems("s_gel")
    s_mmd = sems("s_mmd")
    s_evm = sems("s_evm")
    s_eva = sems("s_eva")
    s_wb2 = sems("s_wb2")
    s_wrow = sems("s_wrow")
    s_n2 = sems("s_n2")
    s_n2g = sems("s_n2g")
    s_tz = sems("s_tz")
    s_zg = sems("s_zg")
    s_ep = sems("s_ep")
    s_epg = sems("s_epg")
    s_out = sems("s_out")
    s_cz = sems("s_cz")

    for eng in ENGINES:
        p.wait(eng, s_ph1, ph1)
    p.wait("sync", s_cast, cast_done)
    p.wait("scalar", s_cast, cast_done)

    p.op("sync", lambda e: e.dma_start(
        out=gw_sb[:].rearrange("p (c n) -> p c n", c=8),
        in_=gate_w.rearrange("(c p) n -> p c n", p=128)), inc=s_mo, inc_k=16)
    p.op("sync", lambda e: e.dma_start(out=upb_sb[:], in_=upb_t), inc=s_mo, inc_k=16)
    p.op("sync", lambda e: e.dma_start(out=dnb_sb[:], in_=dnb_t), inc=s_mo, inc_k=16)
    p.op("sync", lambda e: e.dma_start(out=sbu_sb[:], in_=sb_up_t), inc=s_mo, inc_k=16)
    p.op("sync", lambda e: e.dma_start(out=sbd_sb[:], in_=sb_dn_t), inc=s_mo, inc_k=16)
    p.op("sync", lambda e: e.dma_start(out=wrow[0:1, 0:E], in_=gate_b), inc=s_mo, inc_k=16)
    p.wait("tensor", s_mo, s_mo.n)
    s_gbb = sems("s_gbb")
    p.op("tensor", lambda e: e.matmul(PB[0][:, 0:E], ones_row[:], wrow[0:1, 0:E],
                                      start=True, stop=True), inc=s_gbb)
    p.wait("vector", s_gbb, s_gbb.n)
    p.op("vector", lambda e: e.tensor_copy(gb_bc[:], PB[0][:, 0:E]), inc=s_gbb)
    gbb_done = s_gbb.n

    done_gs = {}
    done_txe = {}
    up_ld = {}
    dn_ld = {}
    up_cons = {}
    dn_cons = {}
    out_cons = {}
    nup = 0
    ndn = 0
    gel_done = {}
    evm_done = {}

    def load_up(ei, grp):
        gi = ei * 4 + grp
        if gi >= 2:
            p.wait("sync", s_mmu, up_cons[gi - 2])
        src = wbf_su[0] if ei == NE else wbf_up[ei]
        p.op("sync", lambda e, grp=grp, src=src, gi=gi: e.dma_start(
            out=ubuf[gi % 2][:].rearrange("p (c n) -> p c n", c=8),
            in_=src.rearrange("(c p n) -> p c n", p=128, n=II)[:, :, grp * 1024:(grp + 1) * 1024]),
            inc=s_wup, inc_k=16)
        up_ld[gi] = s_wup.n

    def load_dn(ei, dt):
        di = ei * 8 + dt
        if di >= 2:
            p.wait("scalar", s_mmd, dn_cons[di - 2])
        srcd = wbf_sd[0] if ei == NE else wbf_dn[ei]
        p.op("scalar", lambda e, dt=dt, srcd=srcd, di=di: e.dma_start(
            out=dbuf[di % 2][:], in_=srcd[dt].rearrange("(q m) -> q m", q=128)),
            inc=s_wdn, inc_k=16)
        dn_ld[di] = s_wdn.n

    for st in range(NST):
        st_c0 = st * STC
        for ci in range(STC):
            c = st_c0 + ci
            pe = ci % 2
            if ci >= 2:
                p.wait("sync", s_gs, done_gs[ci - 2])
            if ci <= 1 and st > 0:
                p.wait("sync", s_zg, prev_epi)
            p.op("sync", lambda e, c=c, pe=pe: e.dma_start(
                out=xo_ch[pe][:], in_=x_own[c * 128:(c + 1) * 128, :]),
                inc=s_xo[pe], inc_k=16)
            xod = s_xo[pe].n
            p.wait("scalar", s_xo[pe], xod)
            if st == 0 and ci == 0:
                p.wait("scalar", s_ph1, ph1)
            if not (st == 0 and ci == 0):
                p.wait("scalar", s_ln2, sq2_cons)
            if dbg:
                p.wait("scalar", s_mo, s_mo.n)
            p.op("scalar", lambda e, pe=pe: e.activation(sq_mo[:], xo_ch[pe][:],
                                                         AF.Square), inc=s_ln2)
            p.wait("vector", s_xo[pe], xod)
            p.op("vector", lambda e, pe=pe: e.reduce_sum(stat[:, 0:1], xo_ch[pe][:],
                                                         axis=AX.X))
            p.wait("vector", s_ln2, s_ln2.n)
            p.op("vector", lambda e: e.reduce_sum(stat[:, 1:2], sq_mo[:], axis=AX.X),
                 inc=s_ln2)
            sq2_cons = s_ln2.n
            p.wait("vector", s_ln2, s_ln2.n)
            p.op("vector", lambda e: e.tensor_scalar(out=stat[:, 2:3], in0=stat[:, 0:1],
                                                     scalar1=1.0 / D, scalar2=None,
                                                     op0=ALU.mult), inc=s_ln2)
            p.wait("vector", s_ln2, s_ln2.n)
            p.op("vector", lambda e: e.tensor_tensor(stat[:, 3:4], stat[:, 2:3],
                                                     stat[:, 2:3], op=ALU.mult),
                 inc=s_ln2)
            p.wait("vector", s_ln2, s_ln2.n)
            p.op("vector", lambda e: e.scalar_tensor_tensor(
                out=stat[:, 4:5], in0=stat[:, 1:2], scalar=1.0 / D, in1=stat[:, 3:4],
                op0=ALU.mult, op1=ALU.subtract), inc=s_ln2)
            p.wait("vector", s_ln2, s_ln2.n)
            p.op("vector", lambda e: e.tensor_scalar(out=stat[:, 5:6], in0=stat[:, 4:5],
                                                     scalar1=1e-5, scalar2=None,
                                                     op0=ALU.add), inc=s_ln2)
            p.wait("vector", s_ln2, s_ln2.n)
            _rsqrt(p, stat, 5, 1, nsc, cfg.newton)
            p.op("vector", lambda e: e.tensor_tensor(stat[:, 8:9], stat[:, 2:3],
                                                     stat[:, 5:6], op=ALU.mult),
                 inc=s_ln2)
            p.wait("vector", s_ln2, s_ln2.n)
            if ci == 0:
                p.wait("vector", s_out, s_out.n)
            p.op("vector", lambda e, pe=pe: e.tensor_scalar(
                out=xn_mo[:], in0=xo_ch[pe][:], scalar1=stat[:, 5:6],
                scalar2=stat[:, 8:9], op0=ALU.mult, op1=ALU.subtract), inc=s_ln2)
            p.wait("vector", s_ln2, s_ln2.n)
            p.op("vector", lambda e: e.tensor_copy(xnb_mo[:], xn_mo[:]), inc=s_ln2)
            ln2_done = s_ln2.n

            p.wait("tensor", s_ln2, ln2_done)
            if ci >= 2:
                p.wait("tensor", s_txe, done_txe[ci - 2])
            if not (st == 0 and ci == 0):
                p.wait("tensor", s_gs, prev_gs)
            for g in range(8):
                p.op("tensor", lambda e, g=g: e.transpose(
                    PT[:, g * 128:(g + 1) * 128], xnb_mo[:, g * 128:(g + 1) * 128],
                    ident_b[:]), inc=(s_tx if g == 7 else None))
            for half4 in range(2):
                for g in range(4):
                    p.op("tensor", lambda e, g=g, half4=half4: e.transpose(
                        PI[:, g * 128:(g + 1) * 128],
                        xn_mo[:, (half4 * 4 + g) * 128:(half4 * 4 + g + 1) * 128],
                        ident_f[:]), inc=(s_tx if g == 3 else None))
                p.wait("vector", s_tx, s_tx.n)
                p.op("vector", lambda e, half4=half4: e.tensor_copy(
                    xnf_T[:, half4 * 512:(half4 + 1) * 512], PI[:, 0:512]), inc=s_txe)
                if half4 == 0:
                    p.wait("tensor", s_txe, s_txe.n)
            p.wait("vector", s_tx, s_tx.n - 2)
            p.op("vector", lambda e, ci=ci: e.tensor_copy(
                xnT_st[:].rearrange("p (g n) -> p g n", g=8)[:, :, ci * 128:(ci + 1) * 128],
                PT[:].rearrange("p (g n) -> p g n", g=8)), inc=s_txe)
            done_txe[ci] = s_txe.n

            p.wait("tensor", s_txe, done_txe[ci])
            if st == 0 and ci == 0:
                p.wait("tensor", s_gbb, gbb_done)
            if not (st == 0 and ci == 0):
                p.wait("tensor", s_gs, sc_cons)
            for cc in range(8):
                p.op("tensor", lambda e, cc=cc: e.matmul(
                    PB[0][:, 0:E], xnf_T[:, cc * 128:(cc + 1) * 128],
                    gw_sb[:, cc * E:(cc + 1) * E], start=(cc == 0), stop=(cc == 7)),
                    inc=(s_gt if cc == 7 else None))
            p.wait("vector", s_gt, s_gt.n)
            if dbg:
                p.wait("vector", s_mo, s_mo.n)
            p.op("vector", lambda e: e.tensor_tensor(sc_tm[:], PB[0][:, 0:E], gb_bc[:],
                                                     op=ALU.add), inc=s_gs)
            sc_cons = s_gs.n
            p.wait("scalar", s_gs, s_gs.n)
            p.op("scalar", lambda e: e.activation(w_tm[:], sc_tm[:], AF.Tanh, scale=0.5),
                 inc=s_gs)
            p.wait("vector", s_gs, s_gs.n)
            p.op("vector", lambda e: e.tensor_scalar(out=sc_tm[:], in0=w_tm[:],
                                                     scalar1=0.5, scalar2=0.5,
                                                     op0=ALU.mult, op1=ALU.add), inc=s_gs)
            p.wait("vector", s_gs, s_gs.n)
            if "dbg_scores" in dbg:
                p.wait("sync", s_gs, s_gs.n)
                p.op("sync", lambda e, c=c: e.dma_start(
                    out=dbg["dbg_scores"][c * 128:(c + 1) * 128, :], in_=sc_tm[:]),
                    inc=s_mo, inc_k=16)
            p.op("vector", lambda e: e.reduce_max(stat[:, 0:1], sc_tm[:], axis=AX.X),
                 inc=s_gs)
            p.wait("vector", s_gs, s_gs.n)
            p.op("vector", lambda e: e.tensor_scalar(out=w_tm[:], in0=sc_tm[:],
                                                     scalar1=stat[:, 0:1], scalar2=None,
                                                     op0=ALU.is_ge), inc=s_gs)
            p.wait("vector", s_gs, s_gs.n)
            p.op("vector", lambda e: e.scalar_tensor_tensor(
                out=w_tm[:], in0=w_tm[:], scalar=-1e30, in1=sc_tm[:],
                op0=ALU.mult, op1=ALU.add), inc=s_gs)
            p.wait("vector", s_gs, s_gs.n)
            p.op("vector", lambda e: e.reduce_max(stat[:, 1:2], w_tm[:], axis=AX.X),
                 inc=s_gs)
            p.wait("vector", s_gs, s_gs.n)
            p.op("vector", lambda e: e.tensor_tensor(stat[:, 2:3], stat[:, 0:1],
                                                     stat[:, 1:2], op=ALU.add), inc=s_gs)
            p.wait("vector", s_gs, s_gs.n)
            p.op("vector", lambda e: e.tensor_scalar(out=stat[:, 2:3], in0=stat[:, 2:3],
                                                     scalar1=1e-6, scalar2=None,
                                                     op0=ALU.add), inc=s_gs)
            p.wait("vector", s_gs, s_gs.n)
            p.op("vector", lambda e: e.reciprocal(stat[:, 3:4], stat[:, 2:3]), inc=s_gs)
            p.wait("vector", s_gs, s_gs.n)
            p.op("vector", lambda e: e.tensor_scalar(out=w_tm[:], in0=sc_tm[:],
                                                     scalar1=stat[:, 1:2], scalar2=None,
                                                     op0=ALU.is_ge), inc=s_gs)
            p.wait("vector", s_gs, s_gs.n)
            p.op("vector", lambda e: e.tensor_tensor(w_tm[:], w_tm[:], sc_tm[:],
                                                     op=ALU.mult), inc=s_gs)
            p.wait("vector", s_gs, s_gs.n)
            p.op("vector", lambda e: e.tensor_scalar(out=w_tm[:], in0=w_tm[:],
                                                     scalar1=stat[:, 3:4], scalar2=None,
                                                     op0=ALU.mult), inc=s_gs)
            p.wait("vector", s_gs, s_gs.n)
            if "dbg_w8" in dbg:
                p.wait("sync", s_gs, s_gs.n)
                p.op("sync", lambda e, c=c: e.dma_start(
                    out=dbg["dbg_w8"][c * 128:(c + 1) * 128, :], in_=w_tm[:]),
                    inc=s_mo, inc_k=16)
            p.wait("tensor", s_gs, s_gs.n)
            p.op("tensor", lambda e: e.transpose(PI[0:E, 0:128], w_tm[:], ident_f[:]),
                 inc=s_gt)
            p.wait("vector", s_gt, s_gt.n)
            p.op("vector", lambda e, ci=ci: e.tensor_copy(
                wT_all[0:E, ci * 128:(ci + 1) * 128], PI[0:E, 0:128]), inc=s_gs)
            done_gs[ci] = s_gs.n
            prev_gs = s_gs.n
        gates_done = done_gs[STC - 1]

        p.wait("vector", s_eva, s_eva.n)
        p.op("vector", lambda e: e.memset(comb[:], 0.0), inc=s_cz)
        p.wait("gpsimd", s_cz, s_cz.n)
        cz_done = s_cz.n

        for ei in range(NE + 1):
            shared = ei == NE
            egi = ei + st * (NE + 1)
            for g0 in range(2):
                gi = egi * 8 + g0
                if gi >= 2:
                    p.wait("sync", s_mmu, up_cons[gi - 2])
                src = wbf_su[0] if shared else wbf_up[ei]
                p.op("sync", lambda e, g0=g0, src=src, gi=gi: e.dma_start(
                    out=ubuf[gi % 2][:].rearrange("p (c n) -> p c n", c=8),
                    in_=src.rearrange("(c p n) -> p c n", p=128, n=II)
                    [:, :, g0 * 512:(g0 + 1) * 512]), inc=s_wup[gi % 2], inc_k=16)
                up_ld[gi] = s_wup[gi % 2].n
            for d0 in range(2):
                di = egi * 8 + d0
                if di >= 2:
                    p.wait("scalar", s_mmd, dn_cons[di - 2])
                srcd = wbf_sd[0] if shared else wbf_dn[ei]
                p.op("scalar", lambda e, d0=d0, srcd=srcd, di=di: e.dma_start(
                    out=dbuf[di % 2][:], in_=srcd[d0].rearrange("(q m) -> q m", q=128)),
                    inc=s_wdn[di % 2], inc_k=16)
                dn_ld[di] = s_wdn[di % 2].n

            nt512 = STOK // 512
            if not shared:
                p.wait("gpsimd", s_gs, gates_done)
                p.wait("gpsimd", s_evm, s_evm.n)
                for t5 in range(nt512):
                    p.op("gpsimd", lambda e, ei=ei, t5=t5: e.dma_start(
                        out=wrow[0:1, 0:512],
                        in_=wT_all[ei:ei + 1, t5 * 512:(t5 + 1) * 512]),
                        inc=s_wrow, inc_k=16)
                    p.wait("tensor", s_wrow, s_wrow.n)
                    p.wait("tensor", s_evm, s_evm.n)
                    if ei == 0 and t5 == 0:
                        p.wait("tensor", s_gs, gates_done)
                    p.op("tensor", lambda e: e.matmul(
                        PI[:, 0:512], ones_row[:], wrow[0:1, 0:512],
                        start=True, stop=True), inc=s_wb2)
                    p.wait("vector", s_wb2, s_wb2.n)
                    p.op("vector", lambda e, t5=t5: e.tensor_copy(
                        wbc[:, t5 * 512:(t5 + 1) * 512], PI[:, 0:512]), inc=s_wb2)
                    p.wait("gpsimd", s_wb2, s_wb2.n)
                wb2_cons = s_wb2.n

            for grp in range(8):
                gi = egi * 8 + grp
                p.wait("tensor", s_wup[gi % 2], up_ld[gi])
                if ei == 0 and grp == 0:
                    p.wait("tensor", s_gs, gates_done)
                if grp == 0:
                    p.wait("tensor", s_evm, s_evm.n)
                for it4 in range(4):
                    it = grp * 4 + it4
                    if nup >= 2:
                        p.wait("tensor", s_gel, gel_done[nup - 2])
                    bb = (nup % 2) * 2
                    for cc in range(8):
                        for t5 in range(nt512):
                            p.op("tensor", lambda e, it4=it4, cc=cc, gi=gi, t5=t5,
                                 bb=bb: e.matmul(
                                     PB[bb + t5][:, 0:512],
                                     ubuf[gi % 2][:, cc * 512 + it4 * 128:
                                                  cc * 512 + it4 * 128 + 128],
                                     xnT_st[:, cc * STOK + t5 * 512:
                                            cc * STOK + (t5 + 1) * 512],
                                     start=(cc == 0), stop=(cc == 7)),
                                 inc=(s_mmu if (cc == 7 and t5 == nt512 - 1) else None))
                    p.wait("scalar", s_mmu, s_mmu.n)
                    if it == 0 and grp == 0:
                        p.wait("scalar", s_mmd, s_mmd.n)
                    bcol = (0 if shared else ei * 32) + it
                    bsrc = sbu_sb if shared else upb_sb
                    for t5 in range(nt512):
                        p.op("scalar", lambda e, it=it, t5=t5, bb=bb, bcol=bcol,
                             bsrc=bsrc: e.activation(
                                 hact[:, it * STOK + t5 * 512:it * STOK + (t5 + 1) * 512],
                                 PB[bb + t5][:, 0:512], AF.Gelu,
                                 bias=bsrc[:, bcol:bcol + 1], scale=1.0), inc=s_gel)
                    gel_done[nup] = s_gel.n
                    nup += 1
                up_cons[gi] = s_mmu.n
                if grp < 6:
                    gi2 = egi * 8 + grp + 2
                    p.wait("sync", s_mmu, up_cons[gi])
                    src = wbf_su[0] if shared else wbf_up[ei]
                    p.op("sync", lambda e, grp=grp, src=src, gi2=gi2: e.dma_start(
                        out=ubuf[gi2 % 2][:].rearrange("p (c n) -> p c n", c=8),
                        in_=src.rearrange("(c p n) -> p c n", p=128, n=II)
                        [:, :, (grp + 2) * 512:(grp + 3) * 512]), inc=s_wup[gi2 % 2],
                        inc_k=16)
                    up_ld[gi2] = s_wup[gi2 % 2].n

            p.wait("tensor", s_gel, s_gel.n)
            for dt in range(8):
                di = egi * 8 + dt
                p.wait("tensor", s_wdn[di % 2], dn_ld[di])
                if ndn >= 2:
                    p.wait("tensor", s_evm, evm_done[ndn - 2])
                bb = (ndn % 2) * 2
                for cc in range(32):
                    for t5 in range(nt512):
                        p.op("tensor", lambda e, cc=cc, di=di, t5=t5, bb=bb:
                             e.matmul(PB[bb + t5][:, 0:512],
                                      dbuf[di % 2][:, cc * 128:(cc + 1) * 128],
                                      hact[:, cc * STOK + t5 * 512:
                                           cc * STOK + (t5 + 1) * 512],
                                      start=(cc == 0), stop=(cc == 31)),
                             inc=(s_mmd if (cc == 31 and t5 == nt512 - 1) else None))
                p.wait("vector", s_mmd, s_mmd.n)
                if not shared:
                    p.wait("vector", s_wb2, wb2_cons)
                for t5 in range(nt512):
                    p.wait("vector", s_eva, s_eva.n)
                    if shared:
                        p.op("vector", lambda e, dt=dt, t5=t5, bb=bb: e.tensor_scalar(
                            out=tmp_ev[:], in0=PB[bb + t5][:, 0:512],
                            scalar1=sbd_sb[:, dt:dt + 1], scalar2=None, op0=ALU.add),
                            inc=s_evm)
                    else:
                        p.op("vector", lambda e, dt=dt, t5=t5, bb=bb, ei=ei:
                             e.scalar_tensor_tensor(
                                 out=tmp_ev[:], in0=PB[bb + t5][:, 0:512],
                                 scalar=dnb_sb[:, ei * 8 + dt:ei * 8 + dt + 1],
                                 in1=wbc[:, t5 * 512:(t5 + 1) * 512],
                                 op0=ALU.add, op1=ALU.mult), inc=s_evm)
                    p.wait("gpsimd", s_evm, s_evm.n)
                    if st == 0 and ei == 0 and dt == 0 and t5 == 0:
                        p.wait("gpsimd", s_cz, cz_done)
                    p.op("gpsimd", lambda e, dt=dt, t5=t5: e.tensor_tensor(
                        comb[:, dt * STOK + t5 * 512:dt * STOK + (t5 + 1) * 512],
                        comb[:, dt * STOK + t5 * 512:dt * STOK + (t5 + 1) * 512],
                        tmp_ev[:], op=ALU.add), inc=s_eva)
                evm_done[ndn] = s_evm.n
                ndn += 1
                dn_cons[di] = s_mmd.n
                if dt < 6:
                    di2 = egi * 8 + dt + 2
                    p.wait("scalar", s_mmd, dn_cons[di])
                    srcd = wbf_sd[0] if shared else wbf_dn[ei]
                    p.op("scalar", lambda e, dt=dt, srcd=srcd, di2=di2: e.dma_start(
                        out=dbuf[di2 % 2][:],
                        in_=srcd[dt + 2].rearrange("(q m) -> q m", q=128)),
                        inc=s_wdn[di2 % 2], inc_k=16)
                    dn_ld[di2] = s_wdn[di2 % 2].n
        p.wait("scalar", s_eva, s_eva.n)
        p.op("scalar", lambda e: e.activation(mx8[:], comb[:, 0:STOK], AF.Abs), inc=s_n2)
        for g in range(1, 8):
            p.wait("scalar", s_n2, s_n2.n)
            p.op("scalar", lambda e, g=g: e.activation(
                graw[:], comb[:, g * STOK:(g + 1) * STOK], AF.Abs), inc=s_n2)
            p.wait("vector", s_n2, s_n2.n)
            p.op("vector", lambda e: e.tensor_tensor(mx8[:], mx8[:], graw[:],
                                                     op=ALU.max), inc=s_n2)
        p.wait("gpsimd", s_n2, s_n2.n)
        p.op("gpsimd", lambda e: e.dma_start(out=graw[0:64, :], in_=mx8[64:128, :]),
             inc=s_n2g, inc_k=16)
        p.wait("vector", s_n2g, s_n2g.n)
        p.op("vector", lambda e: e.tensor_tensor(mx8[0:64, :], mx8[0:64, :],
                                                 graw[0:64, :], op=ALU.max),
             inc=s_n2)
        p.wait("gpsimd", s_n2, s_n2.n)
        p.op("gpsimd", lambda e: e.dma_start(out=graw[0:32, :], in_=mx8[32:64, :]),
             inc=s_n2g, inc_k=16)
        p.wait("vector", s_n2g, s_n2g.n)
        p.wait("vector", s_n2, s_n2.n)
        p.op("vector", lambda e: e.tensor_tensor(mx8[0:32, :], mx8[0:32, :],
                                                 graw[0:32, :], op=ALU.max), inc=s_n2)
        for ci in range(STC):
            p.wait("tensor", s_n2, s_n2.n)
            p.op("tensor", lambda e, ci=ci: e.transpose(
                PI[:, 0:32], mx8[0:32, ci * 128:(ci + 1) * 128],
                ident_f[0:32, 0:32]), inc=s_tz)
            p.wait("vector", s_tz, s_tz.n)
            p.op("vector", lambda e, ci=ci: e.reduce_max(
                stat[:, 0:1], PI[:, 0:32], axis=AX.X), inc=s_n2)
            p.wait("vector", s_n2, s_n2.n)
            p.op("vector", lambda e: e.tensor_scalar(out=stat[:, 0:1], in0=stat[:, 0:1],
                                                     scalar1=1e-5, scalar2=None,
                                                     op0=ALU.add), inc=s_n2)
            p.wait("vector", s_n2, s_n2.n)
            p.op("vector", lambda e, ci=ci: e.reciprocal(rtm[:, ci:ci + 1],
                                                         stat[:, 0:1]), inc=s_n2)
            p.wait("vector", s_n2, s_n2.n)
            p.wait("tensor", s_n2, s_n2.n)
        p.op("tensor", lambda e: e.transpose(PI[0:STC, 0:128], rtm[:], ident_f[:]),
             inc=s_tz)
        p.wait("vector", s_tz, s_tz.n)
        p.op("vector", lambda e: e.tensor_copy(rT_sb[0:STC, :], PI[0:STC, 0:128]),
             inc=s_n2)
        p.wait("gpsimd", s_n2, s_n2.n)
        for ci in range(STC):
            p.op("gpsimd", lambda e, ci=ci: e.dma_start(
                out=wrow[0:1, ci * 128:(ci + 1) * 128], in_=rT_sb[ci:ci + 1, :]),
                inc=s_n2g, inc_k=16)
        rrow_done = s_n2g.n
        p.wait("tensor", s_n2g, rrow_done)
        for g5 in range(STOK // 512):
            p.op("tensor", lambda e, g5=g5: e.matmul(
                PI[:, 0:512], ones_row[:], wrow[0:1, g5 * 512:(g5 + 1) * 512],
                start=True, stop=True), inc=s_tz)
            p.wait("vector", s_tz, s_tz.n)
            p.op("vector", lambda e, g5=g5: e.tensor_copy(
                rbc[:, g5 * 512:(g5 + 1) * 512], PI[:, 0:512]), inc=s_n2)
            p.wait("tensor", s_n2, s_n2.n)
        for g in range(8):
            p.wait("vector", s_n2, s_n2.n)
            p.op("vector", lambda e, g=g: e.tensor_tensor(
                graw[:], comb[:, g * STOK:(g + 1) * STOK], rbc[:], op=ALU.mult),
                inc=s_n2)
            p.wait("scalar", s_n2, s_n2.n)
            p.op("scalar", lambda e: e.activation(mx8[:], graw[:], AF.Tanh, scale=0.5),
                 inc=s_n2)
            p.wait("vector", s_n2, s_n2.n)
            p.op("vector", lambda e, g=g: e.scalar_tensor_tensor(
                out=zg[:, g * STOK:(g + 1) * STOK], in0=mx8[:], scalar=1.0, in1=graw[:],
                op0=ALU.add, op1=ALU.mult), inc=s_zg)
            p.wait("vector", s_zg, s_zg.n)
        zg_done = s_zg.n
        for ci in range(STC):
            c = st_c0 + ci
            pe = ci % 2
            p.wait("tensor", s_zg, zg_done)
            if not (st == 0 and ci == 0):
                p.wait("tensor", s_zg, zgt_cons)
            for g in range(8):
                p.op("tensor", lambda e, g=g, ci=ci: e.transpose(
                    PT[:, g * 128:(g + 1) * 128],
                    zg[:, g * STOK + ci * 128:g * STOK + (ci + 1) * 128], ident_b[:]),
                    inc=(s_tz if g == 7 else None))
            p.wait("vector", s_tz, s_tz.n)
            p.op("vector", lambda e: e.tensor_copy(zgT[:], PT[:]), inc=s_zg)
            zgt_cons = s_zg.n
            if ci == 0:
                p.wait("sync", s_ln2, s_ln2.n)
            if not (st == 0 and ci == 0):
                p.wait("sync", s_zg, szg_done)
            if not (st == 0 and ci <= 1):
                key = (st, ci - 2) if ci >= 2 else (st - 1, STC + ci - 2)
                p.wait("sync", s_out, out_cons.get(key, 0))
            oc = st * STC + ci
            p.wait("gpsimd", s_sw[0], sw_all[0])
            p.wait("gpsimd", s_sw[1], sw_all[1])
            if not (st == 0 and ci == 0):
                p.wait("gpsimd", s_zg, szg_done)
            p.op("gpsimd", lambda e, oc=oc: e.dma_start(
                out=idxw_sb[:],
                in_=idxw[0:1, oc * 128:(oc + 1) * 128].rearrange("r (p n) -> (r p) n", n=1)),
                inc=s_epg, inc_k=16)
            p.wait("gpsimd", s_epg, s_epg.n)
            p.op("gpsimd", lambda e: e.indirect_dma_start(
                out=sch[0][:], out_offset=None, in_=s_scr,
                in_offset=IndirectOffsetOnAxis(ap=idxw_sb[:], axis=0)),
                inc=s_epg, inc_k=16)
            p.op("sync", lambda e, c=c, pe=pe: e.dma_start(
                out=xo_ch[pe][:], in_=x_own[c * 128:(c + 1) * 128, :]),
                inc=s_ep, inc_k=16)
            ep_ld = s_ep.n
            p.wait("vector", s_ep, ep_ld)
            p.wait("vector", s_out, s_out.n)
            p.wait("vector", s_epg, s_epg.n)
            p.wait("vector", s_zg, zgt_cons)
            if dbg:
                p.wait("vector", s_mo, s_mo.n)
            p.op("vector", lambda e, pe=pe: e.tensor_tensor(
                sq_mo[:], sch[0][:], zgT[:], op=ALU.mult), inc=s_zg)
            szg_done = s_zg.n
            p.wait("vector", s_zg, s_zg.n)
            p.op("vector", lambda e, pe=pe: e.scalar_tensor_tensor(
                out=xn_mo[:], in0=sq_mo[:], scalar=0.5, in1=xo_ch[pe][:],
                op0=ALU.mult, op1=ALU.add), inc=s_zg)
            och_done = s_zg.n
            prev_epi = s_zg.n
            p.wait("sync", s_zg, och_done)
            p.op("sync", lambda e, c=c, pe=pe: e.dma_start(
                out=out_d[c * 128:(c + 1) * 128, :], in_=xn_mo[:]),
                inc=s_out, inc_k=16)
            out_cons[(st, ci)] = s_out.n
            if "dbg_zg" in dbg:
                p.op("sync", lambda e, c=c: e.dma_start(
                    out=dbg["dbg_zg"][c * 128:(c + 1) * 128, :], in_=zgT[:]
                    if False else sq_mo[:]), inc=s_mo, inc_k=16)

    p.wait("sync", s_out, s_out.n)
    if dbg:
        p.wait("sync", s_mo, s_mo.n)
        p.wait("sync", s_dbg, s_dbg.n)
        p.wait("gpsimd", s_dbg2, s_dbg2.n)

    with nc.Block() as blk:
        blk.sync(lambda e: [f(e) for f in p.q["sync"]])
        blk.tensor(lambda e: [f(e) for f in p.q["tensor"]])
        blk.vector(lambda e: [f(e) for f in p.q["vector"]])
        blk.scalar(lambda e: [f(e) for f in p.q["scalar"]])
        blk.gpsimd(lambda e: [f(e) for f in p.q["gpsimd"]])
    oes.close()
    es.close()
    return nc


_CACHE = {}


def _get_nc(cfg_key):
    if cfg_key not in _CACHE:
        cfg = Cfg()
        _CACHE[cfg_key] = build(cfg)
    return _CACHE[cfg_key]


def kernel(L0_attn_x, L0_attn_ln_w, L0_attn_ln_b, L0_attn_Wq, L0_attn_Wk, L0_attn_Wv,
           L0_attn_Wb, L0_attn_o_norm_w, L0_attn_Wo, L0_ffn_gate_w, L0_ffn_bias,
           L0_ffn_up_w, L0_ffn_up_b, L0_ffn_down_w, L0_ffn_down_b, L0_ffn_sw_up,
           L0_ffn_sb_up, L0_ffn_sw_down, L0_ffn_sb_down):
    x = np.ascontiguousarray(L0_attn_x, dtype=np.float32)
    nc = _get_nc("full")

    upb_t = np.ascontiguousarray(
        L0_ffn_up_b.reshape(E, 32, 128).transpose(2, 0, 1).reshape(128, E * 32))
    dnb_t = np.ascontiguousarray(
        L0_ffn_down_b.reshape(E, 8, 128).transpose(2, 0, 1).reshape(128, E * 8))
    sbu_t = np.ascontiguousarray(L0_ffn_sb_up.reshape(32, 128).T)
    sbd_t = np.ascontiguousarray(L0_ffn_sb_down.reshape(8, 128).T)

    common = {
        "ln_w": L0_attn_ln_w.reshape(1, D), "ln_b": L0_attn_ln_b.reshape(1, D),
        "Wq": L0_attn_Wq, "Wk": L0_attn_Wk, "Wv": L0_attn_Wv, "Wb": L0_attn_Wb,
        "onw": L0_attn_o_norm_w.reshape(1, DV), "Wo": L0_attn_Wo,
        "gate_w": L0_ffn_gate_w, "gate_b": L0_ffn_bias.reshape(1, E),
        "upw": L0_ffn_up_w, "upb_t": upb_t, "dnw": L0_ffn_down_w, "dnb_t": dnb_t,
        "sw_up": L0_ffn_sw_up, "sb_up_t": sbu_t, "sw_dn": L0_ffn_sw_down,
        "sb_dn_t": sbd_t,
    }
    in_maps = []
    for core in range(8):
        b = core // 2
        half = core % 2
        m = dict(common)
        m["x_full"] = x[b]
        m["x_own"] = x[b, half * 4096:(half + 1) * 4096]
        idx = np.arange(4096, dtype=np.int32) + 4096 * half
        m["idxw"] = idx.reshape(1, 4096)
        in_maps.append(m)

    res = run_bass_kernel_spmd(nc, in_maps, core_ids=list(range(8)))
    out = np.empty((B, T, D), np.float32)
    for core in range(8):
        b = core // 2
        half = core % 2
        out[b, half * 4096:(half + 1) * 4096] = res.results[core]["out"]
    return out


B = 4
T = 8192


# version 05
# speedup vs baseline: 1.1756x; 1.0011x over previous
import numpy as np
import ml_dtypes
from contextlib import ExitStack

import concourse.bass as bass
import concourse.mybir as mybir
from concourse.bass_utils import run_bass_kernel_spmd
from concourse.bass_core_rust import IndirectOffsetOnAxis

F32 = mybir.dt.float32
BF16 = mybir.dt.bfloat16
I32 = mybir.dt.int32
AF = mybir.ActivationFunctionType
ALU = mybir.AluOpType
AX = mybir.AxisListType

D = 1024
H = 4
DK = 256
DV = 512
E = 8
II = 4096
C = 128


class Cfg:
    n_chunks = 64
    own_chunks = 32
    st_chunks = 8
    n_exp = 8
    newton = 3
    debug = ()


ENGINES = ("sync", "tensor", "vector", "scalar", "gpsimd")


class Sem:
    __slots__ = ("h", "n", "name")

    def __init__(self, h, name):
        self.h = h
        self.n = 0
        self.name = name


class P:
    def __init__(self, nc, es):
        self.nc = nc
        self.es = es
        self.q = {e: [] for e in ENGINES}
        self._sems = {}

    def sem(self, name):
        if name not in self._sems:
            self._sems[name] = Sem(self.es.enter_context(self.nc.semaphore(name)), name)
        return self._sems[name]

    def wait(self, eng, sem, val=None):
        v = sem.n if val is None else val
        if v > 0:
            self.q[eng].append(lambda e, s=sem.h, v=v: e.wait_ge(s, v))

    def op(self, eng, fn, inc=None, inc_k=1):
        if inc is not None:
            inc.n += inc_k

            def run(e, fn=fn, s=inc.h, k=inc_k):
                fn(e).then_inc(s, k)
        else:
            def run(e, fn=fn):
                fn(e)
        self.q[eng].append(run)


def _rsqrt(p, buf, col0, ncol, scr, iters):
    sn = p.sem("snl")
    x = buf[:, col0:col0 + ncol]
    y = scr[:, 0:ncol]
    t = scr[:, ncol:2 * ncol]
    p.op("vector", lambda e: e.tensor_scalar(out=scr[:, 0:ncol].bitcast(I32),
                                             in0=x.bitcast(I32), scalar1=1, scalar2=None,
                                             op0=ALU.logical_shift_right), inc=sn)
    p.wait("vector", sn)
    p.op("vector", lambda e: e.tensor_scalar(out=scr[:, 2 * ncol:3 * ncol].bitcast(I32),
                                             in0=scr[:, 0:ncol].bitcast(I32),
                                             scalar1=0x5F3759DF, scalar2=-1,
                                             op0=ALU.subtract, op1=ALU.mult), inc=sn)
    y = scr[:, 2 * ncol:3 * ncol]
    for _ in range(iters):
        p.wait("vector", sn)
        p.op("vector", lambda e: e.tensor_tensor(t, y, y, op=ALU.mult), inc=sn)
        p.wait("vector", sn)
        p.op("vector", lambda e: e.scalar_tensor_tensor(out=t, in0=x, scalar=-0.5, in1=t,
                                                        op0=ALU.mult, op1=ALU.mult), inc=sn)
        p.wait("vector", sn)
        p.op("vector", lambda e: e.tensor_scalar(out=t, in0=t, scalar1=1.5, scalar2=None,
                                                 op0=ALU.add), inc=sn)
        p.wait("vector", sn)
        p.op("vector", lambda e: e.tensor_tensor(y, y, t, op=ALU.mult), inc=sn)
    p.wait("vector", sn)
    p.op("vector", lambda e: e.tensor_copy(x, y), inc=sn)
    p.wait("vector", sn)


def build(cfg):
    nc = bass.Bass("TRN2", target_bir_lowering=False)
    NCH = cfg.n_chunks
    OCH = cfg.own_chunks
    STC = cfg.st_chunks
    NE = cfg.n_exp
    NST = OCH // STC
    STOK = STC * C
    TOKB = NCH * C
    TOKO = OCH * C

    def din(name, shape, dtype=F32):
        return nc.dram_tensor(name, list(shape), dtype, kind="ExternalInput").ap()

    x_full = din("x_full", (TOKB, D))
    x_own = din("x_own", (TOKO, D))
    ln_w = din("ln_w", (1, D))
    ln_b = din("ln_b", (1, D))
    Wq = din("Wq", (D, H * DK))
    Wk = din("Wk", (D, H * DK))
    Wv = din("Wv", (D, H * DV))
    Wb = din("Wb", (D, H))
    onw = din("onw", (1, DV))
    Wo = din("Wo", (H * DV, D))
    gate_w = din("gate_w", (D, E))
    gate_b = din("gate_b", (1, E))
    upw = din("upw", (NE, D, II))
    upb_t = din("upb_t", (128, NE * 32))
    dnw = din("dnw", (NE, II, D))
    dnb_t = din("dnb_t", (128, NE * 8))
    sw_up = din("sw_up", (D, II))
    sb_up_t = din("sb_up_t", (128, 32))
    sw_dn = din("sw_dn", (II, D))
    sb_dn_t = din("sb_dn_t", (128, 8))
    idxw = din("idxw", (1, OCH * 128), I32)

    out_d = nc.dram_tensor("out", [TOKO, D], F32, kind="ExternalOutput").ap()
    s_scr = nc.dram_tensor("s_scr", [NCH * 128, D], F32, kind="Internal").ap()
    xn_scr = nc.dram_tensor("xn_scr", [NCH * 128, D], F32, kind="Internal").ap()
    on_scr = nc.dram_tensor("on_scr", [NCH * 128, H * DV], BF16, kind="Internal").ap()
    wbf_up = nc.dram_tensor("wbf_up", [NE, D * II], BF16, kind="Internal").ap()
    wbf_dn = nc.dram_tensor("wbf_dn", [NE, 8, 128 * 32 * 128], BF16, kind="Internal").ap()
    wbf_su = nc.dram_tensor("wbf_su", [1, D * II], BF16, kind="Internal").ap()
    wbf_sd = nc.dram_tensor("wbf_sd", [1, 8, 128 * 32 * 128], BF16, kind="Internal").ap()

    dbg = {}
    for name in cfg.debug:
        shp = {"dbg_s": (TOKB, D), "dbg_xn": (TOKB, D), "dbg_nrm": (TOKB, D),
               "dbg_q": (TOKB, H * DK), "dbg_k": (TOKB, H * DK),
               "dbg_v": (TOKB, H * DV), "dbg_beta": (TOKB, H),
               "dbg_o": (TOKB, H * DV), "dbg_scores": (TOKO, E),
               "dbg_w8": (TOKO, E), "dbg_zg": (TOKO, D)}[name]
        dbg[name] = nc.dram_tensor(name, list(shp), F32, kind="ExternalOutput").ap()

    es = ExitStack()
    p = P(nc, es)
    sems = p.sem

    def sb(name, shape, dtype):
        return es.enter_context(nc.sbuf_tensor(name, list(shape), dtype))

    def psum(name, shape, dtype=F32):
        return es.enter_context(nc.psum_tensor(name, list(shape), dtype))

    ident_b = sb("ident_b", (128, 128), BF16)
    ident_f = sb("ident_f", (128, 128), F32)
    tril01 = sb("tril01", (128, 128), F32)
    triu01 = sb("triu01", (128, 128), F32)
    ones_row = sb("ones_row", (1, 128), F32)
    nsc = sb("nsc", (128, 32), F32)
    stat = sb("stat", (128, 24), F32)
    idxw_sb = sb("idxw_sb", (128, 1), I32)

    PB = [psum(f"pb{i}", (128, 512), F32) for i in range(4)]
    PT = psum("pt", (128, 1024), BF16)
    PI = psum("pi", (128, 1536), F32)

    s_init = sems("s_init")
    p.op("gpsimd", lambda e: e.memset(ident_b[:], 0.0), inc=s_init)
    p.wait("gpsimd", s_init)
    p.op("gpsimd", lambda e: e.affine_select(out=ident_b[:], in_=ident_b[:],
                                             compare_op=ALU.not_equal, fill=1.0, base=0,
                                             pattern=[[-1, 128]], channel_multiplier=1),
         inc=s_init)
    p.op("gpsimd", lambda e: e.memset(ident_f[:], 0.0), inc=s_init)
    p.wait("gpsimd", s_init)
    p.op("gpsimd", lambda e: e.affine_select(out=ident_f[:], in_=ident_f[:],
                                             compare_op=ALU.not_equal, fill=1.0, base=0,
                                             pattern=[[-1, 128]], channel_multiplier=1),
         inc=s_init)
    p.op("gpsimd", lambda e: e.memset(tril01[:], 0.0), inc=s_init)
    p.wait("gpsimd", s_init)
    p.op("gpsimd", lambda e: e.affine_select(out=tril01[:], in_=tril01[:],
                                             compare_op=ALU.is_ge, fill=1.0, base=0,
                                             pattern=[[1, 128]], channel_multiplier=-1),
         inc=s_init)
    p.op("gpsimd", lambda e: e.memset(triu01[:], 1.0), inc=s_init)
    p.wait("gpsimd", s_init)
    p.op("gpsimd", lambda e: e.affine_select(out=triu01[:], in_=triu01[:],
                                             compare_op=ALU.is_ge, fill=0.0, base=0,
                                             pattern=[[1, 128]], channel_multiplier=-1),
         inc=s_init)
    p.op("gpsimd", lambda e: e.memset(ones_row[:], 1.0), inc=s_init)

    mes = ExitStack()

    def msb(name, shape, dtype):
        return mes.enter_context(nc.sbuf_tensor(name, list(shape), dtype))

    wq_sb = msb("wq_sb", (128, 8 * 1024), BF16)
    wk_sb = msb("wk_sb", (128, 8 * 1024), BF16)
    wv_sb = msb("wv_sb", (128, 8 * 2048), BF16)
    wb_sb = msb("wb_sb", (128, 8 * H), BF16)
    lnw_bc = msb("lnw_bc", (128, D), F32)
    lnb_bc = msb("lnb_bc", (128, D), F32)
    onw_bc = msb("onw_bc", (128, DV), F32)
    lnrow = msb("lnrow", (1, 2 * D + DV), F32)
    S_b = [msb(f"S_b{h}", (128, 2 * DV), BF16) for h in range(H)]
    xch = [msb(f"xch{i}", (128, D), F32) for i in range(2)]
    xnch = [msb(f"xnch{i}", (128, D), F32) for i in range(2)]
    sqch = msb("sqch", (128, D), F32)
    nrm0 = msb("nrm0", (128, D), F32)
    osq = msb("osq", (128, 2 * DV), F32)
    nrmb = [msb(f"nrmb{i}", (128, D), BF16) for i in range(2)]
    nT = [msb(f"nT{i}", (128, 8 * 128), BF16) for i in range(2)]
    qraw = [msb("qraw0", (128, H * DK), F32)]
    kraw = [msb("kraw0", (128, H * DK), F32)]
    qn = [msb(f"qn{i}", (128, H * DK), BF16) for i in range(2)]
    kn = [msb(f"kn{i}", (128, H * DK), BF16) for i in range(2)]
    vsil = [msb(f"vsil{i}", (128, H * DV), F32) for i in range(2)]
    beta_t = [msb(f"beta_t{i}", (128, 2 * H), F32) for i in range(2)]
    qT = [msb(f"qT{i}", (128, H * 2 * 128), BF16) for i in range(2)]
    kT = [msb(f"kT{i}", (128, H * 2 * 128), BF16) for i in range(2)]
    vd_b = [msb("vd_b0", (128, H * DV), BF16)]
    u_b = msb("u_b", (128, H * DV), BF16)
    gm_b = msb("gm_b", (128, H * 128), BF16)
    m_f = [msb(f"m_f{i}", (128, H * 128), F32) for i in range(2)]
    mt_f = [msb(f"mt_f{i}", (128, H * 128), F32) for i in range(2)]
    x2_f = [msb(f"x2_f{i}", (128, H * 128), F32) for i in range(2)]
    rt_f = [msb(f"rt_f{i}", (128, H * 128), F32) for i in range(2)]
    rt_b = [msb(f"rt_b{i}", (128, H * 128), BF16) for i in range(2)]
    o_nw = [msb(f"o_nw{i}", (128, H * DV), BF16) for i in range(2)]

    s_x = sems("s_x")
    s_wm = sems("s_wm")
    s_ld = sems("s_ld")
    s_bc = sems("s_bc")
    s_cast = sems("s_cast")
    s_stat = sems("s_stat")
    s_nrm = sems("s_nrm")
    s_trn = sems("s_trn")
    s_trnv = sems("s_trnv")
    s_pqk = sems("s_pqk")
    s_sqk = sems("s_sqk")
    s_l2 = sems("s_l2")
    s_tqk = sems("s_tqk")
    s_eqk = sems("s_eqk")
    s_pv = sems("s_pv")
    s_sv = sems("s_sv")
    s_bet = sems("s_bet")
    s_kv = sems("s_kv")
    s_vd = sems("s_vd")
    s_A = sems("s_A")
    s_M = sems("s_M")
    s_Mt = sems("s_Mt")
    s_it = sems("s_it")
    s_iv = sems("s_iv")
    s_U = sems("s_U")
    s_Ue = sems("s_Ue")
    s_G = sems("s_G")
    s_Gm = sems("s_Gm")
    s_O = sems("s_O")
    s_oe = sems("s_oe")
    s_ow = [sems("s_ow0"), sems("s_ow1")]
    s_xw = [sems("s_xw0"), sems("s_xw1")]
    s_to = sems("s_to")
    s_toe = sems("s_toe")
    s_wo = sems("s_wo")
    s_s = sems("s_s")
    s_sw = [sems("s_sw0"), sems("s_sw1")]
    s_su = sems("s_su")
    s_dbg = sems("s_dbg")
    s_dbg2 = sems("s_dbg2")

    done = {k: {} for k in ("nrm", "trnv", "l2", "eqk", "su", "oe", "sv", "bet",
                            "sqk", "kl2", "ktr", "A", "Mt", "vd", "ow", "xw",
                            "G", "O", "stat")}

    def cast_load(dst, src, cdim):
        view = src.rearrange("(c p) n -> p c n", p=128)
        p.op("gpsimd", lambda e: e.dma_start(
            out=dst[:].rearrange("p (c n) -> p c n", c=cdim), in_=view),
            inc=s_wm, inc_k=16)

    cast_load(wq_sb, Wq, 8)
    cast_load(wk_sb, Wk, 8)
    cast_load(wv_sb, Wv, 8)
    cast_load(wb_sb, Wb, 8)
    wm_done = s_wm.n

    p.op("sync", lambda e: e.dma_start(out=lnrow[0:1, 0:D], in_=ln_w), inc=s_ld, inc_k=16)
    p.op("sync", lambda e: e.dma_start(out=lnrow[0:1, D:2 * D], in_=ln_b),
         inc=s_ld, inc_k=16)
    p.op("sync", lambda e: e.dma_start(out=lnrow[0:1, 2 * D:2 * D + DV], in_=onw),
         inc=s_ld, inc_k=16)

    p.wait("tensor", s_ld, s_ld.n)
    p.wait("tensor", s_init)
    for dst, src0, ng in ((lnw_bc, 0, 2), (lnb_bc, D, 2), (onw_bc, 2 * D, 1)):
        for g in range(ng):
            p.op("tensor", lambda e, g=g, src0=src0: e.matmul(
                PB[0][:, 0:512], ones_row[:],
                lnrow[0:1, src0 + g * 512:src0 + (g + 1) * 512],
                start=True, stop=True), inc=s_bc)
            p.wait("vector", s_bc)
            p.op("vector", lambda e, g=g, dst=dst: e.tensor_copy(
                dst[:, g * 512:(g + 1) * 512], PB[0][:, 0:512]), inc=s_bc)
            p.wait("tensor", s_bc)
    bc_done = s_bc.n

    s_z = sems("s_z")
    for h in range(H):
        p.op("vector", lambda e, h=h: e.memset(S_b[h][:], 0.0),
             inc=(s_z if h == H - 1 else None))

    for ei in range(NE):
        p.op("gpsimd", lambda e, ei=ei: e.dma_start(
            out=wbf_up[ei].rearrange("(a b) -> a b", a=II),
            in_=upw[ei].rearrange("d i -> (d i)").rearrange("(a b) -> a b", a=II)),
            inc=s_cast, inc_k=16)
    p.op("gpsimd", lambda e: e.dma_start(
        out=wbf_su[0].rearrange("(a b) -> a b", a=II),
        in_=sw_up.rearrange("d i -> (d i)").rearrange("(a b) -> a b", a=II)),
        inc=s_cast, inc_k=16)
    for ei in range(NE):
        for dt in range(8):
            p.op("gpsimd", lambda e, ei=ei, dt=dt: e.dma_start(
                out=wbf_dn[ei][dt].rearrange("(q c j) -> q c j", q=128, c=32),
                in_=dnw[ei].rearrange("(c q) (dt j) -> dt q c j", q=128, j=128)[dt]),
                inc=s_cast, inc_k=16)
    for dt in range(8):
        p.op("gpsimd", lambda e, dt=dt: e.dma_start(
            out=wbf_sd[0][dt].rearrange("(q c j) -> q c j", q=128, c=32),
            in_=sw_dn.rearrange("(c q) (dt j) -> dt q c j", q=128, j=128)[dt]),
            inc=s_cast, inc_k=16)
    cast_done = s_cast.n

    def front(c):
        pe = c % 2
        if c >= 2:
            p.wait("sync", s_nrm, done["nrm"][c - 2])
        p.op("sync", lambda e, c=c, pe=pe: e.dma_start(
            out=xch[pe][:], in_=x_full[c * 128:(c + 1) * 128, :]), inc=s_x, inc_k=16)
        xld = s_x.n

        p.wait("scalar", s_x, xld)
        if c == 0:
            p.wait("scalar", s_init)
        if c >= 1:
            p.wait("scalar", s_l2, done["kl2"][c - 1])
        p.op("scalar", lambda e, pe=pe: e.activation(sqch[:], xch[pe][:], AF.Square),
             inc=s_stat)
        p.wait("vector", s_x, xld)
        p.op("vector", lambda e, pe=pe: e.reduce_sum(stat[:, 0:1], xch[pe][:], axis=AX.X))
        p.wait("vector", s_stat, s_stat.n)
        p.op("vector", lambda e: e.reduce_sum(stat[:, 1:2], sqch[:], axis=AX.X),
             inc=s_stat)
        p.wait("vector", s_stat, s_stat.n)
        p.op("vector", lambda e: e.tensor_scalar(out=stat[:, 2:3], in0=stat[:, 0:1],
                                                 scalar1=1.0 / D, scalar2=None,
                                                 op0=ALU.mult), inc=s_stat)
        p.wait("vector", s_stat, s_stat.n)
        p.op("vector", lambda e: e.tensor_tensor(stat[:, 3:4], stat[:, 2:3], stat[:, 2:3],
                                                 op=ALU.mult), inc=s_stat)
        p.wait("vector", s_stat, s_stat.n)
        p.op("vector", lambda e: e.scalar_tensor_tensor(
            out=stat[:, 4:5], in0=stat[:, 1:2], scalar=1.0 / D, in1=stat[:, 3:4],
            op0=ALU.mult, op1=ALU.subtract), inc=s_stat)
        p.wait("vector", s_stat, s_stat.n)
        p.op("vector", lambda e: e.tensor_scalar(out=stat[:, 5:6], in0=stat[:, 4:5],
                                                 scalar1=1e-5, scalar2=None, op0=ALU.add))
        p.op("vector", lambda e: e.tensor_scalar(out=stat[:, 6:7], in0=stat[:, 4:5],
                                                 scalar1=1.00001, scalar2=1e-10,
                                                 op0=ALU.mult, op1=ALU.add), inc=s_stat)
        p.wait("vector", s_stat, s_stat.n)
        _rsqrt(p, stat, 5, 2, nsc, cfg.newton)
        p.op("vector", lambda e: e.tensor_tensor(stat[:, 8:9], stat[:, 2:3], stat[:, 5:6],
                                                 op=ALU.mult))
        p.op("vector", lambda e: e.tensor_tensor(stat[:, 9:10], stat[:, 2:3], stat[:, 6:7],
                                                 op=ALU.mult), inc=s_stat)
        p.wait("vector", s_stat, s_stat.n)
        done["stat"][c] = s_stat.n

        if c >= 2:
            p.wait("vector", s_xw[pe], done["xw"][c - 2])
        if dbg:
            p.wait("vector", s_dbg, s_dbg.n)
        p.op("vector", lambda e, pe=pe: e.tensor_scalar(
            out=xnch[pe][:], in0=xch[pe][:], scalar1=stat[:, 5:6], scalar2=stat[:, 8:9],
            op0=ALU.mult, op1=ALU.subtract))
        p.op("vector", lambda e, pe=pe: e.tensor_scalar(
            out=nrm0[:], in0=xch[pe][:], scalar1=stat[:, 6:7], scalar2=stat[:, 9:10],
            op0=ALU.mult, op1=ALU.subtract), inc=s_nrm)
        p.wait("vector", s_nrm, s_nrm.n)
        if c == 0:
            p.wait("vector", s_bc, bc_done)
        p.op("vector", lambda e: e.tensor_tensor(nrm0[:], nrm0[:], lnw_bc[:], op=ALU.mult),
             inc=s_nrm)
        p.wait("vector", s_nrm, s_nrm.n)
        if c >= 2:
            p.wait("vector", s_trn, done["trnv"][c - 2])
        p.op("vector", lambda e, pe=pe: e.tensor_tensor(nrmb[pe][:], nrm0[:], lnb_bc[:],
                                                        op=ALU.add), inc=s_nrm)
        done["nrm"][c] = s_nrm.n
        p.wait("sync", s_nrm, done["nrm"][c])
        p.op("sync", lambda e, c=c, pe=pe: e.dma_start(
            out=xn_scr[c * 128:(c + 1) * 128, :], in_=xnch[pe][:]), inc=s_xw[pe],
            inc_k=16)
        done["xw"][c] = s_xw[pe].n
        if "dbg_xn" in dbg:
            p.op("sync", lambda e, c=c, pe=pe: e.dma_start(
                out=dbg["dbg_xn"][c * 128:(c + 1) * 128, :], in_=xnch[pe][:]),
                inc=s_dbg, inc_k=16)

        if c == 0:
            p.wait("tensor", s_wm, wm_done)
        if c >= 2:
            p.wait("tensor", s_trnv, done["trnv"][c - 2])
        p.wait("tensor", s_nrm, done["nrm"][c])
        for g in range(8):
            p.op("tensor", lambda e, g=g, pe=pe: e.transpose(
                PT[:, g * 128:(g + 1) * 128], nrmb[pe][:, g * 128:(g + 1) * 128],
                ident_b[:]), inc=(s_trn if g == 7 else None))
        p.wait("vector", s_trn, s_trn.n)
        p.op("vector", lambda e, pe=pe: e.tensor_copy(nT[pe][:], PT[:]), inc=s_trnv)
        done["trnv"][c] = s_trnv.n

        p.wait("tensor", s_trnv, done["trnv"][c])
        if c >= 2:
            p.wait("tensor", s_oe, done["oe"][c - 2])
        pqk_base = s_pqk.n
        for g in range(2):
            for cc in range(8):
                p.op("tensor", lambda e, g=g, cc=cc, pe=pe: e.matmul(
                    PB[2 + g][:, 0:512], nT[pe][:, cc * 128:(cc + 1) * 128],
                    wk_sb[:, cc * 1024 + g * 512:cc * 1024 + (g + 1) * 512],
                    start=(cc == 0), stop=(cc == 7)), inc=(s_pqk if cc == 7 else None))
        if c >= 1:
            p.wait("scalar", s_l2, done["l2"][c - 1])
        for g in range(2):
            p.wait("scalar", s_pqk, pqk_base + g + 1)
            p.op("scalar", lambda e, g=g: e.activation(
                kraw[0][:, g * 512:(g + 1) * 512], PB[2 + g][:, 0:512], AF.Silu),
                inc=(s_sqk if g == 1 else None))
        p.wait("scalar", s_sqk, s_sqk.n)
        p.op("scalar", lambda e: e.activation(nrm0[:], kraw[0][:], AF.Square), inc=s_sqk)
        done["sqk"][c] = s_sqk.n
        p.wait("vector", s_sqk, s_sqk.n)
        for h in range(H):
            p.op("vector", lambda e, h=h: e.reduce_sum(
                stat[:, 14 + h:15 + h], nrm0[:, h * DK:(h + 1) * DK], axis=AX.X),
                inc=(s_l2 if h == H - 1 else None))
        p.wait("vector", s_l2, s_l2.n)
        p.op("vector", lambda e: e.tensor_scalar(out=stat[:, 14:18], in0=stat[:, 14:18],
                                                 scalar1=1e-6, scalar2=None, op0=ALU.add),
             inc=s_l2)
        p.wait("vector", s_l2, s_l2.n)
        _rsqrt(p, stat, 14, 4, nsc, cfg.newton)
        if c >= 2:
            p.wait("vector", s_su, done["su"][c - 2])
        if dbg:
            p.wait("vector", s_dbg2, s_dbg2.n)
        for h in range(H):
            p.op("vector", lambda e, h=h, pe=pe: e.tensor_scalar(
                out=kn[pe][:, h * DK:(h + 1) * DK], in0=kraw[0][:, h * DK:(h + 1) * DK],
                scalar1=stat[:, 14 + h:15 + h], scalar2=None, op0=ALU.mult),
                inc=(s_l2 if h == H - 1 else None))
        done["kl2"][c] = s_l2.n
        p.wait("tensor", s_l2, done["kl2"][c])
        if c >= 2:
            p.wait("tensor", s_G, done["G"][c - 2])
        for g8 in range(8):
            p.op("tensor", lambda e, g8=g8, pe=pe: e.transpose(
                PT[:, g8 * 128:(g8 + 1) * 128], kn[pe][:, g8 * 128:(g8 + 1) * 128],
                ident_b[:]), inc=(s_tqk if g8 == 7 else None))
        p.wait("vector", s_tqk, s_tqk.n)
        p.op("vector", lambda e, pe=pe: e.tensor_copy(kT[pe][:], PT[:]), inc=s_eqk)
        done["ktr"][c] = s_eqk.n
        p.wait("tensor", s_eqk, done["ktr"][c])
        p.wait("tensor", s_iv, s_iv.n)
        for h in range(H):
            for g in range(2):
                p.op("tensor", lambda e, h=h, g=g, pe=pe: e.matmul(
                    PI[:, h * 128:(h + 1) * 128],
                    kT[pe][:, (h * 2 + g) * 128:(h * 2 + g + 1) * 128],
                    kT[pe][:, (h * 2 + g) * 128:(h * 2 + g + 1) * 128],
                    start=(g == 0), stop=(g == 1)), inc=(s_A if g == 1 else None))
        done["A"][c] = s_A.n
        s_pb_base = s_pv.n
        if c >= 2:
            p.wait("vector", s_vd, done["vd"][c - 2])
        for cc in range(8):
            p.op("tensor", lambda e, cc=cc, pe=pe: e.matmul(
                PI[:, 1024:1024 + H], nT[pe][:, cc * 128:(cc + 1) * 128],
                wb_sb[:, cc * H:(cc + 1) * H], start=(cc == 0), stop=(cc == 7)),
                inc=(s_pv if cc == 7 else None))
        p.wait("scalar", s_pv, s_pb_base + 1)
        p.op("scalar", lambda e, pe=pe: e.activation(beta_t[pe][:, 0:H],
                                                     PI[:, 1024:1024 + H],
                                                     AF.Tanh, scale=0.5), inc=s_bet)
        p.wait("vector", s_bet, s_bet.n)
        p.op("vector", lambda e, pe=pe: e.tensor_scalar(
            out=beta_t[pe][:, 0:H], in0=beta_t[pe][:, 0:H], scalar1=0.5, scalar2=0.5,
            op0=ALU.mult, op1=ALU.add), inc=s_bet)
        p.wait("vector", s_bet, s_bet.n)
        p.op("vector", lambda e, pe=pe: e.tensor_scalar(
            out=beta_t[pe][:, H:2 * H], in0=beta_t[pe][:, 0:H], scalar1=-1.0,
            scalar2=None, op0=ALU.mult), inc=s_bet)
        done["bet"][c] = s_bet.n

        for g in range(2):
            for cc in range(8):
                p.op("tensor", lambda e, g=g, cc=cc, pe=pe: e.matmul(
                    PB[g][:, 0:512], nT[pe][:, cc * 128:(cc + 1) * 128],
                    wq_sb[:, cc * 1024 + g * 512:cc * 1024 + (g + 1) * 512],
                    start=(cc == 0), stop=(cc == 7)), inc=(s_pqk if cc == 7 else None))
        for g in range(2):
            p.wait("scalar", s_pqk, pqk_base + 2 + g + 1)
            p.op("scalar", lambda e, g=g: e.activation(
                qraw[0][:, g * 512:(g + 1) * 512], PB[g][:, 0:512], AF.Silu),
                inc=(s_sqk if g == 1 else None))
        p.wait("scalar", s_sqk, s_sqk.n)
        p.op("scalar", lambda e: e.activation(sqch[:], qraw[0][:], AF.Square), inc=s_sqk)
        p.wait("vector", s_sqk, s_sqk.n)
        for h in range(H):
            p.op("vector", lambda e, h=h: e.reduce_sum(
                stat[:, 10 + h:11 + h], sqch[:, h * DK:(h + 1) * DK], axis=AX.X),
                inc=(s_l2 if h == H - 1 else None))
        p.wait("vector", s_l2, s_l2.n)
        p.op("vector", lambda e: e.tensor_scalar(out=stat[:, 10:14], in0=stat[:, 10:14],
                                                 scalar1=1e-6, scalar2=None, op0=ALU.add),
             inc=s_l2)
        p.wait("vector", s_l2, s_l2.n)
        _rsqrt(p, stat, 10, 4, nsc, cfg.newton)
        p.op("vector", lambda e: e.tensor_scalar(out=stat[:, 10:14], in0=stat[:, 10:14],
                                                 scalar1=DK ** -0.5, scalar2=None,
                                                 op0=ALU.mult), inc=s_l2)
        p.wait("vector", s_l2, s_l2.n)
        if c >= 2:
            p.wait("vector", s_O, done["O"][c - 2])
        for h in range(H):
            p.op("vector", lambda e, h=h, pe=pe: e.tensor_scalar(
                out=qn[pe][:, h * DK:(h + 1) * DK], in0=qraw[0][:, h * DK:(h + 1) * DK],
                scalar1=stat[:, 10 + h:11 + h], scalar2=None, op0=ALU.mult),
                inc=(s_l2 if h == H - 1 else None))
        done["l2"][c] = s_l2.n
        if "dbg_q" in dbg:
            p.wait("gpsimd", s_l2, done["l2"][c])
            p.op("gpsimd", lambda e, c=c, pe=pe: e.dma_start(
                out=dbg["dbg_q"][c * 128:(c + 1) * 128, :], in_=qn[pe][:]),
                inc=s_dbg2, inc_k=16)
            p.op("gpsimd", lambda e, c=c, pe=pe: e.dma_start(
                out=dbg["dbg_k"][c * 128:(c + 1) * 128, :], in_=kn[pe][:]),
                inc=s_dbg2, inc_k=16)
        p.wait("vector", s_A, done["A"][c])
        p.wait("vector", s_bet, done["bet"][c])
        p.wait("vector", s_it, s_it.n)
        for h in range(H):
            p.op("vector", lambda e, h=h, pe=pe: e.scalar_tensor_tensor(
                out=m_f[pe][:, h * 128:(h + 1) * 128], in0=PI[:, h * 128:(h + 1) * 128],
                scalar=beta_t[pe][:, H + h:H + h + 1], in1=tril01[:],
                op0=ALU.mult, op1=ALU.mult), inc=(s_M if h == H - 1 else None))
        p.wait("tensor", s_M, s_M.n)
        for h in range(H):
            p.op("tensor", lambda e, h=h, pe=pe: e.transpose(
                PI[:, 512 + h * 128:512 + (h + 1) * 128],
                m_f[pe][:, h * 128:(h + 1) * 128],
                ident_f[:]), inc=(s_Mt if h == H - 1 else None))
        p.wait("vector", s_Mt, s_Mt.n)
        p.op("vector", lambda e, pe=pe: e.tensor_copy(mt_f[pe][:], PI[:, 512:1024]),
             inc=s_Mt)
        p.wait("vector", s_Mt, s_Mt.n)
        for h in range(H):
            p.op("vector", lambda e, h=h, pe=pe: e.tensor_tensor(
                rt_f[pe][:, h * 128:(h + 1) * 128], mt_f[pe][:, h * 128:(h + 1) * 128],
                ident_f[:], op=ALU.add), inc=(s_Mt if h == H - 1 else None))
        done["Mt"][c] = s_Mt.n

        p.wait("tensor", s_l2, done["l2"][c])
        for g8 in range(8):
            p.op("tensor", lambda e, g8=g8, pe=pe: e.transpose(
                PT[:, g8 * 128:(g8 + 1) * 128], qn[pe][:, g8 * 128:(g8 + 1) * 128],
                ident_b[:]), inc=(s_tqk if g8 == 7 else None))
        p.wait("vector", s_tqk, s_tqk.n)
        p.op("vector", lambda e, pe=pe: e.tensor_copy(qT[pe][:], PT[:]), inc=s_eqk)
        done["eqk"][c] = s_eqk.n

        p.wait("tensor", s_sqk, done["sqk"][c])
        if c >= 2:
            p.wait("tensor", s_vd, done["vd"][c - 2])
        pv_base = s_pv.n
        for g in range(4):
            for cc in range(8):
                p.op("tensor", lambda e, g=g, cc=cc, pe=pe: e.matmul(
                    PB[g][:, 0:512], nT[pe][:, cc * 128:(cc + 1) * 128],
                    wv_sb[:, cc * 2048 + g * 512:cc * 2048 + (g + 1) * 512],
                    start=(cc == 0), stop=(cc == 7)), inc=(s_pv if cc == 7 else None))
        if c >= 2:
            p.wait("scalar", s_vd, done["vd"][c - 2])
        if dbg:
            p.wait("scalar", s_dbg, s_dbg.n)
        for g in range(4):
            p.wait("scalar", s_pv, pv_base + g + 1)
            p.op("scalar", lambda e, g=g, pe=pe: e.activation(
                vsil[pe][:, g * 512:(g + 1) * 512], PB[g][:, 0:512], AF.Silu),
                inc=(s_sv if g == 3 else None))
        done["sv"][c] = s_sv.n

        if "dbg_v" in dbg:
            p.wait("sync", s_sv, done["sv"][c])
            p.op("sync", lambda e, c=c, pe=pe: e.dma_start(
                out=dbg["dbg_v"][c * 128:(c + 1) * 128, :], in_=vsil[pe][:]),
                inc=s_dbg, inc_k=16)
        if "dbg_beta" in dbg:
            p.wait("sync", s_bet, done["bet"][c])
            p.op("sync", lambda e, c=c, pe=pe: e.dma_start(
                out=dbg["dbg_beta"][c * 128:(c + 1) * 128, :], in_=beta_t[pe][:, 0:H]),
                inc=s_dbg, inc_k=16)

    def inv_level(cn, lev):
        pn = cn % 2
        last = lev == 6
        if lev == 1:
            p.wait("tensor", s_bet, s_bet.n)
            p.wait("tensor", s_Mt, s_Mt.n)
        for h in range(H):
            p.op("tensor", lambda e, h=h, pn=pn: e.matmul(
                PI[:, h * 128:(h + 1) * 128], mt_f[pn][:, h * 128:(h + 1) * 128],
                m_f[pn][:, h * 128:(h + 1) * 128], start=True, stop=True),
                inc=(s_it if (last and h == H - 1) else None))
        if not last:
            for h in range(H):
                p.op("tensor", lambda e, h=h, pn=pn: e.matmul(
                    PI[:, 512 + h * 128:512 + (h + 1) * 128],
                    m_f[pn][:, h * 128:(h + 1) * 128],
                    mt_f[pn][:, h * 128:(h + 1) * 128],
                    start=True, stop=True), inc=(s_it if h == H - 1 else None))
        p.wait("vector", s_it, s_it.n)
        p.op("vector", lambda e, pn=pn: e.tensor_copy(x2_f[pn][:], PI[:, 0:512]),
             inc=s_iv)
        if not last:
            p.op("vector", lambda e, pn=pn: e.tensor_copy(mt_f[pn][:], PI[:, 512:1024]),
                 inc=s_iv)
        p.wait("tensor", s_iv, s_iv.n)
        for h in range(H):
            p.op("tensor", lambda e, h=h, pn=pn: e.matmul(
                PI[:, 1024 + h * 128:1024 + (h + 1) * 128],
                x2_f[pn][:, h * 128:(h + 1) * 128], rt_f[pn][:, h * 128:(h + 1) * 128],
                start=True, stop=True), inc=(s_it if h == H - 1 else None))
        p.wait("vector", s_it, s_it.n)
        for h in range(H):
            p.op("vector", lambda e, h=h, pn=pn: e.tensor_tensor(
                rt_f[pn][:, h * 128:(h + 1) * 128], rt_f[pn][:, h * 128:(h + 1) * 128],
                PI[:, 1024 + h * 128:1024 + (h + 1) * 128], op=ALU.add),
                inc=(s_iv if h == H - 1 else None))
        if not last:
            p.op("vector", lambda e, pn=pn: e.tensor_copy(m_f[pn][:], x2_f[pn][:]),
                 inc=s_iv)
        p.wait("tensor", s_iv, s_iv.n)
        if last:
            p.wait("vector", s_iv, s_iv.n)
            p.wait("vector", s_U, s_U.n)
            p.op("vector", lambda e, pn=pn: e.tensor_copy(rt_b[pn][:], rt_f[pn][:]),
                 inc=s_iv)
            inv_done[cn] = s_iv.n

    def rec(c):
        pe = c % 2
        cn = c + 1 if c + 1 < NCH else None
        p.wait("tensor", s_sv, s_sv.n)
        p.wait("tensor", s_eqk, done["eqk"][c])
        if c >= 1:
            p.wait("tensor", s_su, done["su"][c - 1])
        for h in range(H):
            for g in range(2):
                p.op("tensor", lambda e, h=h, g=g, pe=pe: e.matmul(
                    PB[h][:, 0:512], kT[pe][:, (h * 2 + g) * 128:(h * 2 + g + 1) * 128],
                    S_b[h][:, g * 512:(g + 1) * 512], start=(g == 0), stop=(g == 1)),
                    inc=(s_kv if g == 1 else None))
        kv_now = s_kv.n
        p.wait("vector", s_kv, kv_now)
        p.wait("vector", s_sv, done["sv"][c])
        for h in range(H):
            p.op("vector", lambda e, h=h, pe=pe: e.tensor_tensor(
                osq[:, (h % 2) * DV:((h % 2) + 1) * DV],
                vsil[pe][:, h * DV:(h + 1) * DV],
                PB[h][:, 0:512], op=ALU.subtract), inc=s_vd)
            if h == 0 and c >= 1:
                p.wait("vector", s_U, u_done_prev[0])
            p.wait("vector", s_vd, s_vd.n)
            p.op("vector", lambda e, h=h, pe=pe: e.tensor_scalar(
                out=vd_b[0][:, h * DV:(h + 1) * DV],
                in0=osq[:, (h % 2) * DV:((h % 2) + 1) * DV],
                scalar1=beta_t[pe][:, h:h + 1], scalar2=None, op0=ALU.mult),
                inc=(s_vd if h == H - 1 else None))
        done["vd"][c] = s_vd.n
        if cn is not None:
            inv_level(cn, 1)
        p.wait("tensor", s_vd, done["vd"][c])
        p.wait("tensor", s_iv, inv_done[c])
        for h in range(H):
            p.op("tensor", lambda e, h=h, pe=pe: e.matmul(
                PB[h][:, 0:512], rt_b[pe][:, h * 128:(h + 1) * 128],
                vd_b[0][:, h * DV:(h + 1) * DV], start=True, stop=True),
                inc=(s_U if h == H - 1 else None))
        u_done_prev[0] = s_U.n
        p.wait("vector", s_U, s_U.n)
        for h in range(H):
            p.op("vector", lambda e, h=h: e.tensor_copy(
                u_b[:, h * DV:(h + 1) * DV], PB[h][:, 0:512]),
                inc=(s_Ue if h == H - 1 else None))
        if cn is not None:
            inv_level(cn, 2)
        p.wait("tensor", s_Ue, s_Ue.n)
        for h in range(H):
            for g in range(2):
                p.op("tensor", lambda e, h=h, g=g, pe=pe: e.matmul(
                    PB[(h % 2) * 2 + g][:, 0:512],
                    kn[pe][:, h * DK + g * 128:h * DK + (g + 1) * 128],
                    u_b[:, h * DV:(h + 1) * DV], start=True, stop=True),
                    inc=(s_kv if g == 1 else None))
            p.wait("vector", s_kv, s_kv.n)
            for g in range(2):
                p.op("vector", lambda e, h=h, g=g: e.tensor_tensor(
                    S_b[h][:, g * 512:(g + 1) * 512], S_b[h][:, g * 512:(g + 1) * 512],
                    PB[(h % 2) * 2 + g][:, 0:512], op=ALU.add),
                    inc=(s_su if g == 1 else None))
            if h % 2 == 1 and h < H - 1:
                p.wait("tensor", s_su, s_su.n)
        done["su"][c] = s_su.n
        if cn is not None:
            inv_level(cn, 3)
        p.wait("tensor", s_su, done["su"][c])
        for h in range(H):
            for g in range(2):
                p.op("tensor", lambda e, h=h, g=g, pe=pe: e.matmul(
                    PB[h][:, 0:128],
                    kT[pe][:, (h * 2 + g) * 128:(h * 2 + g + 1) * 128],
                    qT[pe][:, (h * 2 + g) * 128:(h * 2 + g + 1) * 128],
                    start=(g == 0), stop=(g == 1)), inc=(s_G if g == 1 else None))
        done["G"][c] = s_G.n
        p.wait("vector", s_G, s_G.n)
        for h in range(H):
            p.op("vector", lambda e, h=h: e.scalar_tensor_tensor(
                out=gm_b[:, h * 128:(h + 1) * 128], in0=PB[h][:, 0:128],
                scalar=-1.0, in1=tril01[:], op0=ALU.mult, op1=ALU.mult),
                inc=(s_Gm if h == H - 1 else None))
        if cn is not None:
            inv_level(cn, 4)
        p.wait("tensor", s_Gm, s_Gm.n)
        for h in range(H):
            for g in range(2):
                p.op("tensor", lambda e, h=h, g=g, pe=pe: e.matmul(
                    PB[h][:, 0:512], qT[pe][:, (h * 2 + g) * 128:(h * 2 + g + 1) * 128],
                    S_b[h][:, g * 512:(g + 1) * 512], start=(g == 0), stop=False))
            p.op("tensor", lambda e, h=h: e.matmul(
                PB[h][:, 0:512], gm_b[:, h * 128:(h + 1) * 128],
                u_b[:, h * DV:(h + 1) * DV], start=False, stop=True),
                inc=(s_O if h == H - 1 else None))
        done["O"][c] = s_O.n
        if cn is not None:
            inv_level(cn, 5)
        p.wait("scalar", s_O, done["O"][c])
        for h in range(H):
            p.op("scalar", lambda e, h=h: e.activation(
                osq[:, (h % 2) * DV:((h % 2) + 1) * DV], PB[h][:, 0:512], AF.Square),
                inc=s_oe)
            p.wait("vector", s_oe, s_oe.n)
            p.op("vector", lambda e, h=h: e.reduce_sum(
                stat[:, 18 + h:19 + h], osq[:, (h % 2) * DV:((h % 2) + 1) * DV],
                axis=AX.X), inc=s_oe)
            if h < H - 1:
                p.wait("scalar", s_oe, s_oe.n)
        p.wait("vector", s_oe, s_oe.n)
        p.op("vector", lambda e: e.tensor_scalar(out=stat[:, 18:22], in0=stat[:, 18:22],
                                                 scalar1=1.0 / DV, scalar2=1e-5,
                                                 op0=ALU.mult, op1=ALU.add), inc=s_oe)
        p.wait("vector", s_oe, s_oe.n)
        _rsqrt(p, stat, 18, 4, nsc, cfg.newton)
        if c >= 2:
            p.wait("vector", s_ow[pe], done["ow"][c - 2])
        if dbg:
            p.wait("vector", s_dbg2, s_dbg2.n)
        for h in range(H):
            p.op("vector", lambda e, h=h, pe=pe: e.scalar_tensor_tensor(
                out=o_nw[pe][:, h * DV:(h + 1) * DV], in0=PB[h][:, 0:512],
                scalar=stat[:, 18 + h:19 + h], in1=onw_bc[:],
                op0=ALU.mult, op1=ALU.mult), inc=(s_oe if h == H - 1 else None))
        done["oe"][c] = s_oe.n
        p.wait("sync", s_oe, done["oe"][c])
        p.op("sync", lambda e, c=c, pe=pe: e.dma_start(
            out=on_scr[c * 128:(c + 1) * 128, :], in_=o_nw[pe][:]), inc=s_ow[pe],
            inc_k=16)
        done["ow"][c] = s_ow[pe].n
        if "dbg_o" in dbg:
            p.wait("gpsimd", s_oe, done["oe"][c])
            p.op("gpsimd", lambda e, c=c, pe=pe: e.dma_start(
                out=dbg["dbg_o"][c * 128:(c + 1) * 128, :], in_=o_nw[pe][:]),
                inc=s_dbg2, inc_k=16)
        if cn is not None:
            inv_level(cn, 6)

    u_done_prev = [0]
    inv_done = {}
    front(0)
    for c in range(NCH):
        if c + 1 < NCH:
            front(c + 1)
        if c == 0:
            for lev in range(1, 7):
                inv_level(0, lev)
        rec(c)

    for eng in ("sync", "gpsimd"):
        p.wait(eng, s_oe, s_oe.n)
        p.wait(eng, s_su, s_su.n)
        p.wait(eng, s_O, s_O.n)
        p.wait(eng, s_iv, s_iv.n)
    p.wait("sync", s_x, s_x.n)
    mes.close()
    spes = ExitStack()

    def ssb(name, shape, dtype):
        return spes.enter_context(nc.sbuf_tensor(name, list(shape), dtype))

    wo_sb = ssb("wo_sb", (128, 16 * 1024), BF16)
    o_in = [ssb(f"o_in{i}", (128, H * DV), BF16) for i in range(2)]
    xn_in = [ssb(f"xn_in{i}", (128, D), F32) for i in range(2)]
    oT_b = [ssb(f"oT_b{i}", (128, 2048), BF16) for i in range(2)]
    s_out_b = [ssb(f"s_out_b{i}", (128, D), F32) for i in range(2)]

    s_oin = [sems("s_oin0"), sems("s_oin1")]
    s_xin = [sems("s_xin0"), sems("s_xin1")]
    p.op("gpsimd", lambda e: e.dma_start(
        out=wo_sb[:].rearrange("p (c n) -> p c n", c=16),
        in_=Wo.rearrange("(c p) n -> p c n", p=128)), inc=s_wm, inc_k=16)
    p.wait("tensor", s_wm, s_wm.n)
    done_sp = {}
    for c in range(NCH):
        pe = c % 2
        if c >= 2:
            p.wait("sync", s_toe, done_sp[c - 2][0])
            p.wait("sync", s_s, done_sp[c - 2][1])
            p.wait("sync", s_sw[pe], done_sp[c - 2][2])
        p.wait("sync", s_ow[pe], done["ow"][c])
        p.wait("sync", s_xw[pe], done["xw"][c])
        p.op("sync", lambda e, c=c, pe=pe: e.dma_start(
            out=o_in[pe][:], in_=on_scr[c * 128:(c + 1) * 128, :]),
            inc=s_oin[pe], inc_k=16)
        p.op("sync", lambda e, c=c, pe=pe: e.dma_start(
            out=xn_in[pe][:], in_=xn_scr[c * 128:(c + 1) * 128, :]),
            inc=s_xin[pe], inc_k=16)
        oin_d = s_oin[pe].n
        xin_d = s_xin[pe].n
        p.wait("tensor", s_oin[pe], oin_d)
        for grp in range(2):
            if not (c == 0 and grp == 0):
                p.wait("tensor", s_toe, s_toe.n)
            for j in range(8):
                p.op("tensor", lambda e, grp=grp, j=j, pe=pe: e.transpose(
                    PT[:, j * 128:(j + 1) * 128],
                    o_in[pe][:, (grp * 8 + j) * 128:(grp * 8 + j + 1) * 128],
                    ident_b[:]), inc=(s_to if j == 7 else None))
            p.wait("vector", s_to, s_to.n)
            p.op("vector", lambda e, grp=grp, pe=pe: e.tensor_copy(
                oT_b[pe][:, grp * 1024:(grp + 1) * 1024], PT[:]), inc=s_toe)
        toe_d = s_toe.n
        p.wait("tensor", s_toe, toe_d)
        if c >= 1:
            p.wait("tensor", s_s, s_s.n)
        for g in range(2):
            for cc in range(16):
                p.op("tensor", lambda e, g=g, cc=cc, pe=pe: e.matmul(
                    PB[g][:, 0:512], oT_b[pe][:, cc * 128:(cc + 1) * 128],
                    wo_sb[:, cc * 1024 + g * 512:cc * 1024 + (g + 1) * 512],
                    start=(cc == 0), stop=(cc == 15)), inc=(s_wo if cc == 15 else None))
        p.wait("vector", s_wo, s_wo.n)
        p.wait("vector", s_xin[pe], xin_d)
        if c >= 2:
            p.wait("vector", s_sw[pe], done_sp[c - 2][2])
        if "dbg_s" in dbg:
            p.wait("vector", s_dbg, s_dbg.n)
        for g in range(2):
            p.op("vector", lambda e, g=g, pe=pe: e.tensor_tensor(
                s_out_b[pe][:, g * 512:(g + 1) * 512],
                xn_in[pe][:, g * 512:(g + 1) * 512],
                PB[g][:, 0:512], op=ALU.add), inc=(s_s if g == 1 else None))
        p.wait("sync", s_s, s_s.n)
        p.op("sync", lambda e, c=c, pe=pe: e.dma_start(
            out=s_scr[c * 128:(c + 1) * 128, :], in_=s_out_b[pe][:]), inc=s_sw[pe],
            inc_k=16)
        done_sp[c] = (s_to.n, s_s.n, s_sw[pe].n)
        if "dbg_s" in dbg:
            p.op("sync", lambda e, c=c, pe=pe: e.dma_start(
                out=dbg["dbg_s"][c * 128:(c + 1) * 128, :], in_=s_out_b[pe][:]),
                inc=s_dbg, inc_k=16)
    spes.close()
    sw_all = (s_sw[0].n, s_sw[1].n)
    p.wait("sync", s_ow[0], s_ow[0].n)
    p.wait("sync", s_ow[1], s_ow[1].n)
    p.wait("sync", s_xw[0], s_xw[0].n)
    p.wait("sync", s_xw[1], s_xw[1].n)
    p.wait("sync", s_dbg, s_dbg.n)
    p.wait("gpsimd", s_wm, s_wm.n)
    p.wait("gpsimd", s_dbg2, s_dbg2.n)
    ph1_waits = [(s_s, s_s.n), (s_wo, s_wo.n), (s_toe, s_toe.n),
                 (s_sw[0], s_sw[0].n), (s_sw[1], s_sw[1].n),
                 (s_oin[0], s_oin[0].n), (s_oin[1], s_oin[1].n),
                 (s_xin[0], s_xin[0].n), (s_xin[1], s_xin[1].n)]

    oes = ExitStack()

    def osb(name, shape, dtype):
        return oes.enter_context(nc.sbuf_tensor(name, list(shape), dtype))

    xnT_st = osb("xnT_st", (128, 8 * STOK), BF16)
    comb = osb("comb", (128, 8 * STOK), BF16)
    hact = osb("hact", (128, 32 * STOK), BF16)
    ubuf = [osb(f"ubuf{i}", (128, 8 * 512), BF16) for i in range(2)]
    dbuf = [osb(f"dbuf{i}", (128, 32 * 128), BF16) for i in range(2)]
    gw_sb = osb("gw_sb", (128, 8 * E), F32)
    gb_bc = osb("gb_bc", (128, E), F32)
    upb_sb = osb("upb_sb", (128, NE * 32), F32)
    dnb_sb = osb("dnb_sb", (128, NE * 8), F32)
    sbu_sb = osb("sbu_sb", (128, 32), F32)
    sbd_sb = osb("sbd_sb", (128, 8), F32)
    xo_ch = [osb(f"xo_ch{i}", (128, D), F32) for i in range(2)]
    xn_mo = osb("xn_mo", (128, D), F32)
    xnb_mo = osb("xnb_mo", (128, D), BF16)
    sq_mo = osb("sq_mo", (128, D), F32)
    xnf_T = osb("xnf_T", (128, 8 * 128), F32)
    sc_tm = osb("sc_tm", (128, E), F32)
    w_tm = osb("w_tm", (128, E), F32)
    wT_all = osb("wT_all", (8, STOK), F32)
    rtm = osb("rtm", (128, STC), F32)
    rT_sb = osb("rT_sb", (16, 128), F32)
    wrow = osb("wrow", (1, STOK), F32)
    wbc = osb("wbc", (128, 1024), F32)
    tmp_ev = osb("tmp_ev", (128, 512), F32)
    mx8 = osb("mx8", (128, STOK), F32)
    rbc = osb("rbc", (128, STOK), F32)
    graw = osb("graw", (128, STOK), F32)
    zg = osb("zg", (128, 8 * STOK), BF16)
    zgT = osb("zgT", (128, D), BF16)
    sch = [osb("sch0", (128, D), F32)]

    s_mo = sems("s_mo")
    s_xo = [sems("s_xo0"), sems("s_xo1")]
    s_ln2 = sems("s_ln2")
    s_xb = sems("s_xb")
    s_tx = sems("s_tx")
    s_txe = sems("s_txe")
    s_gt = sems("s_gt")
    s_gs = sems("s_gs")
    s_gw = sems("s_gw")
    s_wup = [sems("s_wup0"), sems("s_wup1")]
    s_wdn = [sems("s_wdn0"), sems("s_wdn1")]
    s_mmu = sems("s_mmu")
    s_gel = sems("s_gel")
    s_mmd = sems("s_mmd")
    s_evm = sems("s_evm")
    s_eva = sems("s_eva")
    s_wb2 = sems("s_wb2")
    s_wrow = sems("s_wrow")
    s_n2 = sems("s_n2")
    s_n2g = sems("s_n2g")
    s_tz = sems("s_tz")
    s_zg = sems("s_zg")
    s_ep = sems("s_ep")
    s_epg = sems("s_epg")
    s_out = sems("s_out")
    s_cz = sems("s_cz")

    for eng in ENGINES:
        for sm, v in ph1_waits:
            p.wait(eng, sm, v)
    p.wait("sync", s_cast, cast_done)
    p.wait("scalar", s_cast, cast_done)

    p.op("sync", lambda e: e.dma_start(
        out=gw_sb[:].rearrange("p (c n) -> p c n", c=8),
        in_=gate_w.rearrange("(c p) n -> p c n", p=128)), inc=s_mo, inc_k=16)
    p.op("sync", lambda e: e.dma_start(out=upb_sb[:], in_=upb_t), inc=s_mo, inc_k=16)
    p.op("sync", lambda e: e.dma_start(out=dnb_sb[:], in_=dnb_t), inc=s_mo, inc_k=16)
    p.op("sync", lambda e: e.dma_start(out=sbu_sb[:], in_=sb_up_t), inc=s_mo, inc_k=16)
    p.op("sync", lambda e: e.dma_start(out=sbd_sb[:], in_=sb_dn_t), inc=s_mo, inc_k=16)
    p.op("sync", lambda e: e.dma_start(out=wrow[0:1, 0:E], in_=gate_b), inc=s_mo, inc_k=16)
    p.wait("tensor", s_mo, s_mo.n)
    s_gbb = sems("s_gbb")
    p.op("tensor", lambda e: e.matmul(PB[0][:, 0:E], ones_row[:], wrow[0:1, 0:E],
                                      start=True, stop=True), inc=s_gbb)
    p.wait("vector", s_gbb, s_gbb.n)
    p.op("vector", lambda e: e.tensor_copy(gb_bc[:], PB[0][:, 0:E]), inc=s_gbb)
    gbb_done = s_gbb.n

    done_gs = {}
    done_txe = {}
    up_ld = {}
    dn_ld = {}
    up_cons = {}
    dn_cons = {}
    out_cons = {}
    nup = 0
    ndn = 0
    gel_done = {}
    evm_done = {}

    def load_up(ei, grp):
        gi = ei * 4 + grp
        if gi >= 2:
            p.wait("sync", s_mmu, up_cons[gi - 2])
        src = wbf_su[0] if ei == NE else wbf_up[ei]
        p.op("sync", lambda e, grp=grp, src=src, gi=gi: e.dma_start(
            out=ubuf[gi % 2][:].rearrange("p (c n) -> p c n", c=8),
            in_=src.rearrange("(c p n) -> p c n", p=128, n=II)[:, :, grp * 1024:(grp + 1) * 1024]),
            inc=s_wup, inc_k=16)
        up_ld[gi] = s_wup.n

    def load_dn(ei, dt):
        di = ei * 8 + dt
        if di >= 2:
            p.wait("scalar", s_mmd, dn_cons[di - 2])
        srcd = wbf_sd[0] if ei == NE else wbf_dn[ei]
        p.op("scalar", lambda e, dt=dt, srcd=srcd, di=di: e.dma_start(
            out=dbuf[di % 2][:], in_=srcd[dt].rearrange("(q m) -> q m", q=128)),
            inc=s_wdn, inc_k=16)
        dn_ld[di] = s_wdn.n

    for st in range(NST):
        st_c0 = st * STC
        for ci in range(STC):
            c = st_c0 + ci
            pe = ci % 2
            if ci >= 2:
                p.wait("sync", s_gs, done_gs[ci - 2])
            if ci <= 1 and st > 0:
                p.wait("sync", s_zg, prev_epi)
            p.op("sync", lambda e, c=c, pe=pe: e.dma_start(
                out=xo_ch[pe][:], in_=x_own[c * 128:(c + 1) * 128, :]),
                inc=s_xo[pe], inc_k=16)
            xod = s_xo[pe].n
            p.wait("scalar", s_xo[pe], xod)

            if not (st == 0 and ci == 0):
                p.wait("scalar", s_ln2, sq2_cons)
            if dbg:
                p.wait("scalar", s_mo, s_mo.n)
            p.op("scalar", lambda e, pe=pe: e.activation(sq_mo[:], xo_ch[pe][:],
                                                         AF.Square), inc=s_ln2)
            p.wait("vector", s_xo[pe], xod)
            p.op("vector", lambda e, pe=pe: e.reduce_sum(stat[:, 0:1], xo_ch[pe][:],
                                                         axis=AX.X))
            p.wait("vector", s_ln2, s_ln2.n)
            p.op("vector", lambda e: e.reduce_sum(stat[:, 1:2], sq_mo[:], axis=AX.X),
                 inc=s_ln2)
            sq2_cons = s_ln2.n
            p.wait("vector", s_ln2, s_ln2.n)
            p.op("vector", lambda e: e.tensor_scalar(out=stat[:, 2:3], in0=stat[:, 0:1],
                                                     scalar1=1.0 / D, scalar2=None,
                                                     op0=ALU.mult), inc=s_ln2)
            p.wait("vector", s_ln2, s_ln2.n)
            p.op("vector", lambda e: e.tensor_tensor(stat[:, 3:4], stat[:, 2:3],
                                                     stat[:, 2:3], op=ALU.mult),
                 inc=s_ln2)
            p.wait("vector", s_ln2, s_ln2.n)
            p.op("vector", lambda e: e.scalar_tensor_tensor(
                out=stat[:, 4:5], in0=stat[:, 1:2], scalar=1.0 / D, in1=stat[:, 3:4],
                op0=ALU.mult, op1=ALU.subtract), inc=s_ln2)
            p.wait("vector", s_ln2, s_ln2.n)
            p.op("vector", lambda e: e.tensor_scalar(out=stat[:, 5:6], in0=stat[:, 4:5],
                                                     scalar1=1e-5, scalar2=None,
                                                     op0=ALU.add), inc=s_ln2)
            p.wait("vector", s_ln2, s_ln2.n)
            _rsqrt(p, stat, 5, 1, nsc, cfg.newton)
            p.op("vector", lambda e: e.tensor_tensor(stat[:, 8:9], stat[:, 2:3],
                                                     stat[:, 5:6], op=ALU.mult),
                 inc=s_ln2)
            p.wait("vector", s_ln2, s_ln2.n)
            if ci == 0:
                p.wait("vector", s_out, s_out.n)
            p.op("vector", lambda e, pe=pe: e.tensor_scalar(
                out=xn_mo[:], in0=xo_ch[pe][:], scalar1=stat[:, 5:6],
                scalar2=stat[:, 8:9], op0=ALU.mult, op1=ALU.subtract), inc=s_ln2)
            p.wait("vector", s_ln2, s_ln2.n)
            p.op("vector", lambda e: e.tensor_copy(xnb_mo[:], xn_mo[:]), inc=s_ln2)
            ln2_done = s_ln2.n

            p.wait("tensor", s_ln2, ln2_done)
            if ci >= 2:
                p.wait("tensor", s_txe, done_txe[ci - 2])
            if not (st == 0 and ci == 0):
                p.wait("tensor", s_gs, prev_gs)
            for g in range(8):
                p.op("tensor", lambda e, g=g: e.transpose(
                    PT[:, g * 128:(g + 1) * 128], xnb_mo[:, g * 128:(g + 1) * 128],
                    ident_b[:]), inc=(s_tx if g == 7 else None))
            for half4 in range(2):
                for g in range(4):
                    p.op("tensor", lambda e, g=g, half4=half4: e.transpose(
                        PI[:, g * 128:(g + 1) * 128],
                        xn_mo[:, (half4 * 4 + g) * 128:(half4 * 4 + g + 1) * 128],
                        ident_f[:]), inc=(s_tx if g == 3 else None))
                p.wait("vector", s_tx, s_tx.n)
                p.op("vector", lambda e, half4=half4: e.tensor_copy(
                    xnf_T[:, half4 * 512:(half4 + 1) * 512], PI[:, 0:512]), inc=s_txe)
                if half4 == 0:
                    p.wait("tensor", s_txe, s_txe.n)
            p.wait("vector", s_tx, s_tx.n - 2)
            p.op("vector", lambda e, ci=ci: e.tensor_copy(
                xnT_st[:].rearrange("p (g n) -> p g n", g=8)[:, :, ci * 128:(ci + 1) * 128],
                PT[:].rearrange("p (g n) -> p g n", g=8)), inc=s_txe)
            done_txe[ci] = s_txe.n

            p.wait("tensor", s_txe, done_txe[ci])
            if st == 0 and ci == 0:
                p.wait("tensor", s_gbb, gbb_done)
            if not (st == 0 and ci == 0):
                p.wait("tensor", s_gs, sc_cons)
            for cc in range(8):
                p.op("tensor", lambda e, cc=cc: e.matmul(
                    PB[0][:, 0:E], xnf_T[:, cc * 128:(cc + 1) * 128],
                    gw_sb[:, cc * E:(cc + 1) * E], start=(cc == 0), stop=(cc == 7)),
                    inc=(s_gt if cc == 7 else None))
            p.wait("vector", s_gt, s_gt.n)
            if dbg:
                p.wait("vector", s_mo, s_mo.n)
            p.op("vector", lambda e: e.tensor_tensor(sc_tm[:], PB[0][:, 0:E], gb_bc[:],
                                                     op=ALU.add), inc=s_gs)
            sc_cons = s_gs.n
            p.wait("scalar", s_gs, s_gs.n)
            p.op("scalar", lambda e: e.activation(w_tm[:], sc_tm[:], AF.Tanh, scale=0.5),
                 inc=s_gs)
            p.wait("vector", s_gs, s_gs.n)
            p.op("vector", lambda e: e.tensor_scalar(out=sc_tm[:], in0=w_tm[:],
                                                     scalar1=0.5, scalar2=0.5,
                                                     op0=ALU.mult, op1=ALU.add), inc=s_gs)
            p.wait("vector", s_gs, s_gs.n)
            if "dbg_scores" in dbg:
                p.wait("sync", s_gs, s_gs.n)
                p.op("sync", lambda e, c=c: e.dma_start(
                    out=dbg["dbg_scores"][c * 128:(c + 1) * 128, :], in_=sc_tm[:]),
                    inc=s_mo, inc_k=16)
            p.op("vector", lambda e: e.reduce_max(stat[:, 0:1], sc_tm[:], axis=AX.X),
                 inc=s_gs)
            p.wait("vector", s_gs, s_gs.n)
            p.op("vector", lambda e: e.tensor_scalar(out=w_tm[:], in0=sc_tm[:],
                                                     scalar1=stat[:, 0:1], scalar2=None,
                                                     op0=ALU.is_ge), inc=s_gs)
            p.wait("vector", s_gs, s_gs.n)
            p.op("vector", lambda e: e.scalar_tensor_tensor(
                out=w_tm[:], in0=w_tm[:], scalar=-1e30, in1=sc_tm[:],
                op0=ALU.mult, op1=ALU.add), inc=s_gs)
            p.wait("vector", s_gs, s_gs.n)
            p.op("vector", lambda e: e.reduce_max(stat[:, 1:2], w_tm[:], axis=AX.X),
                 inc=s_gs)
            p.wait("vector", s_gs, s_gs.n)
            p.op("vector", lambda e: e.tensor_tensor(stat[:, 2:3], stat[:, 0:1],
                                                     stat[:, 1:2], op=ALU.add), inc=s_gs)
            p.wait("vector", s_gs, s_gs.n)
            p.op("vector", lambda e: e.tensor_scalar(out=stat[:, 2:3], in0=stat[:, 2:3],
                                                     scalar1=1e-6, scalar2=None,
                                                     op0=ALU.add), inc=s_gs)
            p.wait("vector", s_gs, s_gs.n)
            p.op("vector", lambda e: e.reciprocal(stat[:, 3:4], stat[:, 2:3]), inc=s_gs)
            p.wait("vector", s_gs, s_gs.n)
            p.op("vector", lambda e: e.tensor_scalar(out=w_tm[:], in0=sc_tm[:],
                                                     scalar1=stat[:, 1:2], scalar2=None,
                                                     op0=ALU.is_ge), inc=s_gs)
            p.wait("vector", s_gs, s_gs.n)
            p.op("vector", lambda e: e.tensor_tensor(w_tm[:], w_tm[:], sc_tm[:],
                                                     op=ALU.mult), inc=s_gs)
            p.wait("vector", s_gs, s_gs.n)
            p.op("vector", lambda e: e.tensor_scalar(out=w_tm[:], in0=w_tm[:],
                                                     scalar1=stat[:, 3:4], scalar2=None,
                                                     op0=ALU.mult), inc=s_gs)
            p.wait("vector", s_gs, s_gs.n)
            if "dbg_w8" in dbg:
                p.wait("sync", s_gs, s_gs.n)
                p.op("sync", lambda e, c=c: e.dma_start(
                    out=dbg["dbg_w8"][c * 128:(c + 1) * 128, :], in_=w_tm[:]),
                    inc=s_mo, inc_k=16)
            p.wait("tensor", s_gs, s_gs.n)
            p.op("tensor", lambda e: e.transpose(PI[0:E, 0:128], w_tm[:], ident_f[:]),
                 inc=s_gt)
            p.wait("vector", s_gt, s_gt.n)
            p.op("vector", lambda e, ci=ci: e.tensor_copy(
                wT_all[0:E, ci * 128:(ci + 1) * 128], PI[0:E, 0:128]), inc=s_gs)
            done_gs[ci] = s_gs.n
            prev_gs = s_gs.n
        gates_done = done_gs[STC - 1]

        p.wait("vector", s_eva, s_eva.n)
        p.op("vector", lambda e: e.memset(comb[:], 0.0), inc=s_cz)
        p.wait("gpsimd", s_cz, s_cz.n)
        cz_done = s_cz.n

        for ei in range(NE + 1):
            shared = ei == NE
            egi = ei + st * (NE + 1)
            for g0 in range(2):
                gi = egi * 8 + g0
                if gi >= 2:
                    p.wait("sync", s_mmu, up_cons[gi - 2])
                src = wbf_su[0] if shared else wbf_up[ei]
                p.op("sync", lambda e, g0=g0, src=src, gi=gi: e.dma_start(
                    out=ubuf[gi % 2][:].rearrange("p (c n) -> p c n", c=8),
                    in_=src.rearrange("(c p n) -> p c n", p=128, n=II)
                    [:, :, g0 * 512:(g0 + 1) * 512]), inc=s_wup[gi % 2], inc_k=16)
                up_ld[gi] = s_wup[gi % 2].n
            for d0 in range(2):
                di = egi * 8 + d0
                if di >= 2:
                    p.wait("scalar", s_mmd, dn_cons[di - 2])
                srcd = wbf_sd[0] if shared else wbf_dn[ei]
                p.op("scalar", lambda e, d0=d0, srcd=srcd, di=di: e.dma_start(
                    out=dbuf[di % 2][:], in_=srcd[d0].rearrange("(q m) -> q m", q=128)),
                    inc=s_wdn[di % 2], inc_k=16)
                dn_ld[di] = s_wdn[di % 2].n

            nt512 = STOK // 512
            if not shared:
                p.wait("gpsimd", s_gs, gates_done)
                p.wait("gpsimd", s_evm, s_evm.n)
                for t5 in range(nt512):
                    p.op("gpsimd", lambda e, ei=ei, t5=t5: e.dma_start(
                        out=wrow[0:1, 0:512],
                        in_=wT_all[ei:ei + 1, t5 * 512:(t5 + 1) * 512]),
                        inc=s_wrow, inc_k=16)
                    p.wait("tensor", s_wrow, s_wrow.n)
                    p.wait("tensor", s_evm, s_evm.n)
                    if ei == 0 and t5 == 0:
                        p.wait("tensor", s_gs, gates_done)
                    p.op("tensor", lambda e: e.matmul(
                        PI[:, 0:512], ones_row[:], wrow[0:1, 0:512],
                        start=True, stop=True), inc=s_wb2)
                    p.wait("vector", s_wb2, s_wb2.n)
                    p.op("vector", lambda e, t5=t5: e.tensor_copy(
                        wbc[:, t5 * 512:(t5 + 1) * 512], PI[:, 0:512]), inc=s_wb2)
                    p.wait("gpsimd", s_wb2, s_wb2.n)
                wb2_cons = s_wb2.n

            for grp in range(8):
                gi = egi * 8 + grp
                p.wait("tensor", s_wup[gi % 2], up_ld[gi])
                if ei == 0 and grp == 0:
                    p.wait("tensor", s_gs, gates_done)
                if grp == 0:
                    p.wait("tensor", s_evm, s_evm.n)
                for it4 in range(4):
                    it = grp * 4 + it4
                    if nup >= 2:
                        p.wait("tensor", s_gel, gel_done[nup - 2])
                    bb = (nup % 2) * 2
                    for cc in range(8):
                        for t5 in range(nt512):
                            p.op("tensor", lambda e, it4=it4, cc=cc, gi=gi, t5=t5,
                                 bb=bb: e.matmul(
                                     PB[bb + t5][:, 0:512],
                                     ubuf[gi % 2][:, cc * 512 + it4 * 128:
                                                  cc * 512 + it4 * 128 + 128],
                                     xnT_st[:, cc * STOK + t5 * 512:
                                            cc * STOK + (t5 + 1) * 512],
                                     start=(cc == 0), stop=(cc == 7)),
                                 inc=(s_mmu if (cc == 7 and t5 == nt512 - 1) else None))
                    p.wait("scalar", s_mmu, s_mmu.n)
                    if it == 0 and grp == 0:
                        p.wait("scalar", s_mmd, s_mmd.n)
                    bcol = (0 if shared else ei * 32) + it
                    bsrc = sbu_sb if shared else upb_sb
                    for t5 in range(nt512):
                        p.op("scalar", lambda e, it=it, t5=t5, bb=bb, bcol=bcol,
                             bsrc=bsrc: e.activation(
                                 hact[:, it * STOK + t5 * 512:it * STOK + (t5 + 1) * 512],
                                 PB[bb + t5][:, 0:512], AF.Gelu,
                                 bias=bsrc[:, bcol:bcol + 1], scale=1.0), inc=s_gel)
                    gel_done[nup] = s_gel.n
                    nup += 1
                up_cons[gi] = s_mmu.n
                if grp < 6:
                    gi2 = egi * 8 + grp + 2
                    p.wait("sync", s_mmu, up_cons[gi])
                    src = wbf_su[0] if shared else wbf_up[ei]
                    p.op("sync", lambda e, grp=grp, src=src, gi2=gi2: e.dma_start(
                        out=ubuf[gi2 % 2][:].rearrange("p (c n) -> p c n", c=8),
                        in_=src.rearrange("(c p n) -> p c n", p=128, n=II)
                        [:, :, (grp + 2) * 512:(grp + 3) * 512]), inc=s_wup[gi2 % 2],
                        inc_k=16)
                    up_ld[gi2] = s_wup[gi2 % 2].n

            p.wait("tensor", s_gel, s_gel.n)
            for dt in range(8):
                di = egi * 8 + dt
                p.wait("tensor", s_wdn[di % 2], dn_ld[di])
                if ndn >= 2:
                    p.wait("tensor", s_evm, evm_done[ndn - 2])
                bb = (ndn % 2) * 2
                for cc in range(32):
                    for t5 in range(nt512):
                        p.op("tensor", lambda e, cc=cc, di=di, t5=t5, bb=bb:
                             e.matmul(PB[bb + t5][:, 0:512],
                                      dbuf[di % 2][:, cc * 128:(cc + 1) * 128],
                                      hact[:, cc * STOK + t5 * 512:
                                           cc * STOK + (t5 + 1) * 512],
                                      start=(cc == 0), stop=(cc == 31)),
                             inc=(s_mmd if (cc == 31 and t5 == nt512 - 1) else None))
                p.wait("vector", s_mmd, s_mmd.n)
                if not shared:
                    p.wait("vector", s_wb2, wb2_cons)
                for t5 in range(nt512):
                    p.wait("vector", s_eva, s_eva.n)
                    if shared:
                        p.op("vector", lambda e, dt=dt, t5=t5, bb=bb: e.tensor_scalar(
                            out=tmp_ev[:], in0=PB[bb + t5][:, 0:512],
                            scalar1=sbd_sb[:, dt:dt + 1], scalar2=None, op0=ALU.add),
                            inc=s_evm)
                    else:
                        p.op("vector", lambda e, dt=dt, t5=t5, bb=bb, ei=ei:
                             e.scalar_tensor_tensor(
                                 out=tmp_ev[:], in0=PB[bb + t5][:, 0:512],
                                 scalar=dnb_sb[:, ei * 8 + dt:ei * 8 + dt + 1],
                                 in1=wbc[:, t5 * 512:(t5 + 1) * 512],
                                 op0=ALU.add, op1=ALU.mult), inc=s_evm)
                    p.wait("gpsimd", s_evm, s_evm.n)
                    if st == 0 and ei == 0 and dt == 0 and t5 == 0:
                        p.wait("gpsimd", s_cz, cz_done)
                    p.op("gpsimd", lambda e, dt=dt, t5=t5: e.tensor_tensor(
                        comb[:, dt * STOK + t5 * 512:dt * STOK + (t5 + 1) * 512],
                        comb[:, dt * STOK + t5 * 512:dt * STOK + (t5 + 1) * 512],
                        tmp_ev[:], op=ALU.add), inc=s_eva)
                evm_done[ndn] = s_evm.n
                ndn += 1
                dn_cons[di] = s_mmd.n
                if dt < 6:
                    di2 = egi * 8 + dt + 2
                    p.wait("scalar", s_mmd, dn_cons[di])
                    srcd = wbf_sd[0] if shared else wbf_dn[ei]
                    p.op("scalar", lambda e, dt=dt, srcd=srcd, di2=di2: e.dma_start(
                        out=dbuf[di2 % 2][:],
                        in_=srcd[dt + 2].rearrange("(q m) -> q m", q=128)),
                        inc=s_wdn[di2 % 2], inc_k=16)
                    dn_ld[di2] = s_wdn[di2 % 2].n
        p.wait("scalar", s_eva, s_eva.n)
        p.op("scalar", lambda e: e.activation(mx8[:], comb[:, 0:STOK], AF.Abs), inc=s_n2)
        for g in range(1, 8):
            p.wait("scalar", s_n2, s_n2.n)
            p.op("scalar", lambda e, g=g: e.activation(
                graw[:], comb[:, g * STOK:(g + 1) * STOK], AF.Abs), inc=s_n2)
            p.wait("vector", s_n2, s_n2.n)
            p.op("vector", lambda e: e.tensor_tensor(mx8[:], mx8[:], graw[:],
                                                     op=ALU.max), inc=s_n2)
        p.wait("gpsimd", s_n2, s_n2.n)
        p.op("gpsimd", lambda e: e.dma_start(out=graw[0:64, :], in_=mx8[64:128, :]),
             inc=s_n2g, inc_k=16)
        p.wait("vector", s_n2g, s_n2g.n)
        p.op("vector", lambda e: e.tensor_tensor(mx8[0:64, :], mx8[0:64, :],
                                                 graw[0:64, :], op=ALU.max),
             inc=s_n2)
        p.wait("gpsimd", s_n2, s_n2.n)
        p.op("gpsimd", lambda e: e.dma_start(out=graw[0:32, :], in_=mx8[32:64, :]),
             inc=s_n2g, inc_k=16)
        p.wait("vector", s_n2g, s_n2g.n)
        p.wait("vector", s_n2, s_n2.n)
        p.op("vector", lambda e: e.tensor_tensor(mx8[0:32, :], mx8[0:32, :],
                                                 graw[0:32, :], op=ALU.max), inc=s_n2)
        for ci in range(STC):
            p.wait("tensor", s_n2, s_n2.n)
            p.op("tensor", lambda e, ci=ci: e.transpose(
                PI[:, 0:32], mx8[0:32, ci * 128:(ci + 1) * 128],
                ident_f[0:32, 0:32]), inc=s_tz)
            p.wait("vector", s_tz, s_tz.n)
            p.op("vector", lambda e, ci=ci: e.reduce_max(
                stat[:, 0:1], PI[:, 0:32], axis=AX.X), inc=s_n2)
            p.wait("vector", s_n2, s_n2.n)
            p.op("vector", lambda e: e.tensor_scalar(out=stat[:, 0:1], in0=stat[:, 0:1],
                                                     scalar1=1e-5, scalar2=None,
                                                     op0=ALU.add), inc=s_n2)
            p.wait("vector", s_n2, s_n2.n)
            p.op("vector", lambda e, ci=ci: e.reciprocal(rtm[:, ci:ci + 1],
                                                         stat[:, 0:1]), inc=s_n2)
            p.wait("vector", s_n2, s_n2.n)
            p.wait("tensor", s_n2, s_n2.n)
        p.op("tensor", lambda e: e.transpose(PI[0:STC, 0:128], rtm[:], ident_f[:]),
             inc=s_tz)
        p.wait("vector", s_tz, s_tz.n)
        p.op("vector", lambda e: e.tensor_copy(rT_sb[0:STC, :], PI[0:STC, 0:128]),
             inc=s_n2)
        p.wait("gpsimd", s_n2, s_n2.n)
        for ci in range(STC):
            p.op("gpsimd", lambda e, ci=ci: e.dma_start(
                out=wrow[0:1, ci * 128:(ci + 1) * 128], in_=rT_sb[ci:ci + 1, :]),
                inc=s_n2g, inc_k=16)
        rrow_done = s_n2g.n
        p.wait("tensor", s_n2g, rrow_done)
        for g5 in range(STOK // 512):
            p.op("tensor", lambda e, g5=g5: e.matmul(
                PI[:, 0:512], ones_row[:], wrow[0:1, g5 * 512:(g5 + 1) * 512],
                start=True, stop=True), inc=s_tz)
            p.wait("vector", s_tz, s_tz.n)
            p.op("vector", lambda e, g5=g5: e.tensor_copy(
                rbc[:, g5 * 512:(g5 + 1) * 512], PI[:, 0:512]), inc=s_n2)
            p.wait("tensor", s_n2, s_n2.n)
        for g in range(8):
            p.wait("vector", s_n2, s_n2.n)
            p.op("vector", lambda e, g=g: e.tensor_tensor(
                graw[:], comb[:, g * STOK:(g + 1) * STOK], rbc[:], op=ALU.mult),
                inc=s_n2)
            p.wait("scalar", s_n2, s_n2.n)
            p.op("scalar", lambda e: e.activation(mx8[:], graw[:], AF.Tanh, scale=0.5),
                 inc=s_n2)
            p.wait("vector", s_n2, s_n2.n)
            p.op("vector", lambda e, g=g: e.scalar_tensor_tensor(
                out=zg[:, g * STOK:(g + 1) * STOK], in0=mx8[:], scalar=1.0, in1=graw[:],
                op0=ALU.add, op1=ALU.mult), inc=s_zg)
            p.wait("vector", s_zg, s_zg.n)
        zg_done = s_zg.n
        for ci in range(STC):
            c = st_c0 + ci
            pe = ci % 2
            p.wait("tensor", s_zg, zg_done)
            if not (st == 0 and ci == 0):
                p.wait("tensor", s_zg, zgt_cons)
            for g in range(8):
                p.op("tensor", lambda e, g=g, ci=ci: e.transpose(
                    PT[:, g * 128:(g + 1) * 128],
                    zg[:, g * STOK + ci * 128:g * STOK + (ci + 1) * 128], ident_b[:]),
                    inc=(s_tz if g == 7 else None))
            p.wait("vector", s_tz, s_tz.n)
            p.op("vector", lambda e: e.tensor_copy(zgT[:], PT[:]), inc=s_zg)
            zgt_cons = s_zg.n
            if ci == 0:
                p.wait("sync", s_ln2, s_ln2.n)
            if not (st == 0 and ci == 0):
                p.wait("sync", s_zg, szg_done)
            if not (st == 0 and ci <= 1):
                key = (st, ci - 2) if ci >= 2 else (st - 1, STC + ci - 2)
                p.wait("sync", s_out, out_cons.get(key, 0))
            oc = st * STC + ci
            p.wait("gpsimd", s_sw[0], sw_all[0])
            p.wait("gpsimd", s_sw[1], sw_all[1])
            if not (st == 0 and ci == 0):
                p.wait("gpsimd", s_zg, szg_done)
            p.op("gpsimd", lambda e, oc=oc: e.dma_start(
                out=idxw_sb[:],
                in_=idxw[0:1, oc * 128:(oc + 1) * 128].rearrange("r (p n) -> (r p) n", n=1)),
                inc=s_epg, inc_k=16)
            p.wait("gpsimd", s_epg, s_epg.n)
            p.op("gpsimd", lambda e: e.indirect_dma_start(
                out=sch[0][:], out_offset=None, in_=s_scr,
                in_offset=IndirectOffsetOnAxis(ap=idxw_sb[:], axis=0)),
                inc=s_epg, inc_k=16)
            p.op("sync", lambda e, c=c, pe=pe: e.dma_start(
                out=xo_ch[pe][:], in_=x_own[c * 128:(c + 1) * 128, :]),
                inc=s_ep, inc_k=16)
            ep_ld = s_ep.n
            p.wait("vector", s_ep, ep_ld)
            p.wait("vector", s_out, s_out.n)
            p.wait("vector", s_epg, s_epg.n)
            p.wait("vector", s_zg, zgt_cons)
            if dbg:
                p.wait("vector", s_mo, s_mo.n)
            p.op("vector", lambda e, pe=pe: e.tensor_tensor(
                sq_mo[:], sch[0][:], zgT[:], op=ALU.mult), inc=s_zg)
            szg_done = s_zg.n
            p.wait("vector", s_zg, s_zg.n)
            p.op("vector", lambda e, pe=pe: e.scalar_tensor_tensor(
                out=xn_mo[:], in0=sq_mo[:], scalar=0.5, in1=xo_ch[pe][:],
                op0=ALU.mult, op1=ALU.add), inc=s_zg)
            och_done = s_zg.n
            prev_epi = s_zg.n
            p.wait("sync", s_zg, och_done)
            p.op("sync", lambda e, c=c, pe=pe: e.dma_start(
                out=out_d[c * 128:(c + 1) * 128, :], in_=xn_mo[:]),
                inc=s_out, inc_k=16)
            out_cons[(st, ci)] = s_out.n
            if "dbg_zg" in dbg:
                p.op("sync", lambda e, c=c: e.dma_start(
                    out=dbg["dbg_zg"][c * 128:(c + 1) * 128, :], in_=zgT[:]
                    if False else sq_mo[:]), inc=s_mo, inc_k=16)

    p.wait("sync", s_out, s_out.n)
    if dbg:
        p.wait("sync", s_mo, s_mo.n)
        p.wait("sync", s_dbg, s_dbg.n)
        p.wait("gpsimd", s_dbg2, s_dbg2.n)

    with nc.Block() as blk:
        blk.sync(lambda e: [f(e) for f in p.q["sync"]])
        blk.tensor(lambda e: [f(e) for f in p.q["tensor"]])
        blk.vector(lambda e: [f(e) for f in p.q["vector"]])
        blk.scalar(lambda e: [f(e) for f in p.q["scalar"]])
        blk.gpsimd(lambda e: [f(e) for f in p.q["gpsimd"]])
    oes.close()
    es.close()
    return nc


_CACHE = {}


def _get_nc(cfg_key):
    if cfg_key not in _CACHE:
        cfg = Cfg()
        _CACHE[cfg_key] = build(cfg)
    return _CACHE[cfg_key]


def kernel(L0_attn_x, L0_attn_ln_w, L0_attn_ln_b, L0_attn_Wq, L0_attn_Wk, L0_attn_Wv,
           L0_attn_Wb, L0_attn_o_norm_w, L0_attn_Wo, L0_ffn_gate_w, L0_ffn_bias,
           L0_ffn_up_w, L0_ffn_up_b, L0_ffn_down_w, L0_ffn_down_b, L0_ffn_sw_up,
           L0_ffn_sb_up, L0_ffn_sw_down, L0_ffn_sb_down):
    x = np.ascontiguousarray(L0_attn_x, dtype=np.float32)
    nc = _get_nc("full")

    upb_t = np.ascontiguousarray(
        L0_ffn_up_b.reshape(E, 32, 128).transpose(2, 0, 1).reshape(128, E * 32))
    dnb_t = np.ascontiguousarray(
        L0_ffn_down_b.reshape(E, 8, 128).transpose(2, 0, 1).reshape(128, E * 8))
    sbu_t = np.ascontiguousarray(L0_ffn_sb_up.reshape(32, 128).T)
    sbd_t = np.ascontiguousarray(L0_ffn_sb_down.reshape(8, 128).T)

    common = {
        "ln_w": L0_attn_ln_w.reshape(1, D), "ln_b": L0_attn_ln_b.reshape(1, D),
        "Wq": L0_attn_Wq, "Wk": L0_attn_Wk, "Wv": L0_attn_Wv, "Wb": L0_attn_Wb,
        "onw": L0_attn_o_norm_w.reshape(1, DV), "Wo": L0_attn_Wo,
        "gate_w": L0_ffn_gate_w, "gate_b": L0_ffn_bias.reshape(1, E),
        "upw": L0_ffn_up_w, "upb_t": upb_t, "dnw": L0_ffn_down_w, "dnb_t": dnb_t,
        "sw_up": L0_ffn_sw_up, "sb_up_t": sbu_t, "sw_dn": L0_ffn_sw_down,
        "sb_dn_t": sbd_t,
    }
    in_maps = []
    for core in range(8):
        b = core // 2
        half = core % 2
        m = dict(common)
        m["x_full"] = x[b]
        m["x_own"] = x[b, half * 4096:(half + 1) * 4096]
        idx = np.arange(4096, dtype=np.int32) + 4096 * half
        m["idxw"] = idx.reshape(1, 4096)
        in_maps.append(m)

    res = run_bass_kernel_spmd(nc, in_maps, core_ids=list(range(8)))
    out = np.empty((B, T, D), np.float32)
    for core in range(8):
        b = core // 2
        half = core % 2
        out[b, half * 4096:(half + 1) * 4096] = res.results[core]["out"]
    return out


B = 4
T = 8192


# version 06
# speedup vs baseline: 1.2593x; 1.0711x over previous
import numpy as np
import ml_dtypes
from contextlib import ExitStack

import concourse.bass as bass
import concourse.mybir as mybir
from concourse.bass_utils import run_bass_kernel_spmd
from concourse.bass_core_rust import IndirectOffsetOnAxis

F32 = mybir.dt.float32
BF16 = mybir.dt.bfloat16
I32 = mybir.dt.int32
AF = mybir.ActivationFunctionType
ALU = mybir.AluOpType
AX = mybir.AxisListType

D = 1024
H = 4
DK = 256
DV = 512
E = 8
II = 4096
C = 128


class Cfg:
    n_chunks = 64
    own_chunks = 32
    st_chunks = 8
    n_exp = 8
    newton = 3
    debug = ()


ENGINES = ("sync", "tensor", "vector", "scalar", "gpsimd")


class Sem:
    __slots__ = ("h", "n", "name")

    def __init__(self, h, name):
        self.h = h
        self.n = 0
        self.name = name


class P:
    def __init__(self, nc, es):
        self.nc = nc
        self.es = es
        self.q = {e: [] for e in ENGINES}
        self._sems = {}

    def sem(self, name):
        if name not in self._sems:
            self._sems[name] = Sem(self.es.enter_context(self.nc.semaphore(name)), name)
        return self._sems[name]

    def wait(self, eng, sem, val=None):
        v = sem.n if val is None else val
        if v > 0:
            self.q[eng].append(lambda e, s=sem.h, v=v: e.wait_ge(s, v))

    def op(self, eng, fn, inc=None, inc_k=1):
        if inc is not None:
            inc.n += inc_k

            def run(e, fn=fn, s=inc.h, k=inc_k):
                fn(e).then_inc(s, k)
        else:
            def run(e, fn=fn):
                fn(e)
        self.q[eng].append(run)


def _rsqrt(p, buf, col0, ncol, scr, iters):
    sn = p.sem("snl")
    x = buf[:, col0:col0 + ncol]
    y = scr[:, 0:ncol]
    t = scr[:, ncol:2 * ncol]
    p.op("vector", lambda e: e.tensor_scalar(out=scr[:, 0:ncol].bitcast(I32),
                                             in0=x.bitcast(I32), scalar1=1, scalar2=None,
                                             op0=ALU.logical_shift_right), inc=sn)
    p.wait("vector", sn)
    p.op("vector", lambda e: e.tensor_scalar(out=scr[:, 2 * ncol:3 * ncol].bitcast(I32),
                                             in0=scr[:, 0:ncol].bitcast(I32),
                                             scalar1=0x5F3759DF, scalar2=-1,
                                             op0=ALU.subtract, op1=ALU.mult), inc=sn)
    y = scr[:, 2 * ncol:3 * ncol]
    for _ in range(iters):
        p.wait("vector", sn)
        p.op("vector", lambda e: e.tensor_tensor(t, y, y, op=ALU.mult), inc=sn)
        p.wait("vector", sn)
        p.op("vector", lambda e: e.scalar_tensor_tensor(out=t, in0=x, scalar=-0.5, in1=t,
                                                        op0=ALU.mult, op1=ALU.mult), inc=sn)
        p.wait("vector", sn)
        p.op("vector", lambda e: e.tensor_scalar(out=t, in0=t, scalar1=1.5, scalar2=None,
                                                 op0=ALU.add), inc=sn)
        p.wait("vector", sn)
        p.op("vector", lambda e: e.tensor_tensor(y, y, t, op=ALU.mult), inc=sn)
    p.wait("vector", sn)
    p.op("vector", lambda e: e.tensor_copy(x, y), inc=sn)
    p.wait("vector", sn)


def build(cfg):
    nc = bass.Bass("TRN2", target_bir_lowering=False)
    NCH = cfg.n_chunks
    OCH = cfg.own_chunks
    STC = cfg.st_chunks
    NE = cfg.n_exp
    NST = OCH // STC
    STOK = STC * C
    TOKB = NCH * C
    TOKO = OCH * C

    def din(name, shape, dtype=F32):
        return nc.dram_tensor(name, list(shape), dtype, kind="ExternalInput").ap()

    x_full = din("x_full", (TOKB, D))
    x_own = din("x_own", (TOKO, D))
    ln_w = din("ln_w", (1, D))
    ln_b = din("ln_b", (1, D))
    Wq = din("Wq", (D, H * DK))
    Wk = din("Wk", (D, H * DK))
    Wv = din("Wv", (D, H * DV))
    Wb = din("Wb", (D, H))
    onw = din("onw", (1, DV))
    Wo = din("Wo", (H * DV, D))
    gate_w = din("gate_w", (D, E))
    gate_b = din("gate_b", (1, E))
    upw = din("upw", (NE, D, II))
    upb_t = din("upb_t", (128, NE * 32))
    dnw = din("dnw", (NE, II, D))
    dnb_t = din("dnb_t", (128, NE * 8))
    sw_up = din("sw_up", (D, II))
    sb_up_t = din("sb_up_t", (128, 32))
    sw_dn = din("sw_dn", (II, D))
    sb_dn_t = din("sb_dn_t", (128, 8))
    idxw = din("idxw", (1, OCH * 128), I32)

    out_d = nc.dram_tensor("out", [TOKO, D], F32, kind="ExternalOutput").ap()
    s_scr = nc.dram_tensor("s_scr", [NCH * 128, D], F32, kind="Internal").ap()
    xn_scr = nc.dram_tensor("xn_scr", [NCH * 128, D], F32, kind="Internal").ap()
    on_scr = nc.dram_tensor("on_scr", [NCH * 128, H * DV], BF16, kind="Internal").ap()
    wbf_up = nc.dram_tensor("wbf_up", [NE, D * II], BF16, kind="Internal").ap()
    wbf_dn = nc.dram_tensor("wbf_dn", [NE, 8, 128 * 32 * 128], BF16, kind="Internal").ap()
    wbf_su = nc.dram_tensor("wbf_su", [1, D * II], BF16, kind="Internal").ap()
    wbf_sd = nc.dram_tensor("wbf_sd", [1, 8, 128 * 32 * 128], BF16, kind="Internal").ap()

    dbg = {}
    for name in cfg.debug:
        shp = {"dbg_s": (TOKB, D), "dbg_xn": (TOKB, D), "dbg_nrm": (TOKB, D),
               "dbg_q": (TOKB, H * DK), "dbg_k": (TOKB, H * DK),
               "dbg_v": (TOKB, H * DV), "dbg_beta": (TOKB, H),
               "dbg_o": (TOKB, H * DV), "dbg_scores": (TOKO, E),
               "dbg_w8": (TOKO, E), "dbg_zg": (TOKO, D)}[name]
        dbg[name] = nc.dram_tensor(name, list(shp), F32, kind="ExternalOutput").ap()

    es = ExitStack()
    p = P(nc, es)
    sems = p.sem

    def sb(name, shape, dtype):
        return es.enter_context(nc.sbuf_tensor(name, list(shape), dtype))

    def psum(name, shape, dtype=F32):
        return es.enter_context(nc.psum_tensor(name, list(shape), dtype))

    ident_b = sb("ident_b", (128, 128), BF16)
    ident_f = sb("ident_f", (128, 128), F32)
    tril01 = sb("tril01", (128, 128), F32)
    triu01 = sb("triu01", (128, 128), F32)
    ones_row = sb("ones_row", (1, 128), F32)
    nsc = sb("nsc", (128, 32), F32)
    stat = sb("stat", (128, 24), F32)
    idxw_sb = sb("idxw_sb", (128, 1), I32)

    PB = [psum(f"pb{i}", (128, 512), F32) for i in range(4)]
    PT = psum("pt", (128, 1024), BF16)
    PI = psum("pi", (128, 1536), F32)

    s_init = sems("s_init")
    p.op("gpsimd", lambda e: e.memset(ident_b[:], 0.0), inc=s_init)
    p.wait("gpsimd", s_init)
    p.op("gpsimd", lambda e: e.affine_select(out=ident_b[:], in_=ident_b[:],
                                             compare_op=ALU.not_equal, fill=1.0, base=0,
                                             pattern=[[-1, 128]], channel_multiplier=1),
         inc=s_init)
    p.op("gpsimd", lambda e: e.memset(ident_f[:], 0.0), inc=s_init)
    p.wait("gpsimd", s_init)
    p.op("gpsimd", lambda e: e.affine_select(out=ident_f[:], in_=ident_f[:],
                                             compare_op=ALU.not_equal, fill=1.0, base=0,
                                             pattern=[[-1, 128]], channel_multiplier=1),
         inc=s_init)
    p.op("gpsimd", lambda e: e.memset(tril01[:], 0.0), inc=s_init)
    p.wait("gpsimd", s_init)
    p.op("gpsimd", lambda e: e.affine_select(out=tril01[:], in_=tril01[:],
                                             compare_op=ALU.is_ge, fill=1.0, base=0,
                                             pattern=[[1, 128]], channel_multiplier=-1),
         inc=s_init)
    p.op("gpsimd", lambda e: e.memset(triu01[:], 1.0), inc=s_init)
    p.wait("gpsimd", s_init)
    p.op("gpsimd", lambda e: e.affine_select(out=triu01[:], in_=triu01[:],
                                             compare_op=ALU.is_ge, fill=0.0, base=0,
                                             pattern=[[1, 128]], channel_multiplier=-1),
         inc=s_init)
    p.op("gpsimd", lambda e: e.memset(ones_row[:], 1.0), inc=s_init)

    mes = ExitStack()

    def msb(name, shape, dtype):
        return mes.enter_context(nc.sbuf_tensor(name, list(shape), dtype))

    wq_sb = msb("wq_sb", (128, 8 * 1024), BF16)
    wk_sb = msb("wk_sb", (128, 8 * 1024), BF16)
    wv_sb = msb("wv_sb", (128, 8 * 2048), BF16)
    wb_sb = msb("wb_sb", (128, 8 * H), BF16)
    lnw_bc = msb("lnw_bc", (128, D), F32)
    lnb_bc = msb("lnb_bc", (128, D), F32)
    onw_bc = msb("onw_bc", (128, DV), F32)
    lnrow = msb("lnrow", (1, 2 * D + DV), F32)
    S_b = [msb(f"S_b{h}", (128, 2 * DV), BF16) for h in range(H)]
    xch = [msb(f"xch{i}", (128, D), F32) for i in range(2)]
    xnch = [msb(f"xnch{i}", (128, D), F32) for i in range(2)]
    sqch = msb("sqch", (128, D), F32)
    nrm0 = msb("nrm0", (128, D), F32)
    osq = msb("osq", (128, 2 * DV), F32)
    nrmb = [msb(f"nrmb{i}", (128, D), BF16) for i in range(2)]
    nT = [msb(f"nT{i}", (128, 8 * 128), BF16) for i in range(2)]
    qraw = [msb("qraw0", (128, H * DK), F32)]
    kraw = [msb("kraw0", (128, H * DK), F32)]
    qn = [msb(f"qn{i}", (128, H * DK), BF16) for i in range(2)]
    kn = [msb(f"kn{i}", (128, H * DK), BF16) for i in range(2)]
    vsil = [msb(f"vsil{i}", (128, H * DV), F32) for i in range(2)]
    beta_t = [msb(f"beta_t{i}", (128, 2 * H), F32) for i in range(2)]
    qT = [msb(f"qT{i}", (128, H * 2 * 128), BF16) for i in range(2)]
    kT = [msb(f"kT{i}", (128, H * 2 * 128), BF16) for i in range(2)]
    vd_b = [msb("vd_b0", (128, H * DV), BF16)]
    u_b = msb("u_b", (128, H * DV), BF16)
    gm_b = msb("gm_b", (128, H * 128), BF16)
    m_f = [msb(f"m_f{i}", (128, H * 128), F32) for i in range(2)]
    mt_f = [msb(f"mt_f{i}", (128, H * 128), F32) for i in range(2)]
    x2_f = [msb(f"x2_f{i}", (128, H * 128), F32) for i in range(2)]
    rt_f = [msb(f"rt_f{i}", (128, H * 128), F32) for i in range(2)]
    rt_b = [msb(f"rt_b{i}", (128, H * 128), BF16) for i in range(2)]
    o_nw = [msb(f"o_nw{i}", (128, H * DV), BF16) for i in range(2)]

    s_x = sems("s_x")
    s_wm = sems("s_wm")
    s_ld = sems("s_ld")
    s_bc = sems("s_bc")
    s_cast = sems("s_cast")
    s_stat = sems("s_stat")
    s_nrm = sems("s_nrm")
    s_trn = sems("s_trn")
    s_trnv = sems("s_trnv")
    s_pqk = sems("s_pqk")
    s_sqk = sems("s_sqk")
    s_l2 = sems("s_l2")
    s_tqk = sems("s_tqk")
    s_eqk = sems("s_eqk")
    s_pv = sems("s_pv")
    s_sv = sems("s_sv")
    s_bet = sems("s_bet")
    s_kv = sems("s_kv")
    s_vd = sems("s_vd")
    s_A = sems("s_A")
    s_M = sems("s_M")
    s_Mt = sems("s_Mt")
    s_it = sems("s_it")
    s_iv = sems("s_iv")
    s_U = sems("s_U")
    s_Ue = sems("s_Ue")
    s_G = sems("s_G")
    s_Gm = sems("s_Gm")
    s_O = sems("s_O")
    s_oe = sems("s_oe")
    s_ow = [sems("s_ow0"), sems("s_ow1")]
    s_xw = [sems("s_xw0"), sems("s_xw1")]
    s_to = sems("s_to")
    s_toe = sems("s_toe")
    s_wo = sems("s_wo")
    s_s = sems("s_s")
    s_sw = [sems("s_sw0"), sems("s_sw1")]
    s_su = sems("s_su")
    s_dbg = sems("s_dbg")
    s_dbg2 = sems("s_dbg2")

    done = {k: {} for k in ("nrm", "trnv", "l2", "eqk", "su", "oe", "sv", "bet",
                            "sqk", "kl2", "ktr", "A", "Mt", "vd", "ow", "xw",
                            "G", "O", "stat")}

    def cast_load(dst, src, cdim):
        view = src.rearrange("(c p) n -> p c n", p=128)
        p.op("gpsimd", lambda e: e.dma_start(
            out=dst[:].rearrange("p (c n) -> p c n", c=cdim), in_=view),
            inc=s_wm, inc_k=16)

    cast_load(wq_sb, Wq, 8)
    cast_load(wk_sb, Wk, 8)
    cast_load(wv_sb, Wv, 8)
    cast_load(wb_sb, Wb, 8)
    wm_done = s_wm.n

    p.op("sync", lambda e: e.dma_start(out=lnrow[0:1, 0:D], in_=ln_w), inc=s_ld, inc_k=16)
    p.op("sync", lambda e: e.dma_start(out=lnrow[0:1, D:2 * D], in_=ln_b),
         inc=s_ld, inc_k=16)
    p.op("sync", lambda e: e.dma_start(out=lnrow[0:1, 2 * D:2 * D + DV], in_=onw),
         inc=s_ld, inc_k=16)

    p.wait("tensor", s_ld, s_ld.n)
    p.wait("tensor", s_init)
    for dst, src0, ng in ((lnw_bc, 0, 2), (lnb_bc, D, 2), (onw_bc, 2 * D, 1)):
        for g in range(ng):
            p.op("tensor", lambda e, g=g, src0=src0: e.matmul(
                PB[0][:, 0:512], ones_row[:],
                lnrow[0:1, src0 + g * 512:src0 + (g + 1) * 512],
                start=True, stop=True), inc=s_bc)
            p.wait("vector", s_bc)
            p.op("vector", lambda e, g=g, dst=dst: e.tensor_copy(
                dst[:, g * 512:(g + 1) * 512], PB[0][:, 0:512]), inc=s_bc)
            p.wait("tensor", s_bc)
    bc_done = s_bc.n

    s_z = sems("s_z")
    for h in range(H):
        p.op("vector", lambda e, h=h: e.memset(S_b[h][:], 0.0),
             inc=(s_z if h == H - 1 else None))

    for ei in range(NE):
        p.op("gpsimd", lambda e, ei=ei: e.dma_start(
            out=wbf_up[ei].rearrange("(a b) -> a b", a=II),
            in_=upw[ei].rearrange("d i -> (d i)").rearrange("(a b) -> a b", a=II)),
            inc=s_cast, inc_k=16)
    p.op("gpsimd", lambda e: e.dma_start(
        out=wbf_su[0].rearrange("(a b) -> a b", a=II),
        in_=sw_up.rearrange("d i -> (d i)").rearrange("(a b) -> a b", a=II)),
        inc=s_cast, inc_k=16)
    for ei in range(NE):
        for dt in range(8):
            p.op("gpsimd", lambda e, ei=ei, dt=dt: e.dma_start(
                out=wbf_dn[ei][dt].rearrange("(q c j) -> q c j", q=128, c=32),
                in_=dnw[ei].rearrange("(c q) (dt j) -> dt q c j", q=128, j=128)[dt]),
                inc=s_cast, inc_k=16)
    for dt in range(8):
        p.op("gpsimd", lambda e, dt=dt: e.dma_start(
            out=wbf_sd[0][dt].rearrange("(q c j) -> q c j", q=128, c=32),
            in_=sw_dn.rearrange("(c q) (dt j) -> dt q c j", q=128, j=128)[dt]),
            inc=s_cast, inc_k=16)
    cast_done = s_cast.n

    def front(c):
        pe = c % 2
        if c >= 2:
            p.wait("sync", s_nrm, done["nrm"][c - 2])
        p.op("sync", lambda e, c=c, pe=pe: e.dma_start(
            out=xch[pe][:], in_=x_full[c * 128:(c + 1) * 128, :]), inc=s_x, inc_k=16)
        xld = s_x.n

        p.wait("scalar", s_x, xld)
        if c == 0:
            p.wait("scalar", s_init)
        if c >= 1:
            p.wait("scalar", s_l2, done["kl2"][c - 1])
        p.op("scalar", lambda e, pe=pe: e.activation(sqch[:], xch[pe][:], AF.Square),
             inc=s_stat)
        p.wait("vector", s_x, xld)
        p.op("vector", lambda e, pe=pe: e.reduce_sum(stat[:, 0:1], xch[pe][:], axis=AX.X))
        p.wait("vector", s_stat, s_stat.n)
        p.op("vector", lambda e: e.reduce_sum(stat[:, 1:2], sqch[:], axis=AX.X),
             inc=s_stat)
        p.wait("vector", s_stat, s_stat.n)
        p.op("vector", lambda e: e.tensor_scalar(out=stat[:, 2:3], in0=stat[:, 0:1],
                                                 scalar1=1.0 / D, scalar2=None,
                                                 op0=ALU.mult), inc=s_stat)
        p.wait("vector", s_stat, s_stat.n)
        p.op("vector", lambda e: e.tensor_tensor(stat[:, 3:4], stat[:, 2:3], stat[:, 2:3],
                                                 op=ALU.mult), inc=s_stat)
        p.wait("vector", s_stat, s_stat.n)
        p.op("vector", lambda e: e.scalar_tensor_tensor(
            out=stat[:, 4:5], in0=stat[:, 1:2], scalar=1.0 / D, in1=stat[:, 3:4],
            op0=ALU.mult, op1=ALU.subtract), inc=s_stat)
        p.wait("vector", s_stat, s_stat.n)
        p.op("vector", lambda e: e.tensor_scalar(out=stat[:, 5:6], in0=stat[:, 4:5],
                                                 scalar1=1e-5, scalar2=None, op0=ALU.add))
        p.op("vector", lambda e: e.tensor_scalar(out=stat[:, 6:7], in0=stat[:, 4:5],
                                                 scalar1=1.00001, scalar2=1e-10,
                                                 op0=ALU.mult, op1=ALU.add), inc=s_stat)
        p.wait("vector", s_stat, s_stat.n)
        _rsqrt(p, stat, 5, 2, nsc, cfg.newton)
        p.op("vector", lambda e: e.tensor_tensor(stat[:, 8:9], stat[:, 2:3], stat[:, 5:6],
                                                 op=ALU.mult))
        p.op("vector", lambda e: e.tensor_tensor(stat[:, 9:10], stat[:, 2:3], stat[:, 6:7],
                                                 op=ALU.mult), inc=s_stat)
        p.wait("vector", s_stat, s_stat.n)
        done["stat"][c] = s_stat.n

        if c >= 2:
            p.wait("vector", s_xw[pe], done["xw"][c - 2])
        if dbg:
            p.wait("vector", s_dbg, s_dbg.n)
        p.op("vector", lambda e, pe=pe: e.tensor_scalar(
            out=xnch[pe][:], in0=xch[pe][:], scalar1=stat[:, 5:6], scalar2=stat[:, 8:9],
            op0=ALU.mult, op1=ALU.subtract))
        p.op("vector", lambda e, pe=pe: e.tensor_scalar(
            out=nrm0[:], in0=xch[pe][:], scalar1=stat[:, 6:7], scalar2=stat[:, 9:10],
            op0=ALU.mult, op1=ALU.subtract), inc=s_nrm)
        p.wait("vector", s_nrm, s_nrm.n)
        if c == 0:
            p.wait("vector", s_bc, bc_done)
        p.op("vector", lambda e: e.tensor_tensor(nrm0[:], nrm0[:], lnw_bc[:], op=ALU.mult),
             inc=s_nrm)
        p.wait("vector", s_nrm, s_nrm.n)
        if c >= 2:
            p.wait("vector", s_trn, done["trnv"][c - 2])
        p.op("vector", lambda e, pe=pe: e.tensor_tensor(nrmb[pe][:], nrm0[:], lnb_bc[:],
                                                        op=ALU.add), inc=s_nrm)
        done["nrm"][c] = s_nrm.n
        p.wait("sync", s_nrm, done["nrm"][c])
        p.op("sync", lambda e, c=c, pe=pe: e.dma_start(
            out=xn_scr[c * 128:(c + 1) * 128, :], in_=xnch[pe][:]), inc=s_xw[pe],
            inc_k=16)
        done["xw"][c] = s_xw[pe].n
        if "dbg_xn" in dbg:
            p.op("sync", lambda e, c=c, pe=pe: e.dma_start(
                out=dbg["dbg_xn"][c * 128:(c + 1) * 128, :], in_=xnch[pe][:]),
                inc=s_dbg, inc_k=16)

        if c == 0:
            p.wait("tensor", s_wm, wm_done)
        if c >= 2:
            p.wait("tensor", s_trnv, done["trnv"][c - 2])
        p.wait("tensor", s_nrm, done["nrm"][c])
        for g in range(8):
            p.op("tensor", lambda e, g=g, pe=pe: e.transpose(
                PT[:, g * 128:(g + 1) * 128], nrmb[pe][:, g * 128:(g + 1) * 128],
                ident_b[:]), inc=(s_trn if g == 7 else None))
        p.wait("vector", s_trn, s_trn.n)
        p.op("vector", lambda e, pe=pe: e.tensor_copy(nT[pe][:], PT[:]), inc=s_trnv)
        done["trnv"][c] = s_trnv.n

        p.wait("tensor", s_trnv, done["trnv"][c])
        if c >= 2:
            p.wait("tensor", s_oe, done["oe"][c - 2])
        pqk_base = s_pqk.n
        for g in range(2):
            for cc in range(8):
                p.op("tensor", lambda e, g=g, cc=cc, pe=pe: e.matmul(
                    PB[2 + g][:, 0:512], nT[pe][:, cc * 128:(cc + 1) * 128],
                    wk_sb[:, cc * 1024 + g * 512:cc * 1024 + (g + 1) * 512],
                    start=(cc == 0), stop=(cc == 7)), inc=(s_pqk if cc == 7 else None))
        if c >= 1:
            p.wait("scalar", s_l2, done["l2"][c - 1])
        for g in range(2):
            p.wait("scalar", s_pqk, pqk_base + g + 1)
            p.op("scalar", lambda e, g=g: e.activation(
                kraw[0][:, g * 512:(g + 1) * 512], PB[2 + g][:, 0:512], AF.Silu),
                inc=(s_sqk if g == 1 else None))
        p.wait("scalar", s_sqk, s_sqk.n)
        p.op("scalar", lambda e: e.activation(nrm0[:], kraw[0][:], AF.Square), inc=s_sqk)
        done["sqk"][c] = s_sqk.n
        p.wait("vector", s_sqk, s_sqk.n)
        for h in range(H):
            p.op("vector", lambda e, h=h: e.reduce_sum(
                stat[:, 14 + h:15 + h], nrm0[:, h * DK:(h + 1) * DK], axis=AX.X),
                inc=(s_l2 if h == H - 1 else None))
        p.wait("vector", s_l2, s_l2.n)
        p.op("vector", lambda e: e.tensor_scalar(out=stat[:, 14:18], in0=stat[:, 14:18],
                                                 scalar1=1e-6, scalar2=None, op0=ALU.add),
             inc=s_l2)
        p.wait("vector", s_l2, s_l2.n)
        _rsqrt(p, stat, 14, 4, nsc, cfg.newton)
        if c >= 2:
            p.wait("vector", s_su, done["su"][c - 2])
        if dbg:
            p.wait("vector", s_dbg2, s_dbg2.n)
        for h in range(H):
            p.op("vector", lambda e, h=h, pe=pe: e.tensor_scalar(
                out=kn[pe][:, h * DK:(h + 1) * DK], in0=kraw[0][:, h * DK:(h + 1) * DK],
                scalar1=stat[:, 14 + h:15 + h], scalar2=None, op0=ALU.mult),
                inc=(s_l2 if h == H - 1 else None))
        done["kl2"][c] = s_l2.n
        p.wait("tensor", s_l2, done["kl2"][c])
        if c >= 2:
            p.wait("tensor", s_G, done["G"][c - 2])
        for g8 in range(8):
            p.op("tensor", lambda e, g8=g8, pe=pe: e.transpose(
                PT[:, g8 * 128:(g8 + 1) * 128], kn[pe][:, g8 * 128:(g8 + 1) * 128],
                ident_b[:]), inc=(s_tqk if g8 == 7 else None))
        p.wait("vector", s_tqk, s_tqk.n)
        p.op("vector", lambda e, pe=pe: e.tensor_copy(kT[pe][:], PT[:]), inc=s_eqk)
        done["ktr"][c] = s_eqk.n
        p.wait("tensor", s_eqk, done["ktr"][c])
        p.wait("tensor", s_iv, s_iv.n)
        for h in range(H):
            for g in range(2):
                p.op("tensor", lambda e, h=h, g=g, pe=pe: e.matmul(
                    PI[:, h * 128:(h + 1) * 128],
                    kT[pe][:, (h * 2 + g) * 128:(h * 2 + g + 1) * 128],
                    kT[pe][:, (h * 2 + g) * 128:(h * 2 + g + 1) * 128],
                    start=(g == 0), stop=(g == 1)), inc=(s_A if g == 1 else None))
        done["A"][c] = s_A.n
        s_pb_base = s_pv.n
        if c >= 2:
            p.wait("vector", s_vd, done["vd"][c - 2])
        for cc in range(8):
            p.op("tensor", lambda e, cc=cc, pe=pe: e.matmul(
                PI[:, 1024:1024 + H], nT[pe][:, cc * 128:(cc + 1) * 128],
                wb_sb[:, cc * H:(cc + 1) * H], start=(cc == 0), stop=(cc == 7)),
                inc=(s_pv if cc == 7 else None))
        p.wait("scalar", s_pv, s_pb_base + 1)
        p.op("scalar", lambda e, pe=pe: e.activation(beta_t[pe][:, 0:H],
                                                     PI[:, 1024:1024 + H],
                                                     AF.Tanh, scale=0.5), inc=s_bet)
        p.wait("vector", s_bet, s_bet.n)
        p.op("vector", lambda e, pe=pe: e.tensor_scalar(
            out=beta_t[pe][:, 0:H], in0=beta_t[pe][:, 0:H], scalar1=0.5, scalar2=0.5,
            op0=ALU.mult, op1=ALU.add), inc=s_bet)
        p.wait("vector", s_bet, s_bet.n)
        p.op("vector", lambda e, pe=pe: e.tensor_scalar(
            out=beta_t[pe][:, H:2 * H], in0=beta_t[pe][:, 0:H], scalar1=-1.0,
            scalar2=None, op0=ALU.mult), inc=s_bet)
        done["bet"][c] = s_bet.n

        for g in range(2):
            for cc in range(8):
                p.op("tensor", lambda e, g=g, cc=cc, pe=pe: e.matmul(
                    PB[g][:, 0:512], nT[pe][:, cc * 128:(cc + 1) * 128],
                    wq_sb[:, cc * 1024 + g * 512:cc * 1024 + (g + 1) * 512],
                    start=(cc == 0), stop=(cc == 7)), inc=(s_pqk if cc == 7 else None))
        for g in range(2):
            p.wait("scalar", s_pqk, pqk_base + 2 + g + 1)
            p.op("scalar", lambda e, g=g: e.activation(
                qraw[0][:, g * 512:(g + 1) * 512], PB[g][:, 0:512], AF.Silu),
                inc=(s_sqk if g == 1 else None))
        p.wait("scalar", s_sqk, s_sqk.n)
        p.op("scalar", lambda e: e.activation(sqch[:], qraw[0][:], AF.Square), inc=s_sqk)
        p.wait("vector", s_sqk, s_sqk.n)
        for h in range(H):
            p.op("vector", lambda e, h=h: e.reduce_sum(
                stat[:, 10 + h:11 + h], sqch[:, h * DK:(h + 1) * DK], axis=AX.X),
                inc=(s_l2 if h == H - 1 else None))
        p.wait("vector", s_l2, s_l2.n)
        p.op("vector", lambda e: e.tensor_scalar(out=stat[:, 10:14], in0=stat[:, 10:14],
                                                 scalar1=1e-6, scalar2=None, op0=ALU.add),
             inc=s_l2)
        p.wait("vector", s_l2, s_l2.n)
        _rsqrt(p, stat, 10, 4, nsc, cfg.newton)
        p.op("vector", lambda e: e.tensor_scalar(out=stat[:, 10:14], in0=stat[:, 10:14],
                                                 scalar1=DK ** -0.5, scalar2=None,
                                                 op0=ALU.mult), inc=s_l2)
        p.wait("vector", s_l2, s_l2.n)
        if c >= 2:
            p.wait("vector", s_O, done["O"][c - 2])
        for h in range(H):
            p.op("vector", lambda e, h=h, pe=pe: e.tensor_scalar(
                out=qn[pe][:, h * DK:(h + 1) * DK], in0=qraw[0][:, h * DK:(h + 1) * DK],
                scalar1=stat[:, 10 + h:11 + h], scalar2=None, op0=ALU.mult),
                inc=(s_l2 if h == H - 1 else None))
        done["l2"][c] = s_l2.n
        if "dbg_q" in dbg:
            p.wait("gpsimd", s_l2, done["l2"][c])
            p.op("gpsimd", lambda e, c=c, pe=pe: e.dma_start(
                out=dbg["dbg_q"][c * 128:(c + 1) * 128, :], in_=qn[pe][:]),
                inc=s_dbg2, inc_k=16)
            p.op("gpsimd", lambda e, c=c, pe=pe: e.dma_start(
                out=dbg["dbg_k"][c * 128:(c + 1) * 128, :], in_=kn[pe][:]),
                inc=s_dbg2, inc_k=16)
        p.wait("vector", s_A, done["A"][c])
        p.wait("vector", s_bet, done["bet"][c])
        p.wait("vector", s_it, s_it.n)
        for h in range(H):
            p.op("vector", lambda e, h=h, pe=pe: e.scalar_tensor_tensor(
                out=m_f[pe][:, h * 128:(h + 1) * 128], in0=PI[:, h * 128:(h + 1) * 128],
                scalar=beta_t[pe][:, H + h:H + h + 1], in1=tril01[:],
                op0=ALU.mult, op1=ALU.mult), inc=(s_M if h == H - 1 else None))
        p.wait("tensor", s_M, s_M.n)
        for h in range(H):
            p.op("tensor", lambda e, h=h, pe=pe: e.transpose(
                PI[:, 512 + h * 128:512 + (h + 1) * 128],
                m_f[pe][:, h * 128:(h + 1) * 128],
                ident_f[:]), inc=(s_Mt if h == H - 1 else None))
        p.wait("vector", s_Mt, s_Mt.n)
        p.op("vector", lambda e, pe=pe: e.tensor_copy(mt_f[pe][:], PI[:, 512:1024]),
             inc=s_Mt)
        p.wait("vector", s_Mt, s_Mt.n)
        for h in range(H):
            p.op("vector", lambda e, h=h, pe=pe: e.tensor_tensor(
                rt_f[pe][:, h * 128:(h + 1) * 128], mt_f[pe][:, h * 128:(h + 1) * 128],
                ident_f[:], op=ALU.add), inc=(s_Mt if h == H - 1 else None))
        done["Mt"][c] = s_Mt.n

        p.wait("tensor", s_l2, done["l2"][c])
        for g8 in range(8):
            p.op("tensor", lambda e, g8=g8, pe=pe: e.transpose(
                PT[:, g8 * 128:(g8 + 1) * 128], qn[pe][:, g8 * 128:(g8 + 1) * 128],
                ident_b[:]), inc=(s_tqk if g8 == 7 else None))
        p.wait("vector", s_tqk, s_tqk.n)
        p.op("vector", lambda e, pe=pe: e.tensor_copy(qT[pe][:], PT[:]), inc=s_eqk)
        done["eqk"][c] = s_eqk.n

        p.wait("tensor", s_sqk, done["sqk"][c])
        if c >= 2:
            p.wait("tensor", s_vd, done["vd"][c - 2])
        pv_base = s_pv.n
        for g in range(4):
            for cc in range(8):
                p.op("tensor", lambda e, g=g, cc=cc, pe=pe: e.matmul(
                    PB[g][:, 0:512], nT[pe][:, cc * 128:(cc + 1) * 128],
                    wv_sb[:, cc * 2048 + g * 512:cc * 2048 + (g + 1) * 512],
                    start=(cc == 0), stop=(cc == 7)), inc=(s_pv if cc == 7 else None))
        if c >= 2:
            p.wait("scalar", s_vd, done["vd"][c - 2])
        if dbg:
            p.wait("scalar", s_dbg, s_dbg.n)
        for g in range(4):
            p.wait("scalar", s_pv, pv_base + g + 1)
            p.op("scalar", lambda e, g=g, pe=pe: e.activation(
                vsil[pe][:, g * 512:(g + 1) * 512], PB[g][:, 0:512], AF.Silu),
                inc=(s_sv if g == 3 else None))
        done["sv"][c] = s_sv.n

        if "dbg_v" in dbg:
            p.wait("sync", s_sv, done["sv"][c])
            p.op("sync", lambda e, c=c, pe=pe: e.dma_start(
                out=dbg["dbg_v"][c * 128:(c + 1) * 128, :], in_=vsil[pe][:]),
                inc=s_dbg, inc_k=16)
        if "dbg_beta" in dbg:
            p.wait("sync", s_bet, done["bet"][c])
            p.op("sync", lambda e, c=c, pe=pe: e.dma_start(
                out=dbg["dbg_beta"][c * 128:(c + 1) * 128, :], in_=beta_t[pe][:, 0:H]),
                inc=s_dbg, inc_k=16)

    def inv_level(cn, lev):
        pn = cn % 2
        last = lev == 6
        if lev == 1:
            p.wait("tensor", s_bet, s_bet.n)
            p.wait("tensor", s_Mt, s_Mt.n)
        for h in range(H):
            p.op("tensor", lambda e, h=h, pn=pn: e.matmul(
                PI[:, h * 128:(h + 1) * 128], mt_f[pn][:, h * 128:(h + 1) * 128],
                m_f[pn][:, h * 128:(h + 1) * 128], start=True, stop=True),
                inc=(s_it if (last and h == H - 1) else None))
        if not last:
            for h in range(H):
                p.op("tensor", lambda e, h=h, pn=pn: e.matmul(
                    PI[:, 512 + h * 128:512 + (h + 1) * 128],
                    m_f[pn][:, h * 128:(h + 1) * 128],
                    mt_f[pn][:, h * 128:(h + 1) * 128],
                    start=True, stop=True), inc=(s_it if h == H - 1 else None))
        p.wait("vector", s_it, s_it.n)
        p.op("vector", lambda e, pn=pn: e.tensor_copy(x2_f[pn][:], PI[:, 0:512]),
             inc=s_iv)
        if not last:
            p.op("vector", lambda e, pn=pn: e.tensor_copy(mt_f[pn][:], PI[:, 512:1024]),
                 inc=s_iv)
        p.wait("tensor", s_iv, s_iv.n)
        for h in range(H):
            p.op("tensor", lambda e, h=h, pn=pn: e.matmul(
                PI[:, 1024 + h * 128:1024 + (h + 1) * 128],
                x2_f[pn][:, h * 128:(h + 1) * 128], rt_f[pn][:, h * 128:(h + 1) * 128],
                start=True, stop=True), inc=(s_it if h == H - 1 else None))
        p.wait("vector", s_it, s_it.n)
        for h in range(H):
            p.op("vector", lambda e, h=h, pn=pn: e.tensor_tensor(
                rt_f[pn][:, h * 128:(h + 1) * 128], rt_f[pn][:, h * 128:(h + 1) * 128],
                PI[:, 1024 + h * 128:1024 + (h + 1) * 128], op=ALU.add),
                inc=(s_iv if h == H - 1 else None))
        if not last:
            p.op("vector", lambda e, pn=pn: e.tensor_copy(m_f[pn][:], x2_f[pn][:]),
                 inc=s_iv)
        p.wait("tensor", s_iv, s_iv.n)
        if last:
            p.wait("vector", s_iv, s_iv.n)
            p.wait("vector", s_U, s_U.n)
            p.op("vector", lambda e, pn=pn: e.tensor_copy(rt_b[pn][:], rt_f[pn][:]),
                 inc=s_iv)
            inv_done[cn] = s_iv.n

    def rec(c):
        pe = c % 2
        cn = c + 1 if c + 1 < NCH else None
        p.wait("tensor", s_sv, s_sv.n)
        p.wait("tensor", s_eqk, done["eqk"][c])
        if c >= 1:
            p.wait("tensor", s_su, done["su"][c - 1])
        for h in range(H):
            for g in range(2):
                p.op("tensor", lambda e, h=h, g=g, pe=pe: e.matmul(
                    PB[h][:, 0:512], kT[pe][:, (h * 2 + g) * 128:(h * 2 + g + 1) * 128],
                    S_b[h][:, g * 512:(g + 1) * 512], start=(g == 0), stop=(g == 1)),
                    inc=(s_kv if g == 1 else None))
        kv_now = s_kv.n
        p.wait("vector", s_kv, kv_now)
        p.wait("vector", s_sv, done["sv"][c])
        for h in range(H):
            p.op("vector", lambda e, h=h, pe=pe: e.tensor_tensor(
                osq[:, (h % 2) * DV:((h % 2) + 1) * DV],
                vsil[pe][:, h * DV:(h + 1) * DV],
                PB[h][:, 0:512], op=ALU.subtract), inc=s_vd)
            if h == 0 and c >= 1:
                p.wait("vector", s_U, u_done_prev[0])
            p.wait("vector", s_vd, s_vd.n)
            p.op("vector", lambda e, h=h, pe=pe: e.tensor_scalar(
                out=vd_b[0][:, h * DV:(h + 1) * DV],
                in0=osq[:, (h % 2) * DV:((h % 2) + 1) * DV],
                scalar1=beta_t[pe][:, h:h + 1], scalar2=None, op0=ALU.mult),
                inc=(s_vd if h == H - 1 else None))
        done["vd"][c] = s_vd.n
        if cn is not None:
            inv_level(cn, 1)
        p.wait("tensor", s_vd, done["vd"][c])
        p.wait("tensor", s_iv, inv_done[c])
        for h in range(H):
            p.op("tensor", lambda e, h=h, pe=pe: e.matmul(
                PB[h][:, 0:512], rt_b[pe][:, h * 128:(h + 1) * 128],
                vd_b[0][:, h * DV:(h + 1) * DV], start=True, stop=True),
                inc=(s_U if h == H - 1 else None))
        u_done_prev[0] = s_U.n
        p.wait("vector", s_U, s_U.n)
        for h in range(H):
            p.op("vector", lambda e, h=h: e.tensor_copy(
                u_b[:, h * DV:(h + 1) * DV], PB[h][:, 0:512]),
                inc=(s_Ue if h == H - 1 else None))
        if cn is not None:
            inv_level(cn, 2)
        p.wait("tensor", s_Ue, s_Ue.n)
        for h in range(H):
            for g in range(2):
                p.op("tensor", lambda e, h=h, g=g, pe=pe: e.matmul(
                    PB[(h % 2) * 2 + g][:, 0:512],
                    kn[pe][:, h * DK + g * 128:h * DK + (g + 1) * 128],
                    u_b[:, h * DV:(h + 1) * DV], start=True, stop=True),
                    inc=(s_kv if g == 1 else None))
            p.wait("vector", s_kv, s_kv.n)
            for g in range(2):
                p.op("vector", lambda e, h=h, g=g: e.tensor_tensor(
                    S_b[h][:, g * 512:(g + 1) * 512], S_b[h][:, g * 512:(g + 1) * 512],
                    PB[(h % 2) * 2 + g][:, 0:512], op=ALU.add),
                    inc=(s_su if g == 1 else None))
            if h % 2 == 1 and h < H - 1:
                p.wait("tensor", s_su, s_su.n)
        done["su"][c] = s_su.n
        if cn is not None:
            inv_level(cn, 3)
        p.wait("tensor", s_su, done["su"][c])
        for h in range(H):
            for g in range(2):
                p.op("tensor", lambda e, h=h, g=g, pe=pe: e.matmul(
                    PB[h][:, 0:128],
                    kT[pe][:, (h * 2 + g) * 128:(h * 2 + g + 1) * 128],
                    qT[pe][:, (h * 2 + g) * 128:(h * 2 + g + 1) * 128],
                    start=(g == 0), stop=(g == 1)), inc=(s_G if g == 1 else None))
        done["G"][c] = s_G.n
        p.wait("vector", s_G, s_G.n)
        for h in range(H):
            p.op("vector", lambda e, h=h: e.scalar_tensor_tensor(
                out=gm_b[:, h * 128:(h + 1) * 128], in0=PB[h][:, 0:128],
                scalar=-1.0, in1=tril01[:], op0=ALU.mult, op1=ALU.mult),
                inc=(s_Gm if h == H - 1 else None))
        if cn is not None:
            inv_level(cn, 4)
        p.wait("tensor", s_Gm, s_Gm.n)
        for h in range(H):
            for g in range(2):
                p.op("tensor", lambda e, h=h, g=g, pe=pe: e.matmul(
                    PB[h][:, 0:512], qT[pe][:, (h * 2 + g) * 128:(h * 2 + g + 1) * 128],
                    S_b[h][:, g * 512:(g + 1) * 512], start=(g == 0), stop=False))
            p.op("tensor", lambda e, h=h: e.matmul(
                PB[h][:, 0:512], gm_b[:, h * 128:(h + 1) * 128],
                u_b[:, h * DV:(h + 1) * DV], start=False, stop=True),
                inc=(s_O if h == H - 1 else None))
        done["O"][c] = s_O.n
        if cn is not None:
            inv_level(cn, 5)
        p.wait("scalar", s_O, done["O"][c])
        for h in range(H):
            p.op("scalar", lambda e, h=h: e.activation(
                osq[:, (h % 2) * DV:((h % 2) + 1) * DV], PB[h][:, 0:512], AF.Square),
                inc=s_oe)
            p.wait("vector", s_oe, s_oe.n)
            p.op("vector", lambda e, h=h: e.reduce_sum(
                stat[:, 18 + h:19 + h], osq[:, (h % 2) * DV:((h % 2) + 1) * DV],
                axis=AX.X), inc=s_oe)
            if h < H - 1:
                p.wait("scalar", s_oe, s_oe.n)
        p.wait("vector", s_oe, s_oe.n)
        p.op("vector", lambda e: e.tensor_scalar(out=stat[:, 18:22], in0=stat[:, 18:22],
                                                 scalar1=1.0 / DV, scalar2=1e-5,
                                                 op0=ALU.mult, op1=ALU.add), inc=s_oe)
        p.wait("vector", s_oe, s_oe.n)
        _rsqrt(p, stat, 18, 4, nsc, cfg.newton)
        if c >= 2:
            p.wait("vector", s_ow[pe], done["ow"][c - 2])
        if dbg:
            p.wait("vector", s_dbg2, s_dbg2.n)
        for h in range(H):
            p.op("vector", lambda e, h=h, pe=pe: e.scalar_tensor_tensor(
                out=o_nw[pe][:, h * DV:(h + 1) * DV], in0=PB[h][:, 0:512],
                scalar=stat[:, 18 + h:19 + h], in1=onw_bc[:],
                op0=ALU.mult, op1=ALU.mult), inc=(s_oe if h == H - 1 else None))
        done["oe"][c] = s_oe.n
        p.wait("sync", s_oe, done["oe"][c])
        p.op("sync", lambda e, c=c, pe=pe: e.dma_start(
            out=on_scr[c * 128:(c + 1) * 128, :], in_=o_nw[pe][:]), inc=s_ow[pe],
            inc_k=16)
        done["ow"][c] = s_ow[pe].n
        if "dbg_o" in dbg:
            p.wait("gpsimd", s_oe, done["oe"][c])
            p.op("gpsimd", lambda e, c=c, pe=pe: e.dma_start(
                out=dbg["dbg_o"][c * 128:(c + 1) * 128, :], in_=o_nw[pe][:]),
                inc=s_dbg2, inc_k=16)
        if cn is not None:
            inv_level(cn, 6)

    u_done_prev = [0]
    inv_done = {}
    front(0)
    for c in range(NCH):
        if c + 1 < NCH:
            front(c + 1)
        if c == 0:
            for lev in range(1, 7):
                inv_level(0, lev)
        rec(c)

    for eng in ("sync", "gpsimd"):
        p.wait(eng, s_oe, s_oe.n)
        p.wait(eng, s_su, s_su.n)
        p.wait(eng, s_O, s_O.n)
        p.wait(eng, s_iv, s_iv.n)
    p.wait("sync", s_x, s_x.n)
    mes.close()
    spes = ExitStack()

    def ssb(name, shape, dtype):
        return spes.enter_context(nc.sbuf_tensor(name, list(shape), dtype))

    wo_sb = ssb("wo_sb", (128, 16 * 1024), BF16)
    o_in = [ssb(f"o_in{i}", (128, H * DV), BF16) for i in range(2)]
    xn_in = [ssb(f"xn_in{i}", (128, D), F32) for i in range(2)]
    oT_b = [ssb(f"oT_b{i}", (128, 2048), BF16) for i in range(2)]
    s_out_b = [ssb(f"s_out_b{i}", (128, D), F32) for i in range(2)]

    s_oin = [sems("s_oin0"), sems("s_oin1")]
    s_xin = [sems("s_xin0"), sems("s_xin1")]
    p.op("gpsimd", lambda e: e.dma_start(
        out=wo_sb[:].rearrange("p (c n) -> p c n", c=16),
        in_=Wo.rearrange("(c p) n -> p c n", p=128)), inc=s_wm, inc_k=16)
    p.wait("tensor", s_wm, s_wm.n)
    done_sp = {}
    for c in range(NCH):
        pe = c % 2
        if c >= 2:
            p.wait("sync", s_toe, done_sp[c - 2][0])
            p.wait("sync", s_s, done_sp[c - 2][1])
            p.wait("sync", s_sw[pe], done_sp[c - 2][2])
        p.wait("sync", s_ow[pe], done["ow"][c])
        p.wait("sync", s_xw[pe], done["xw"][c])
        p.op("sync", lambda e, c=c, pe=pe: e.dma_start(
            out=o_in[pe][:], in_=on_scr[c * 128:(c + 1) * 128, :]),
            inc=s_oin[pe], inc_k=16)
        p.op("sync", lambda e, c=c, pe=pe: e.dma_start(
            out=xn_in[pe][:], in_=xn_scr[c * 128:(c + 1) * 128, :]),
            inc=s_xin[pe], inc_k=16)
        oin_d = s_oin[pe].n
        xin_d = s_xin[pe].n
        p.wait("tensor", s_oin[pe], oin_d)
        for grp in range(2):
            if not (c == 0 and grp == 0):
                p.wait("tensor", s_toe, s_toe.n)
            for j in range(8):
                p.op("tensor", lambda e, grp=grp, j=j, pe=pe: e.transpose(
                    PT[:, j * 128:(j + 1) * 128],
                    o_in[pe][:, (grp * 8 + j) * 128:(grp * 8 + j + 1) * 128],
                    ident_b[:]), inc=(s_to if j == 7 else None))
            p.wait("vector", s_to, s_to.n)
            p.op("vector", lambda e, grp=grp, pe=pe: e.tensor_copy(
                oT_b[pe][:, grp * 1024:(grp + 1) * 1024], PT[:]), inc=s_toe)
        toe_d = s_toe.n
        p.wait("tensor", s_toe, toe_d)
        if c >= 1:
            p.wait("tensor", s_s, s_s.n)
        for g in range(2):
            for cc in range(16):
                p.op("tensor", lambda e, g=g, cc=cc, pe=pe: e.matmul(
                    PB[g][:, 0:512], oT_b[pe][:, cc * 128:(cc + 1) * 128],
                    wo_sb[:, cc * 1024 + g * 512:cc * 1024 + (g + 1) * 512],
                    start=(cc == 0), stop=(cc == 15)), inc=(s_wo if cc == 15 else None))
        p.wait("vector", s_wo, s_wo.n)
        p.wait("vector", s_xin[pe], xin_d)
        if c >= 2:
            p.wait("vector", s_sw[pe], done_sp[c - 2][2])
        if "dbg_s" in dbg:
            p.wait("vector", s_dbg, s_dbg.n)
        for g in range(2):
            p.op("vector", lambda e, g=g, pe=pe: e.tensor_tensor(
                s_out_b[pe][:, g * 512:(g + 1) * 512],
                xn_in[pe][:, g * 512:(g + 1) * 512],
                PB[g][:, 0:512], op=ALU.add), inc=(s_s if g == 1 else None))
        p.wait("sync", s_s, s_s.n)
        p.op("sync", lambda e, c=c, pe=pe: e.dma_start(
            out=s_scr[c * 128:(c + 1) * 128, :], in_=s_out_b[pe][:]), inc=s_sw[pe],
            inc_k=16)
        done_sp[c] = (s_to.n, s_s.n, s_sw[pe].n)
        if "dbg_s" in dbg:
            p.op("sync", lambda e, c=c, pe=pe: e.dma_start(
                out=dbg["dbg_s"][c * 128:(c + 1) * 128, :], in_=s_out_b[pe][:]),
                inc=s_dbg, inc_k=16)
    spes.close()
    sw_all = (s_sw[0].n, s_sw[1].n)
    p.wait("sync", s_ow[0], s_ow[0].n)
    p.wait("sync", s_ow[1], s_ow[1].n)
    p.wait("sync", s_xw[0], s_xw[0].n)
    p.wait("sync", s_xw[1], s_xw[1].n)
    p.wait("sync", s_dbg, s_dbg.n)
    p.wait("gpsimd", s_wm, s_wm.n)
    p.wait("gpsimd", s_dbg2, s_dbg2.n)
    ph1_waits = [(s_s, s_s.n), (s_wo, s_wo.n), (s_toe, s_toe.n),
                 (s_sw[0], s_sw[0].n), (s_sw[1], s_sw[1].n),
                 (s_oin[0], s_oin[0].n), (s_oin[1], s_oin[1].n),
                 (s_xin[0], s_xin[0].n), (s_xin[1], s_xin[1].n)]

    oes = ExitStack()

    def osb(name, shape, dtype):
        return oes.enter_context(nc.sbuf_tensor(name, list(shape), dtype))

    xnT_st = [osb(f"xnT_st{i}", (128, 8 * STOK), BF16) for i in range(2)]
    comb = osb("comb", (128, 8 * STOK), BF16)
    hact = osb("hact", (128, 32 * STOK), BF16)
    ubuf = [osb(f"ubuf{i}", (128, 8 * 512), BF16) for i in range(2)]
    dbuf = [osb(f"dbuf{i}", (128, 32 * 128), BF16) for i in range(2)]
    gw_sb = osb("gw_sb", (128, 8 * E), F32)
    gb_bc = osb("gb_bc", (128, E), F32)
    upb_sb = osb("upb_sb", (128, NE * 32), F32)
    dnb_sb = osb("dnb_sb", (128, NE * 8), F32)
    sbu_sb = osb("sbu_sb", (128, 32), F32)
    sbd_sb = osb("sbd_sb", (128, 8), F32)
    xo_ch = [osb("xo_ch0", (128, D), F32)]
    xn_mo = osb("xn_mo", (128, D), F32)
    xnb_mo = osb("xnb_mo", (128, D), BF16)
    sq_mo = osb("sq_mo", (128, D), F32)
    xnf_T = osb("xnf_T", (128, 8 * 128), F32)
    sc_tm = osb("sc_tm", (128, E), F32)
    w_tm = osb("w_tm", (128, E), F32)
    wT_all = [osb(f"wT_all{i}", (8, STOK), F32) for i in range(2)]
    xo_ep = osb("xo_ep", (128, D), F32)
    rtm = osb("rtm", (128, STC), F32)
    rT_sb = osb("rT_sb", (16, 128), F32)
    wrow = osb("wrow", (1, STOK), F32)
    wbc = osb("wbc", (128, 1024), F32)
    tmp_ev = osb("tmp_ev", (128, 512), F32)
    mx8 = osb("mx8", (128, STOK), F32)
    rbc = osb("rbc", (128, STOK), F32)
    graw = osb("graw", (128, max(STOK, D)), F32)
    zgT = osb("zgT", (128, D), BF16)
    sch = [osb("sch0", (128, D), F32)]

    s_mo = sems("s_mo")
    s_xo = [sems("s_xo0"), sems("s_xo1")]
    s_ln2 = sems("s_ln2")
    s_xb = sems("s_xb")
    s_tx = sems("s_tx")
    s_txe = sems("s_txe")
    s_gt = sems("s_gt")
    s_gs = sems("s_gs")
    s_gw = sems("s_gw")
    s_wup = [sems("s_wup0"), sems("s_wup1")]
    s_wdn = [sems("s_wdn0"), sems("s_wdn1")]
    s_mmu = sems("s_mmu")
    s_gel = sems("s_gel")
    s_mmd = sems("s_mmd")
    s_evm = sems("s_evm")
    s_eva = sems("s_eva")
    s_wb2 = sems("s_wb2")
    s_wrow = sems("s_wrow")
    s_n2 = sems("s_n2")
    s_n2g = sems("s_n2g")
    s_tz = sems("s_tz")
    s_zg = sems("s_zg")
    s_ep = sems("s_ep")
    s_epg = sems("s_epg")
    s_out = sems("s_out")
    s_cz = sems("s_cz")

    for eng in ENGINES:
        for sm, v in ph1_waits:
            p.wait(eng, sm, v)
    p.wait("sync", s_cast, cast_done)
    p.wait("scalar", s_cast, cast_done)

    p.op("sync", lambda e: e.dma_start(
        out=gw_sb[:].rearrange("p (c n) -> p c n", c=8),
        in_=gate_w.rearrange("(c p) n -> p c n", p=128)), inc=s_mo, inc_k=16)
    p.op("sync", lambda e: e.dma_start(out=upb_sb[:], in_=upb_t), inc=s_mo, inc_k=16)
    p.op("sync", lambda e: e.dma_start(out=dnb_sb[:], in_=dnb_t), inc=s_mo, inc_k=16)
    p.op("sync", lambda e: e.dma_start(out=sbu_sb[:], in_=sb_up_t), inc=s_mo, inc_k=16)
    p.op("sync", lambda e: e.dma_start(out=sbd_sb[:], in_=sb_dn_t), inc=s_mo, inc_k=16)
    p.op("sync", lambda e: e.dma_start(out=wrow[0:1, 0:E], in_=gate_b), inc=s_mo, inc_k=16)
    p.wait("tensor", s_mo, s_mo.n)
    s_gbb = sems("s_gbb")
    p.op("tensor", lambda e: e.matmul(PB[0][:, 0:E], ones_row[:], wrow[0:1, 0:E],
                                      start=True, stop=True), inc=s_gbb)
    p.wait("vector", s_gbb, s_gbb.n)
    p.op("vector", lambda e: e.tensor_copy(gb_bc[:], PB[0][:, 0:E]), inc=s_gbb)
    gbb_done = s_gbb.n

    done_gs = {}
    done_txe = {}
    up_ld = {}
    dn_ld = {}
    up_cons = {}
    dn_cons = {}
    out_cons = {}
    nup = 0
    ndn = 0
    gel_done = {}
    evm_done = {}

    def load_up(ei, grp):
        gi = ei * 4 + grp
        if gi >= 2:
            p.wait("sync", s_mmu, up_cons[gi - 2])
        src = wbf_su[0] if ei == NE else wbf_up[ei]
        p.op("sync", lambda e, grp=grp, src=src, gi=gi: e.dma_start(
            out=ubuf[gi % 2][:].rearrange("p (c n) -> p c n", c=8),
            in_=src.rearrange("(c p n) -> p c n", p=128, n=II)[:, :, grp * 1024:(grp + 1) * 1024]),
            inc=s_wup, inc_k=16)
        up_ld[gi] = s_wup.n

    def load_dn(ei, dt):
        di = ei * 8 + dt
        if di >= 2:
            p.wait("scalar", s_mmd, dn_cons[di - 2])
        srcd = wbf_sd[0] if ei == NE else wbf_dn[ei]
        p.op("scalar", lambda e, dt=dt, srcd=srcd, di=di: e.dma_start(
            out=dbuf[di % 2][:], in_=srcd[dt].rearrange("(q m) -> q m", q=128)),
            inc=s_wdn, inc_k=16)
        dn_ld[di] = s_wdn.n

    astate = {}

    def a_chunk(st, ci):
        c = st * STC + ci
        sp = st % 2
        if ci >= 1:
            p.wait("sync", s_gs, done_gs[(st, ci - 1)])
        elif st >= 1:
            p.wait("sync", s_gs, done_gs[(st - 1, STC - 1)])
        p.op("sync", lambda e, c=c: e.dma_start(
            out=xo_ch[0][:], in_=x_own[c * 128:(c + 1) * 128, :]),
            inc=s_xo[0], inc_k=16)
        xod = s_xo[0].n
        p.wait("scalar", s_xo[0], xod)

        if not (st == 0 and ci == 0):
            p.wait("scalar", s_ln2, astate["sq2"])
        if dbg:
            p.wait("scalar", s_mo, s_mo.n)
        p.op("scalar", lambda e: e.activation(sq_mo[:], xo_ch[0][:],
                                                AF.Square), inc=s_ln2)
        p.wait("vector", s_xo[0], xod)
        p.op("vector", lambda e: e.reduce_sum(stat[:, 0:1], xo_ch[0][:],
                                              axis=AX.X))
        p.wait("vector", s_ln2, s_ln2.n)
        p.op("vector", lambda e: e.reduce_sum(stat[:, 1:2], sq_mo[:], axis=AX.X),
             inc=s_ln2)
        astate['sq2'] = s_ln2.n
        p.wait("vector", s_ln2, s_ln2.n)
        p.op("vector", lambda e: e.tensor_scalar(out=stat[:, 2:3], in0=stat[:, 0:1],
                                                 scalar1=1.0 / D, scalar2=None,
                                                 op0=ALU.mult), inc=s_ln2)
        p.wait("vector", s_ln2, s_ln2.n)
        p.op("vector", lambda e: e.tensor_tensor(stat[:, 3:4], stat[:, 2:3],
                                                 stat[:, 2:3], op=ALU.mult),
             inc=s_ln2)
        p.wait("vector", s_ln2, s_ln2.n)
        p.op("vector", lambda e: e.scalar_tensor_tensor(
            out=stat[:, 4:5], in0=stat[:, 1:2], scalar=1.0 / D, in1=stat[:, 3:4],
            op0=ALU.mult, op1=ALU.subtract), inc=s_ln2)
        p.wait("vector", s_ln2, s_ln2.n)
        p.op("vector", lambda e: e.tensor_scalar(out=stat[:, 5:6], in0=stat[:, 4:5],
                                                 scalar1=1e-5, scalar2=None,
                                                 op0=ALU.add), inc=s_ln2)
        p.wait("vector", s_ln2, s_ln2.n)
        _rsqrt(p, stat, 5, 1, nsc, cfg.newton)
        p.op("vector", lambda e: e.tensor_tensor(stat[:, 8:9], stat[:, 2:3],
                                                 stat[:, 5:6], op=ALU.mult),
             inc=s_ln2)
        p.wait("vector", s_ln2, s_ln2.n)
        p.op("vector", lambda e: e.tensor_scalar(
            out=xn_mo[:], in0=xo_ch[0][:], scalar1=stat[:, 5:6],
            scalar2=stat[:, 8:9], op0=ALU.mult, op1=ALU.subtract), inc=s_ln2)
        p.wait("vector", s_ln2, s_ln2.n)
        p.op("vector", lambda e: e.tensor_copy(xnb_mo[:], xn_mo[:]), inc=s_ln2)
        ln2_done = s_ln2.n

        p.wait("tensor", s_ln2, ln2_done)
        if ci >= 2:
            p.wait("tensor", s_txe, done_txe[(st, ci - 2)])
        if not (st == 0 and ci == 0):
            p.wait("tensor", s_gs, astate["pg"])
        for g in range(8):
            p.op("tensor", lambda e, g=g: e.transpose(
                PT[:, g * 128:(g + 1) * 128], xnb_mo[:, g * 128:(g + 1) * 128],
                ident_b[:]), inc=(s_tx if g == 7 else None))
        for half4 in range(2):
            for g in range(4):
                p.op("tensor", lambda e, g=g, half4=half4: e.transpose(
                    PI[:, 1024 + g * 128:1024 + (g + 1) * 128],
                    xn_mo[:, (half4 * 4 + g) * 128:(half4 * 4 + g + 1) * 128],
                    ident_f[:]), inc=(s_tx if g == 3 else None))
            p.wait("vector", s_tx, s_tx.n)
            p.op("vector", lambda e, half4=half4: e.tensor_copy(
                xnf_T[:, half4 * 512:(half4 + 1) * 512], PI[:, 1024:1536]), inc=s_txe)
            if half4 == 0:
                p.wait("tensor", s_txe, s_txe.n)
        p.wait("vector", s_tx, s_tx.n - 2)
        p.op("vector", lambda e, ci=ci: e.tensor_copy(
            xnT_st[sp][:].rearrange("p (g n) -> p g n", g=8)[:, :, ci * 128:(ci + 1) * 128],
            PT[:].rearrange("p (g n) -> p g n", g=8)), inc=s_txe)
        done_txe[(st, ci)] = s_txe.n

        p.wait("tensor", s_txe, done_txe[(st, ci)])
        if st == 0 and ci == 0:
            p.wait("tensor", s_gbb, gbb_done)
        if not (st == 0 and ci == 0):
            p.wait("tensor", s_gs, astate["sc"])
        for cc in range(8):
            p.op("tensor", lambda e, cc=cc: e.matmul(
                PI[:, 1024:1024 + E], xnf_T[:, cc * 128:(cc + 1) * 128],
                gw_sb[:, cc * E:(cc + 1) * E], start=(cc == 0), stop=(cc == 7)),
                inc=(s_gt if cc == 7 else None))
        p.wait("vector", s_gt, s_gt.n)
        if dbg:
            p.wait("vector", s_mo, s_mo.n)
        p.op("vector", lambda e: e.tensor_tensor(sc_tm[:], PI[:, 1024:1024 + E], gb_bc[:],
                                                 op=ALU.add), inc=s_gs)
        astate['sc'] = s_gs.n
        p.wait("scalar", s_gs, s_gs.n)
        p.op("scalar", lambda e: e.activation(w_tm[:], sc_tm[:], AF.Tanh, scale=0.5),
             inc=s_gs)
        p.wait("vector", s_gs, s_gs.n)
        p.op("vector", lambda e: e.tensor_scalar(out=sc_tm[:], in0=w_tm[:],
                                                 scalar1=0.5, scalar2=0.5,
                                                 op0=ALU.mult, op1=ALU.add), inc=s_gs)
        p.wait("vector", s_gs, s_gs.n)
        if "dbg_scores" in dbg:
            p.wait("sync", s_gs, s_gs.n)
            p.op("sync", lambda e, c=c: e.dma_start(
                out=dbg["dbg_scores"][c * 128:(c + 1) * 128, :], in_=sc_tm[:]),
                inc=s_mo, inc_k=16)
        p.op("vector", lambda e: e.reduce_max(stat[:, 0:1], sc_tm[:], axis=AX.X),
             inc=s_gs)
        p.wait("vector", s_gs, s_gs.n)
        p.op("vector", lambda e: e.tensor_scalar(out=w_tm[:], in0=sc_tm[:],
                                                 scalar1=stat[:, 0:1], scalar2=None,
                                                 op0=ALU.is_ge), inc=s_gs)
        p.wait("vector", s_gs, s_gs.n)
        p.op("vector", lambda e: e.scalar_tensor_tensor(
            out=w_tm[:], in0=w_tm[:], scalar=-1e30, in1=sc_tm[:],
            op0=ALU.mult, op1=ALU.add), inc=s_gs)
        p.wait("vector", s_gs, s_gs.n)
        p.op("vector", lambda e: e.reduce_max(stat[:, 1:2], w_tm[:], axis=AX.X),
             inc=s_gs)
        p.wait("vector", s_gs, s_gs.n)
        p.op("vector", lambda e: e.tensor_tensor(stat[:, 2:3], stat[:, 0:1],
                                                 stat[:, 1:2], op=ALU.add), inc=s_gs)
        p.wait("vector", s_gs, s_gs.n)
        p.op("vector", lambda e: e.tensor_scalar(out=stat[:, 2:3], in0=stat[:, 2:3],
                                                 scalar1=1e-6, scalar2=None,
                                                 op0=ALU.add), inc=s_gs)
        p.wait("vector", s_gs, s_gs.n)
        p.op("vector", lambda e: e.reciprocal(stat[:, 3:4], stat[:, 2:3]), inc=s_gs)
        p.wait("vector", s_gs, s_gs.n)
        p.op("vector", lambda e: e.tensor_scalar(out=w_tm[:], in0=sc_tm[:],
                                                 scalar1=stat[:, 1:2], scalar2=None,
                                                 op0=ALU.is_ge), inc=s_gs)
        p.wait("vector", s_gs, s_gs.n)
        p.op("vector", lambda e: e.tensor_tensor(w_tm[:], w_tm[:], sc_tm[:],
                                                 op=ALU.mult), inc=s_gs)
        p.wait("vector", s_gs, s_gs.n)
        p.op("vector", lambda e: e.tensor_scalar(out=w_tm[:], in0=w_tm[:],
                                                 scalar1=stat[:, 3:4], scalar2=None,
                                                 op0=ALU.mult), inc=s_gs)
        p.wait("vector", s_gs, s_gs.n)
        if "dbg_w8" in dbg:
            p.wait("sync", s_gs, s_gs.n)
            p.op("sync", lambda e, c=c: e.dma_start(
                out=dbg["dbg_w8"][c * 128:(c + 1) * 128, :], in_=w_tm[:]),
                inc=s_mo, inc_k=16)
        p.wait("tensor", s_gs, s_gs.n)
        p.op("tensor", lambda e: e.transpose(PI[0:E, 1024:1152], w_tm[:], ident_f[:]),
             inc=s_gt)
        p.wait("vector", s_gt, s_gt.n)
        p.op("vector", lambda e, ci=ci: e.tensor_copy(
            wT_all[sp][0:E, ci * 128:(ci + 1) * 128], PI[0:E, 1024:1152]), inc=s_gs)
        done_gs[(st, ci)] = s_gs.n
        astate['pg'] = s_gs.n

    for ci in range(STC):
        a_chunk(0, ci)

    for st in range(NST):
        gates_done = done_gs[(st, STC - 1)]

        p.wait("vector", s_eva, s_eva.n)
        p.op("vector", lambda e: e.memset(comb[:], 0.0), inc=s_cz)
        p.wait("gpsimd", s_cz, s_cz.n)
        cz_done = s_cz.n

        for ei in range(NE + 1):
            shared = ei == NE
            egi = ei + st * (NE + 1)
            for g0 in range(2):
                gi = egi * 8 + g0
                if gi >= 2:
                    p.wait("sync", s_mmu, up_cons[gi - 2])
                src = wbf_su[0] if shared else wbf_up[ei]
                p.op("sync", lambda e, g0=g0, src=src, gi=gi: e.dma_start(
                    out=ubuf[gi % 2][:].rearrange("p (c n) -> p c n", c=8),
                    in_=src.rearrange("(c p n) -> p c n", p=128, n=II)
                    [:, :, g0 * 512:(g0 + 1) * 512]), inc=s_wup[gi % 2], inc_k=16)
                up_ld[gi] = s_wup[gi % 2].n
            for d0 in range(2):
                di = egi * 8 + d0
                if di >= 2:
                    p.wait("scalar", s_mmd, dn_cons[di - 2])
                srcd = wbf_sd[0] if shared else wbf_dn[ei]
                p.op("scalar", lambda e, d0=d0, srcd=srcd, di=di: e.dma_start(
                    out=dbuf[di % 2][:], in_=srcd[d0].rearrange("(q m) -> q m", q=128)),
                    inc=s_wdn[di % 2], inc_k=16)
                dn_ld[di] = s_wdn[di % 2].n

            if st + 1 < NST and 1 <= ei <= STC:
                a_chunk(st + 1, ei - 1)

            nt512 = STOK // 512
            if not shared:
                p.wait("gpsimd", s_gs, gates_done)
                p.wait("gpsimd", s_evm, s_evm.n)
                for t5 in range(nt512):
                    p.op("gpsimd", lambda e, ei=ei, t5=t5, sp2=st % 2: e.dma_start(
                        out=wrow[0:1, 0:512],
                        in_=wT_all[sp2][ei:ei + 1, t5 * 512:(t5 + 1) * 512]),
                        inc=s_wrow, inc_k=16)
                    p.wait("tensor", s_wrow, s_wrow.n)
                    p.wait("tensor", s_evm, s_evm.n)
                    if ei == 0 and t5 == 0:
                        p.wait("tensor", s_gs, gates_done)
                    p.op("tensor", lambda e: e.matmul(
                        PI[:, 0:512], ones_row[:], wrow[0:1, 0:512],
                        start=True, stop=True), inc=s_wb2)
                    p.wait("vector", s_wb2, s_wb2.n)
                    p.op("vector", lambda e, t5=t5: e.tensor_copy(
                        wbc[:, t5 * 512:(t5 + 1) * 512], PI[:, 0:512]), inc=s_wb2)
                    p.wait("gpsimd", s_wb2, s_wb2.n)
                wb2_cons = s_wb2.n

            for grp in range(8):
                gi = egi * 8 + grp
                p.wait("tensor", s_wup[gi % 2], up_ld[gi])
                if ei == 0 and grp == 0:
                    p.wait("tensor", s_gs, gates_done)
                if grp == 0:
                    p.wait("tensor", s_evm, s_evm.n)
                for it4 in range(4):
                    it = grp * 4 + it4
                    if nup >= 2:
                        p.wait("tensor", s_gel, gel_done[nup - 2])
                    bb = (nup % 2) * 2
                    for cc in range(8):
                        for t5 in range(nt512):
                            p.op("tensor", lambda e, it4=it4, cc=cc, gi=gi, t5=t5,
                                 bb=bb, sp2=st % 2: e.matmul(
                                     PB[bb + t5][:, 0:512],
                                     ubuf[gi % 2][:, cc * 512 + it4 * 128:
                                                  cc * 512 + it4 * 128 + 128],
                                     xnT_st[sp2][:, cc * STOK + t5 * 512:
                                            cc * STOK + (t5 + 1) * 512],
                                     start=(cc == 0), stop=(cc == 7)),
                                 inc=(s_mmu if (cc == 7 and t5 == nt512 - 1) else None))
                    p.wait("scalar", s_mmu, s_mmu.n)
                    if it == 0 and grp == 0:
                        p.wait("scalar", s_mmd, s_mmd.n)
                    bcol = (0 if shared else ei * 32) + it
                    bsrc = sbu_sb if shared else upb_sb
                    for t5 in range(nt512):
                        p.op("scalar", lambda e, it=it, t5=t5, bb=bb, bcol=bcol,
                             bsrc=bsrc: e.activation(
                                 hact[:, it * STOK + t5 * 512:it * STOK + (t5 + 1) * 512],
                                 PB[bb + t5][:, 0:512], AF.Gelu,
                                 bias=bsrc[:, bcol:bcol + 1], scale=1.0), inc=s_gel)
                    gel_done[nup] = s_gel.n
                    nup += 1
                up_cons[gi] = s_mmu.n
                if grp < 6:
                    gi2 = egi * 8 + grp + 2
                    p.wait("sync", s_mmu, up_cons[gi])
                    src = wbf_su[0] if shared else wbf_up[ei]
                    p.op("sync", lambda e, grp=grp, src=src, gi2=gi2: e.dma_start(
                        out=ubuf[gi2 % 2][:].rearrange("p (c n) -> p c n", c=8),
                        in_=src.rearrange("(c p n) -> p c n", p=128, n=II)
                        [:, :, (grp + 2) * 512:(grp + 3) * 512]), inc=s_wup[gi2 % 2],
                        inc_k=16)
                    up_ld[gi2] = s_wup[gi2 % 2].n

            p.wait("tensor", s_gel, s_gel.n)
            for dt in range(8):
                di = egi * 8 + dt
                p.wait("tensor", s_wdn[di % 2], dn_ld[di])
                if ndn >= 2:
                    p.wait("tensor", s_evm, evm_done[ndn - 2])
                bb = (ndn % 2) * 2
                for cc in range(32):
                    for t5 in range(nt512):
                        p.op("tensor", lambda e, cc=cc, di=di, t5=t5, bb=bb:
                             e.matmul(PB[bb + t5][:, 0:512],
                                      dbuf[di % 2][:, cc * 128:(cc + 1) * 128],
                                      hact[:, cc * STOK + t5 * 512:
                                           cc * STOK + (t5 + 1) * 512],
                                      start=(cc == 0), stop=(cc == 31)),
                             inc=(s_mmd if (cc == 31 and t5 == nt512 - 1) else None))
                p.wait("vector", s_mmd, s_mmd.n)
                if not shared:
                    p.wait("vector", s_wb2, wb2_cons)
                for t5 in range(nt512):
                    p.wait("vector", s_eva, s_eva.n)
                    if shared:
                        p.op("vector", lambda e, dt=dt, t5=t5, bb=bb: e.tensor_scalar(
                            out=tmp_ev[:], in0=PB[bb + t5][:, 0:512],
                            scalar1=sbd_sb[:, dt:dt + 1], scalar2=None, op0=ALU.add),
                            inc=s_evm)
                    else:
                        p.op("vector", lambda e, dt=dt, t5=t5, bb=bb, ei=ei:
                             e.scalar_tensor_tensor(
                                 out=tmp_ev[:], in0=PB[bb + t5][:, 0:512],
                                 scalar=dnb_sb[:, ei * 8 + dt:ei * 8 + dt + 1],
                                 in1=wbc[:, t5 * 512:(t5 + 1) * 512],
                                 op0=ALU.add, op1=ALU.mult), inc=s_evm)
                    p.wait("gpsimd", s_evm, s_evm.n)
                    if st == 0 and ei == 0 and dt == 0 and t5 == 0:
                        p.wait("gpsimd", s_cz, cz_done)
                    p.op("gpsimd", lambda e, dt=dt, t5=t5: e.tensor_tensor(
                        comb[:, dt * STOK + t5 * 512:dt * STOK + (t5 + 1) * 512],
                        comb[:, dt * STOK + t5 * 512:dt * STOK + (t5 + 1) * 512],
                        tmp_ev[:], op=ALU.add), inc=s_eva)
                evm_done[ndn] = s_evm.n
                ndn += 1
                dn_cons[di] = s_mmd.n
                if dt < 6:
                    di2 = egi * 8 + dt + 2
                    p.wait("scalar", s_mmd, dn_cons[di])
                    srcd = wbf_sd[0] if shared else wbf_dn[ei]
                    p.op("scalar", lambda e, dt=dt, srcd=srcd, di2=di2: e.dma_start(
                        out=dbuf[di2 % 2][:],
                        in_=srcd[dt + 2].rearrange("(q m) -> q m", q=128)),
                        inc=s_wdn[di2 % 2], inc_k=16)
                    dn_ld[di2] = s_wdn[di2 % 2].n
        p.wait("scalar", s_eva, s_eva.n)
        p.op("scalar", lambda e: e.activation(mx8[:], comb[:, 0:STOK], AF.Abs), inc=s_n2)
        for g in range(1, 8):
            p.wait("scalar", s_n2, s_n2.n)
            p.op("scalar", lambda e, g=g: e.activation(
                graw[:, 0:STOK], comb[:, g * STOK:(g + 1) * STOK], AF.Abs), inc=s_n2)
            p.wait("vector", s_n2, s_n2.n)
            p.op("vector", lambda e: e.tensor_tensor(mx8[:], mx8[:], graw[:, 0:STOK],
                                                     op=ALU.max), inc=s_n2)
        p.wait("gpsimd", s_n2, s_n2.n)
        p.op("gpsimd", lambda e: e.dma_start(out=graw[0:64, 0:STOK], in_=mx8[64:128, :]),
             inc=s_n2g, inc_k=16)
        p.wait("vector", s_n2g, s_n2g.n)
        p.op("vector", lambda e: e.tensor_tensor(mx8[0:64, :], mx8[0:64, :],
                                                 graw[0:64, 0:STOK], op=ALU.max),
             inc=s_n2)
        p.wait("gpsimd", s_n2, s_n2.n)
        p.op("gpsimd", lambda e: e.dma_start(out=graw[0:32, 0:STOK], in_=mx8[32:64, :]),
             inc=s_n2g, inc_k=16)
        p.wait("vector", s_n2g, s_n2g.n)
        p.wait("vector", s_n2, s_n2.n)
        p.op("vector", lambda e: e.tensor_tensor(mx8[0:32, :], mx8[0:32, :],
                                                 graw[0:32, 0:STOK], op=ALU.max), inc=s_n2)
        for ci in range(STC):
            p.wait("tensor", s_n2, s_n2.n)
            p.op("tensor", lambda e, ci=ci: e.transpose(
                PI[:, 0:32], mx8[0:32, ci * 128:(ci + 1) * 128],
                ident_f[0:32, 0:32]), inc=s_tz)
            p.wait("vector", s_tz, s_tz.n)
            p.op("vector", lambda e, ci=ci: e.reduce_max(
                stat[:, 0:1], PI[:, 0:32], axis=AX.X), inc=s_n2)
            p.wait("vector", s_n2, s_n2.n)
            p.op("vector", lambda e: e.tensor_scalar(out=stat[:, 0:1], in0=stat[:, 0:1],
                                                     scalar1=1e-5, scalar2=None,
                                                     op0=ALU.add), inc=s_n2)
            p.wait("vector", s_n2, s_n2.n)
            p.op("vector", lambda e, ci=ci: e.reciprocal(rtm[:, ci:ci + 1],
                                                         stat[:, 0:1]), inc=s_n2)
            p.wait("vector", s_n2, s_n2.n)
            p.wait("tensor", s_n2, s_n2.n)
        p.op("tensor", lambda e: e.transpose(PI[0:STC, 0:128], rtm[:], ident_f[:]),
             inc=s_tz)
        p.wait("vector", s_tz, s_tz.n)
        p.op("vector", lambda e: e.tensor_copy(rT_sb[0:STC, :], PI[0:STC, 0:128]),
             inc=s_n2)
        p.wait("gpsimd", s_n2, s_n2.n)
        for ci in range(STC):
            p.op("gpsimd", lambda e, ci=ci: e.dma_start(
                out=wrow[0:1, ci * 128:(ci + 1) * 128], in_=rT_sb[ci:ci + 1, :]),
                inc=s_n2g, inc_k=16)
        rrow_done = s_n2g.n
        p.wait("tensor", s_n2g, rrow_done)
        for g5 in range(STOK // 512):
            p.op("tensor", lambda e, g5=g5: e.matmul(
                PI[:, 0:512], ones_row[:], wrow[0:1, g5 * 512:(g5 + 1) * 512],
                start=True, stop=True), inc=s_tz)
            p.wait("vector", s_tz, s_tz.n)
            p.op("vector", lambda e, g5=g5: e.tensor_copy(
                rbc[:, g5 * 512:(g5 + 1) * 512], PI[:, 0:512]), inc=s_n2)
            p.wait("tensor", s_n2, s_n2.n)
        for g in range(8):
            p.wait("vector", s_n2, s_n2.n)
            p.op("vector", lambda e, g=g: e.tensor_tensor(
                graw[:, 0:STOK], comb[:, g * STOK:(g + 1) * STOK], rbc[:], op=ALU.mult),
                inc=s_n2)
            p.wait("scalar", s_n2, s_n2.n)
            p.op("scalar", lambda e: e.activation(mx8[:], graw[:, 0:STOK], AF.Tanh,
                                                  scale=0.5), inc=s_n2)
            p.wait("vector", s_n2, s_n2.n)
            p.op("vector", lambda e, g=g: e.scalar_tensor_tensor(
                out=comb[:, g * STOK:(g + 1) * STOK], in0=mx8[:], scalar=1.0,
                in1=graw[:, 0:STOK], op0=ALU.add, op1=ALU.mult), inc=s_zg)
            p.wait("vector", s_zg, s_zg.n)
        zg_done = s_zg.n
        for ci in range(STC):
            c = st * STC + ci
            p.wait("tensor", s_zg, zg_done)
            if not (st == 0 and ci == 0):
                p.wait("tensor", s_zg, zgt_cons)
            for g in range(8):
                p.op("tensor", lambda e, g=g, ci=ci: e.transpose(
                    PT[:, g * 128:(g + 1) * 128],
                    comb[:, g * STOK + ci * 128:g * STOK + (ci + 1) * 128], ident_b[:]),
                    inc=(s_tz if g == 7 else None))
            p.wait("vector", s_tz, s_tz.n)
            p.op("vector", lambda e: e.tensor_copy(zgT[:], PT[:]), inc=s_zg)
            zgt_cons = s_zg.n
            if not (st == 0 and ci == 0):
                p.wait("sync", s_zg, prev_epi)
            oc = st * STC + ci
            p.wait("gpsimd", s_sw[0], sw_all[0])
            p.wait("gpsimd", s_sw[1], sw_all[1])
            if not (st == 0 and ci == 0):
                p.wait("gpsimd", s_out, prev_out)
            p.op("gpsimd", lambda e, oc=oc: e.dma_start(
                out=idxw_sb[:],
                in_=idxw[0:1, oc * 128:(oc + 1) * 128].rearrange("r (p n) -> (r p) n", n=1)),
                inc=s_epg, inc_k=16)
            p.wait("gpsimd", s_epg, s_epg.n)
            p.op("gpsimd", lambda e: e.indirect_dma_start(
                out=sch[0][:], out_offset=None, in_=s_scr,
                in_offset=IndirectOffsetOnAxis(ap=idxw_sb[:], axis=0)),
                inc=s_epg, inc_k=16)
            p.op("sync", lambda e, c=c: e.dma_start(
                out=xo_ep[:], in_=x_own[c * 128:(c + 1) * 128, :]),
                inc=s_ep, inc_k=16)
            ep_ld = s_ep.n
            p.wait("vector", s_ep, ep_ld)
            p.wait("vector", s_epg, s_epg.n)
            p.wait("vector", s_zg, zgt_cons)
            if dbg:
                p.wait("vector", s_mo, s_mo.n)
            p.op("vector", lambda e: e.tensor_tensor(
                graw[:, 0:D], sch[0][:], zgT[:], op=ALU.mult), inc=s_zg)
            p.wait("vector", s_zg, s_zg.n)
            p.op("vector", lambda e: e.scalar_tensor_tensor(
                out=sch[0][:], in0=graw[:, 0:D], scalar=0.5, in1=xo_ep[:],
                op0=ALU.mult, op1=ALU.add), inc=s_zg)
            prev_epi = s_zg.n
            p.wait("sync", s_zg, prev_epi)
            p.op("sync", lambda e, c=c: e.dma_start(
                out=out_d[c * 128:(c + 1) * 128, :], in_=sch[0][:]),
                inc=s_out, inc_k=16)
            out_cons[(st, ci)] = s_out.n
            prev_out = s_out.n
            if "dbg_zg" in dbg:
                p.op("sync", lambda e, c=c: e.dma_start(
                    out=dbg["dbg_zg"][c * 128:(c + 1) * 128, :],
                    in_=graw[:, 0:D]), inc=s_mo, inc_k=16)

    p.wait("sync", s_out, s_out.n)
    if dbg:
        p.wait("sync", s_mo, s_mo.n)
        p.wait("sync", s_dbg, s_dbg.n)
        p.wait("gpsimd", s_dbg2, s_dbg2.n)

    with nc.Block() as blk:
        blk.sync(lambda e: [f(e) for f in p.q["sync"]])
        blk.tensor(lambda e: [f(e) for f in p.q["tensor"]])
        blk.vector(lambda e: [f(e) for f in p.q["vector"]])
        blk.scalar(lambda e: [f(e) for f in p.q["scalar"]])
        blk.gpsimd(lambda e: [f(e) for f in p.q["gpsimd"]])
    oes.close()
    es.close()
    return nc


_CACHE = {}


def _get_nc(cfg_key):
    if cfg_key not in _CACHE:
        cfg = Cfg()
        _CACHE[cfg_key] = build(cfg)
    return _CACHE[cfg_key]


def kernel(L0_attn_x, L0_attn_ln_w, L0_attn_ln_b, L0_attn_Wq, L0_attn_Wk, L0_attn_Wv,
           L0_attn_Wb, L0_attn_o_norm_w, L0_attn_Wo, L0_ffn_gate_w, L0_ffn_bias,
           L0_ffn_up_w, L0_ffn_up_b, L0_ffn_down_w, L0_ffn_down_b, L0_ffn_sw_up,
           L0_ffn_sb_up, L0_ffn_sw_down, L0_ffn_sb_down):
    x = np.ascontiguousarray(L0_attn_x, dtype=np.float32)
    nc = _get_nc("full")

    upb_t = np.ascontiguousarray(
        L0_ffn_up_b.reshape(E, 32, 128).transpose(2, 0, 1).reshape(128, E * 32))
    dnb_t = np.ascontiguousarray(
        L0_ffn_down_b.reshape(E, 8, 128).transpose(2, 0, 1).reshape(128, E * 8))
    sbu_t = np.ascontiguousarray(L0_ffn_sb_up.reshape(32, 128).T)
    sbd_t = np.ascontiguousarray(L0_ffn_sb_down.reshape(8, 128).T)

    common = {
        "ln_w": L0_attn_ln_w.reshape(1, D), "ln_b": L0_attn_ln_b.reshape(1, D),
        "Wq": L0_attn_Wq, "Wk": L0_attn_Wk, "Wv": L0_attn_Wv, "Wb": L0_attn_Wb,
        "onw": L0_attn_o_norm_w.reshape(1, DV), "Wo": L0_attn_Wo,
        "gate_w": L0_ffn_gate_w, "gate_b": L0_ffn_bias.reshape(1, E),
        "upw": L0_ffn_up_w, "upb_t": upb_t, "dnw": L0_ffn_down_w, "dnb_t": dnb_t,
        "sw_up": L0_ffn_sw_up, "sb_up_t": sbu_t, "sw_dn": L0_ffn_sw_down,
        "sb_dn_t": sbd_t,
    }
    in_maps = []
    for core in range(8):
        b = core // 2
        half = core % 2
        m = dict(common)
        m["x_full"] = x[b]
        m["x_own"] = x[b, half * 4096:(half + 1) * 4096]
        idx = np.arange(4096, dtype=np.int32) + 4096 * half
        m["idxw"] = idx.reshape(1, 4096)
        in_maps.append(m)

    res = run_bass_kernel_spmd(nc, in_maps, core_ids=list(range(8)))
    out = np.empty((B, T, D), np.float32)
    for core in range(8):
        b = core // 2
        half = core % 2
        out[b, half * 4096:(half + 1) * 4096] = res.results[core]["out"]
    return out


B = 4
T = 8192
